# version 24
# baseline (speedup 1.0000x reference)
import numpy as np
import ml_dtypes
import concourse.bass as bass
import concourse.mybir as mybir
from concourse.bass_utils import run_bass_kernel_spmd

F32 = mybir.dt.float32
BF16 = mybir.dt.bfloat16
I8 = mybir.dt.int8
AF = mybir.ActivationFunctionType
ALU = mybir.AluOpType
AX = mybir.AxisListType

D = 1024
H = 16
HD = 64
CK = 64


class Sem:
    def __init__(self, nc, name):
        self.h = nc.alloc_semaphore(name)
        self.v = 0


class Eng:

    def __init__(self, sem, dma=False):
        self.sem = sem
        self.dma = dma
        self.groups = []

    def g(self, fn, *waits, sem=None, dma=None):
        sem = sem or self.sem
        dmaf = self.dma if dma is None else dma
        if not dmaf and sem.v > 0:
            waits = waits + ((sem, sem.v),)
        self.groups.append((tuple(waits), fn, sem, dmaf))
        if dmaf:
            n = getattr(fn, "n_dma", 1)
            sem.v += 16 * n
        else:
            sem.v += 1
        return sem.v


def _emit(engine, eng_sched):
    for waits, fn, sem, dmaf in eng_sched.groups:
        for (wsem, val) in waits:
            if val > 0:
                engine.wait_ge(wsem.h, val)
        last = fn(engine)
        if dmaf:
            for instr in (last if isinstance(last, (list, tuple)) else [last]):
                instr.then_inc(sem.h, 16)
        else:
            last.then_inc(sem.h, 1)


def dma_group(n):
    def deco(fn):
        fn.n_dma = n
        return fn
    return deco


def mmt(e, out, lhsT, rhs, start, stop, tp):
    return e.matmul(out, lhsT, rhs, start=start, stop=stop, tile_position=tp,
                    skip_group_check=(tp[1] == 64))


def trt(e, out, in_, idn, tp):
    return e.matmul(out, in_, idn, is_transpose=True, tile_position=tp,
                    skip_group_check=True)


def _build(T, F, groups):
    assert T % F == 0 and F % 128 == 0 and 128 <= F <= 512
    NFR = T // F
    NG = F // 128
    NCH = F // CK
    FT = F

    nc = bass.Bass("TRN2", target_bir_lowering=False)

    x_d = nc.dram_tensor("x", [T, D], F32, kind="ExternalInput").ap()
    wq_d = nc.dram_tensor("wq", [D, D], F32, kind="ExternalInput").ap()
    wk_d = nc.dram_tensor("wk", [D, D], F32, kind="ExternalInput").ap()
    wv_d = nc.dram_tensor("wv", [D, D], F32, kind="ExternalInput").ap()
    wo_d = nc.dram_tensor("wo", [D, D], F32, kind="ExternalInput").ap()
    wbf_d = nc.dram_tensor("wbf", [D, 32], F32, kind="ExternalInput").ap()
    normw_d = nc.dram_tensor("normw", [1, D], F32, kind="ExternalInput").ap()
    fqt_d = nc.dram_tensor("fqt", [D, D], I8, kind="ExternalInput").ap()
    fscale_d = nc.dram_tensor("fscale", [1, D], F32, kind="ExternalInput").ap()
    fbias_d = nc.dram_tensor("fbias", [1, D], F32, kind="ExternalInput").ap()
    iden_d = nc.dram_tensor("iden", [128, 128], BF16, kind="ExternalInput").ap()
    mup_d = nc.dram_tensor("mup", [128, 64], BF16, kind="ExternalInput").ap()
    mlo_d = nc.dram_tensor("mlo", [128, 64], BF16, kind="ExternalInput").ap()
    mui_d = nc.dram_tensor("mui", [128, 64], BF16, kind="ExternalInput").ap()
    ep_d = nc.dram_tensor("ep", [16, 1024], BF16, kind="ExternalInput").ap()
    csel_d = nc.dram_tensor("csel", [128, 128], BF16, kind="ExternalInput").ap()
    dup_d = nc.dram_tensor("dup", [64, 128], BF16, kind="ExternalInput").ap()
    reset_d = nc.dram_tensor("reset", [16, F], F32, kind="ExternalInput").ap()
    shalf_d = nc.dram_tensor("shalf", [128, 1], F32, kind="ExternalInput").ap()
    out_d = nc.dram_tensor("out", [T, D], F32, kind="ExternalOutput").ap()
    oz_d = nc.dram_tensor("oz_i", [T // CK, 128, 512], BF16).ap()
    mh_d = nc.dram_tensor("mh_i", [T // CK, 128, 512], BF16).ap()
    qh_d = nc.dram_tensor("qh_i", [T // CK, 128, 512], BF16).ap()
    gz_d = nc.dram_tensor("gz_i", [T, D], BF16).ap()
    eld_d = nc.dram_tensor("eld_i", [T // F, 128, 64], BF16).ap()
    scin_d = nc.dram_tensor("scin_i", [128, 512], BF16).ap()
    scout_d = nc.dram_tensor("scout_i", [256, 512], BF16).ap()

    def sb(name, shape, dtype):
        return nc.alloc_sbuf_tensor(name, shape, dtype).ap()

    wq = sb("wq_s", [128, 8192], BF16)
    wk = sb("wk_s", [128, 8192], BF16)
    wv = sb("wv_s", [128, 8192], BF16)
    wo = sb("wo_s", [128, 8192], BF16)
    wfn = sb("wfn_s", [128, 8192], BF16)
    wbf = sb("wbf_s", [128, 256], BF16)
    wstage8 = None
    iden = sb("iden_s", [128, 128], BF16)
    mup_t = sb("mup_s", [128, 64], BF16)
    mlo_t = sb("mlo_s", [128, 64], BF16)
    mui_t = sb("mui_s", [128, 64], BF16)
    mup = mup_t.unsqueeze(1).broadcast_to((128, 8, 64)).rearrange("p a b -> p (a b)") if False else None
    mup = mup_t.unsqueeze(1).broadcast_to((128, 8, 64))
    mlo = mlo_t.unsqueeze(1).broadcast_to((128, 8, 64))
    mui = mui_t.unsqueeze(1).broadcast_to((128, 8, 64))
    ep = sb("ep_s", [16, 1024], BF16)
    csel = sb("csel_s", [128, 128], BF16)
    dupc = sb("dup_s", [64, 128], BF16)
    resetm = sb("reset_s", [16, F], F32)
    normw_f = [sb(f"normw_f{p}", [128, 1], F32) for p in range(8)]
    fbias_b = sb("fbias_b", [128, D], BF16)
    fws_b = sb("fws_b", [128, D], BF16)

    x_t = sb("x_t", [128, NG * 1024], F32)
    wstage = x_t[:, 0:2048]
    xn_t = sb("xn_t", [128, NG * 1024], BF16)
    xn_f = sb("xn_f", [128, 8 * FT], BF16)
    xq_t = sb("xq_t", [128, NG * 1024], BF16)
    xq_f = sb("xq_f", [128, 8 * FT], BF16)
    q_f = sb("q_f", [128, 8 * FT], BF16)
    k_f = sb("k_f", [128, 8 * FT], BF16)
    kn_f = sb("kn_f", [128, 8 * FT], BF16)
    knb_f = sb("knb_f", [128, 8 * FT], BF16)
    v_f = sb("v_f", [128, 8 * FT], BF16)
    o_f = sb("o_f", [128, 8 * FT], BF16)
    ksq = sb("ksq", [128, FT], BF16)
    field_s = sb("field_s", [128, FT], BF16)
    g_t = sb("g_t", [128, NG * 1024], BF16)
    s_scr = sb("s_scr", [128, 1024], F32)
    out_t = sb("out_t", [128, 1024], F32)
    r2_scr = out_t
    sq_scr = sb("sq_scr", [128, 1024], F32)
    zbf = sq_scr[0:16, 0:2 * F]
    beta_hb = sb("beta_hb", [16, 2 * F], BF16)
    la_h = sb("la_h", [16, F], F32)
    ela_h = sb("ela_h", [16, F], BF16)
    elam_hb = sb("elam_hb", [16, NCH], BF16)
    bem_h = sb("bem_h", [16, 2 * F], BF16)
    rn_h = sb("rn_h", [16, F], BF16)
    rn_f32 = sb("rn_f32", [16, F], F32)
    rms_h = rn_f32
    zero16 = sb("zero16", [16, 1], F32)
    ones1 = sb("ones1", [128, 1], F32)
    flag1 = sb("flag1", [128, 1], F32)
    lam_all = sb("lam_all", [128, 2 * 8 * NCH], F32)
    ela_dup = sb("ela_dup", [128, 2 * 16 * NCH], BF16)

    ln_m = [sb(f"ln_m{g}", [128, 1], F32) for g in range(NG)]
    ln_r = [sb(f"ln_r{g}", [128, 1], F32) for g in range(NG)]
    ln_s2 = [sb(f"ln_s2{g}", [128, 1], F32) for g in range(NG)]
    qs_s = [sb(f"qs_s{g}", [128, 1], F32) for g in range(NG)]
    qsr_s = [sb(f"qsr_s{g}", [128, 1], F32) for g in range(NG)]
    state = sb("state", [128, 512], F32)
    state_b = sb("state_b", [128, 512], BF16)
    t0_d = [sb(f"t0_{s}", [128, 512], BF16) for s in range(2)]
    dt_d = [sb(f"dt_{s}", [128, 512], BF16) for s in range(3)]
    p0_s = sb("p0_s", [128, 512], BF16)
    tl_d = [[sb(f"tl{k}_{s}", [128, 512], BF16) for k in range(5)] for s in range(2)]
    pl_s = [sb(f"pl{k}", [128, 512], BF16) for k in range(4)]
    r_all = sb("r_all", [128, NCH * 1024], BF16)
    knt_all = sb("knt_all", [128, NCH * 1024], BF16)
    xch_s = sb("xch_s", [128, 1024], BF16)
    vt_d = [sb(f"vt_{s}", [128, 512], BF16) for s in range(2)]
    wt_d = [sb(f"wt_{s}", [128, 512], BF16) for s in range(2)]
    wfm_d = [sb(f"wfm_{s}", [128, 512], BF16) for s in range(2)]
    osc_s = sb("osc_s", [128, 512], BF16)
    shalf = sb("shalf_s", [128, 1], F32)
    rms_hb = beta_hb[:, 0:F]
    scr_h = sq_scr.bitcast(BF16)[0:64, 1024:1024 + 16 * NCH]
    wstage8 = r_all.bitcast(I8)[:, 0:8192]

    pb = [nc.alloc_psum_tensor(f"pb{i}", [128, 512], F32).ap() for i in range(6)]
    pt = nc.alloc_psum_tensor("pt6", [128, 1024], BF16).ap()
    pt2 = nc.alloc_psum_tensor("pt7", [128, 1024], BF16).ap()

    s_in = Sem(nc, "s_in")
    s_c = Sem(nc, "s_c")
    s_w = Sem(nc, "s_w")
    s_w2 = Sem(nc, "s_w2")
    s_wb = Sem(nc, "s_wb")
    s_wb2 = Sem(nc, "s_wb2")
    s_te = Sem(nc, "s_te")
    s_dve = Sem(nc, "s_dve")
    s_act = Sem(nc, "s_act")
    s_gp = Sem(nc, "s_gp")
    s_cc = Sem(nc, "s_cc")
    s_st = Sem(nc, "s_st")
    s_sp = Sem(nc, "s_sp")
    s_spo = Sem(nc, "s_spo")
    s_spq = Sem(nc, "s_spq")
    s_out = Sem(nc, "s_out")

    SY = Eng(s_in, dma=True)
    SW = Eng(s_w, dma=True)
    TE = Eng(s_te)
    VE = Eng(s_dve)
    SE = Eng(s_act)
    GP = Eng(s_cc)
    GPD = Eng(s_out, dma=True)
    GPSTG = Eng(s_wb, dma=True)


    @dma_group(19)
    def ld_consts(e):
        a = []
        a.append(e.dma_start(out=iden, in_=iden_d))
        a.append(e.dma_start(out=mup_t, in_=mup_d))
        a.append(e.dma_start(out=mlo_t, in_=mlo_d))
        a.append(e.dma_start(out=mui_t, in_=mui_d))
        a.append(e.dma_start(out=ep, in_=ep_d))
        a.append(e.dma_start(out=csel, in_=csel_d))
        a.append(e.dma_start(out=dupc, in_=dup_d))
        a.append(e.dma_start(out=resetm, in_=reset_d))
        a.append(e.dma_start(out=shalf, in_=shalf_d))
        for p8 in range(8):
            a.append(e.dma_start(out=normw_f[p8],
                                 in_=normw_d[:, p8 * 128:(p8 + 1) * 128].rearrange("o b -> b o")))
        a.append(e.dma_start(out=sq_scr[:, 0:1024], in_=fbias_d.squeeze(0).partition_broadcast(128)))
        a.append(e.dma_start(out=s_scr[:, 0:1024], in_=fscale_d.squeeze(0).partition_broadcast(128)))
        return a
    CONSTS_IN = GPSTG.g(ld_consts, sem=s_c)

    def init_zero(e):
        e.tensor_copy(fbias_b[:], sq_scr[:, 0:1024])
        e.tensor_copy(fws_b[:], s_scr[:, 0:1024])
        e.memset(ones1[:], 1.0)
        e.memset(flag1[:], 2.0 ** -30)
        e.memset(zero16[:], 0.0)
        e.memset(state_b[:], 0.0)
        return e.memset(state[:], 0.0)
    IZ = VE.g(init_zero, (s_c, CONSTS_IN))
    VE.g(lambda e: e.tensor_tensor(flag1[:], ones1[:], flag1[:], op=ALU.add))
    W_DVE = VE.g(lambda e: e.tensor_scalar(flag1[:], flag1[:], 1.0, 2.0 ** 30,
                                           op0=ALU.subtract, op1=ALU.mult))

    xq32 = xq_t.bitcast(F32)

    @dma_group(1)
    def ld_wbf(e):
        return [e.dma_start(out=xq32[:, 0:256].rearrange("p (a b) -> p a b", b=32),
                            in_=wbf_d.rearrange("(a p) b -> p a b", p=128))]
    WBF_IN = SY.g(ld_wbf, sem=s_w)
    CAST_WBF = VE.g(lambda e: e.tensor_copy(wbf[:], xq32[:, 0:256]), (s_w, WBF_IN))

    r32 = r_all.bitcast(F32)
    kn32 = knt_all.bitcast(F32)
    sy_pieces = [(wk_d, wk, kc0) for kc0 in (0, 2, 4, 6)] + \
                [(wq_d, wq, kc0) for kc0 in (0, 2, 4, 6)]
    sy_cast = {}
    for i, (wd, wsb, kc0) in enumerate(sy_pieces):
        slot = x_t[:, 0:2048] if i % 2 == 0 else r32[:, 0:2048]
        sem = s_w if i % 2 == 0 else s_w2

        @dma_group(2)
        def ld(e, wd=wd, kc0=kc0, slot=slot):
            return [e.dma_start(out=slot[:, j * 1024:(j + 1) * 1024],
                                in_=wd[(kc0 + j) * 128:(kc0 + j + 1) * 128, :])
                    for j in range(2)]
        w = [(s_dve, CAST_WBF)] if i < 2 else [(s_dve, sy_cast[i - 2])]
        ld_v = SY.g(ld, *w, sem=sem)

        def cast(e, wsb=wsb, kc0=kc0, slot=slot):
            return e.tensor_copy(wsb[:, kc0 * 1024:(kc0 + 2) * 1024], slot[:])
        sy_cast[i] = VE.g(cast, (sem, ld_v))
    SY_XT_CAST = sy_cast[len(sy_pieces) - 2]
    SY_LAST_CAST = sy_cast[len(sy_pieces) - 1]

    gp_pieces = [(wv_d, wv, kc0) for kc0 in (0, 2, 4, 6)] + \
                [(wo_d, wo, kc0) for kc0 in (0, 2, 4, 6)]
    gp_cast = {}
    for i, (wd, wsb, kc0) in enumerate(gp_pieces):
        sem = s_wb if i % 2 == 0 else s_wb2

        @dma_group(2)
        def ldg(e, wd=wd, kc0=kc0, i=i):
            if i % 2 == 0:
                return [e.dma_start(out=out_t[:, 0:1024], in_=wd[kc0 * 128:(kc0 + 1) * 128, :]),
                        e.dma_start(out=s_scr[:, 0:1024], in_=wd[(kc0 + 1) * 128:(kc0 + 2) * 128, :])]
            return [e.dma_start(out=kn32[:, j * 1024:(j + 1) * 1024],
                                in_=wd[(kc0 + j) * 128:(kc0 + j + 1) * 128, :])
                    for j in range(2)]
        w = [(s_dve, IZ)] if i < 2 else [(s_dve, gp_cast[i - 2])]
        ld_v = GPSTG.g(ldg, *w, sem=sem)

        def castg(e, wsb=wsb, kc0=kc0, i=i):
            if i % 2 == 0:
                e.tensor_copy(wsb[:, kc0 * 1024:(kc0 + 1) * 1024], out_t[:, 0:1024])
                return e.tensor_copy(wsb[:, (kc0 + 1) * 1024:(kc0 + 2) * 1024], s_scr[:, 0:1024])
            return e.tensor_copy(wsb[:, kc0 * 1024:(kc0 + 2) * 1024], kn32[:])
        gp_cast[i] = VE.g(castg, (sem, ld_v))
    GP_LAST_CAST = gp_cast[len(gp_pieces) - 1]

    LDX = {}

    @dma_group(NG)
    def ld_x0(e):
        return [e.dma_start(out=x_t[:, g * 1024:(g + 1) * 1024],
                            in_=x_d[g * 128:(g + 1) * 128, :]) for g in range(NG)]
    LDX[0] = SY.g(ld_x0, (s_dve, SY_XT_CAST))

    @dma_group(8)
    def ld_fqt(e):
        return [e.dma_start(out=wstage8[:, kc * 1024:(kc + 1) * 1024],
                            in_=fqt_d[kc * 128:(kc + 1) * 128, :]) for kc in range(8)]
    FW_IN = SY.g(ld_fqt, (s_dve, SY_LAST_CAST), sem=s_w)
    CAST_FFN = VE.g(lambda e: e.tensor_copy(wfn[:], wstage8[:]), (s_w, FW_IN))

    def sched_ldx(fr, *waits):
        frow = fr * F

        @dma_group(NG)
        def ld_x(e, frow=frow):
            return [e.dma_start(out=x_t[:, g * 1024:(g + 1) * 1024],
                                in_=x_d[frow + g * 128: frow + (g + 1) * 128, :])
                    for g in range(NG)]
        LDX[fr] = SY.g(ld_x, *waits)

    def build_front(fr, PV):
        par = fr % 2
        X_IN = LDX[fr]
        beta_v = beta_hb[:, par * F:(par + 1) * F]
        la_v = la_h[:]
        ela_v = ela_h[:]
        elam_v = elam_hb[:]
        bem_v = bem_h[:, par * F:(par + 1) * F]
        lam_v = lam_all[:, par * 8 * NCH:(par + 1) * 8 * NCH]
        eld_v = ela_dup[:, par * 16 * NCH:(par + 1) * 16 * NCH]
        FD = {}
        th = []

        def ln_sum(e):
            for g in range(NG):
                last = e.tensor_reduce(ln_m[g][:], x_t[:, g * 1024:(g + 1) * 1024],
                                       axis=AX.X, op=ALU.add)
            return last
        th.append(lambda: VE.g(ln_sum, (s_in, X_IN)))

        def ln_sq(e):
            for g in range(NG):
                scr = sq_scr[:] if g % 2 == 0 else xn_t[:, 0:1024]
                last = e.activation(scr, x_t[:, g * 1024:(g + 1) * 1024],
                                    AF.Square, accum_out=ln_s2[g][:])
            return last

        def _lnsq():
            lw = [(s_in, X_IN), (s_dve, W_DVE)]
            if "XQ" in PV:
                lw.append((s_dve, PV["XQ"]))
            FD["LNSQ"] = SE.g(ln_sq, *lw)
        th.append(_lnsq)

        def ln_stats1(e):
            for g in range(NG):
                e.tensor_scalar(ln_m[g][:], ln_m[g][:], 1.0 / D, None, op0=ALU.mult)
                last = e.tensor_scalar(ln_s2[g][:], ln_s2[g][:], 1.0 / D, None,
                                       op0=ALU.mult)
            return last
        th.append(lambda: VE.g(ln_stats1, (s_act, FD["LNSQ"])))

        def ln_stats2(e):
            for g in range(NG):
                last = e.tensor_tensor(ln_r[g][:], ln_m[g][:], ln_m[g][:], op=ALU.mult)
            return last
        th.append(lambda: VE.g(ln_stats2))

        def ln_stats3(e):
            for g in range(NG):
                last = e.tensor_tensor(ln_r[g][:], ln_s2[g][:], ln_r[g][:],
                                       op=ALU.subtract)
            return last
        th.append(lambda: VE.g(ln_stats3))

        def ln_stats4(e):
            for g in range(NG):
                last = e.tensor_scalar(ln_r[g][:], ln_r[g][:], 1e-5, None, op0=ALU.add)
            return last
        th.append(lambda: FD.__setitem__("LNVAR", VE.g(ln_stats4)))

        def ln_sqrt(e):
            for g in range(NG):
                last = e.activation(ln_r[g][:], ln_r[g][:], AF.Sqrt)
            return last
        th.append(lambda: FD.__setitem__("LNSTD", SE.g(ln_sqrt, (s_dve, FD["LNVAR"]))))

        def ln_recip(e):
            for g in range(NG):
                last = e.reciprocal(ln_r[g][:], ln_r[g][:])
            return last
        th.append(lambda: VE.g(ln_recip, (s_act, FD["LNSTD"])))

        def ln_apply(e):
            for g in range(NG):
                last = e.scalar_tensor_tensor(
                    xn_t[:, g * 1024:(g + 1) * 1024], x_t[:, g * 1024:(g + 1) * 1024],
                    ln_m[g][:], ln_r[g][:].broadcast_to((128, 1024)),
                    op0=ALU.subtract, op1=ALU.mult)
            return last

        def _lnapply():
            aw = []
            if "XQ" in PV:
                aw.append((s_dve, PV["XQ"]))
            FD["XN"] = VE.g(ln_apply, *aw)
            if fr + 1 < NFR:
                sched_ldx(fr + 1, (s_dve, FD["XN"]), (s_act, FD["LNSQ"]))
        th.append(_lnapply)

        for g in range(NG):
            def tr(e, g=g):
                for dc in range(8):
                    col = dc * 128
                    last = e.transpose(pt2[:, col:col + 128],
                                       xn_t[:, g * 1024 + dc * 128:g * 1024 + (dc + 1) * 128],
                                       iden[:])
                return last

            def _tr(g=g, tr=tr):
                waits = [(s_dve, FD["XN"]), (s_c, CONSTS_IN)]
                if g > 0:
                    waits.append((s_act, FD["TREV0"]))
                elif "TEV3" in PV:
                    waits.append((s_act, PV["TEV3"]()))
                else:
                    waits.append((s_act, s_act.v))
                FD["TRTE"] = TE.g(tr, *waits)
            th.append(_tr)

            def trev(e, g=g):
                for dc in range(8):
                    col = dc * 128
                    last = e.copy(xn_f[:, dc * FT + g * 128: dc * FT + (g + 1) * 128],
                                  pt2[:, col:col + 128])
                return last

            def _trev(g=g, trev=trev):
                tw = [(s_te, FD["TRTE"])]
                if g == 0 and "VTE7" in PV:
                    tw.append((s_te, PV["VTE7"]))
                FD["TREV0" if g == 0 else "XF"] = SE.g(trev, *tw)
            th.append(_trev)

        def zbf_mm(e):
            for kc in range(8):
                e.matmul(pb[4][0:16, 0:FT], wbf[:, kc * 32:kc * 32 + 16],
                         xn_f[:, kc * FT:(kc + 1) * FT],
                         start=(kc == 0), stop=False)
                last = e.matmul(pb[4][0:16, FT:2 * FT], wbf[:, kc * 32 + 16:(kc + 1) * 32],
                                xn_f[:, kc * FT:(kc + 1) * FT],
                                start=False, stop=(kc == 7))
            return last

        def _zbf():
            zw = [(s_act, FD["XF"])]
            if "GR3" in PV:
                zw.append((s_dve, PV["GR3"]()))
            else:
                zw.append((s_dve, CAST_WBF))
            FD["ZBFTE"] = TE.g(zbf_mm, *zw)
        th.append(_zbf)

        def zbf_ev(e):
            return e.tensor_copy(zbf[:], pb[4][0:16, 0:2 * FT])
        th.append(lambda: FD.__setitem__("ZBFD", VE.g(zbf_ev, (s_te, FD["ZBFTE"]))))

        def sig_ops(e):
            e.activation(beta_v, zbf[:, 0:F], AF.Sigmoid)
            return e.activation(la_v, zbf[:, F:2 * F], AF.Sigmoid)
        th.append(lambda: SE.g(sig_ops, (s_dve, FD["ZBFD"])))
        th.append(lambda: FD.__setitem__("SIG", SE.g(
            lambda e: e.activation(la_v, la_v, AF.Ln))))

        def la_ops(e):
            e.tensor_scalar(beta_v, beta_v, 2.0, None, op0=ALU.mult)
            return e.tensor_tensor_scan(la_v, resetm[:], la_v, zero16[:],
                                        op0=ALU.mult, op1=ALU.add)
        th.append(lambda: FD.__setitem__("BETAD", VE.g(la_ops, (s_act, FD["SIG"]))))
        th.append(lambda: FD.__setitem__("LAD", VE.g(
            lambda e: e.tensor_scalar(la_v, la_v, -60.0, None, op0=ALU.max))))

        def exp_ops(e):
            e.activation(ela_v, la_v, AF.Exp)
            e.activation(elam_v, la_v[:, CK - 1::CK], AF.Exp)
            return e.activation(bem_v, la_v, AF.Exp, scale=-1.0)
        th.append(lambda: FD.__setitem__("EXPA", SE.g(exp_ops, (s_dve, FD["LAD"]))))

        def bem_mul(e):
            return e.tensor_tensor(bem_v, beta_v, bem_v, op=ALU.mult)
        th.append(lambda: FD.__setitem__("BEMD", VE.g(bem_mul, (s_act, FD["EXPA"]))))

        def lam_mm(e):
            for p in range(8):
                last = e.matmul(pb[4][:, p * NCH:(p + 1) * NCH],
                                ep[:, p * 128:(p + 1) * 128], elam_v,
                                start=(p == 0), stop=(p == 7))
            return last
        th.append(lambda: FD.__setitem__("LAMTE", TE.g(
            lam_mm, (s_act, FD["EXPA"]), (s_dve, FD["ZBFD"]))))

        def lam_ev(e):
            return e.tensor_copy(lam_v, pb[4][:, 0:8 * NCH])
        th.append(lambda: FD.__setitem__("LAMD", VE.g(lam_ev, (s_te, FD["LAMTE"]))))

        def elat_mm(e):
            for c in range(NCH):
                last = e.transpose(pt2[0:64, c * 16:(c + 1) * 16],
                                   ela_v[:, c * CK:(c + 1) * CK], iden[0:16, 0:16])
            return last
        th.append(lambda: FD.__setitem__("ELATTE", TE.g(
            elat_mm, (s_act, FD["EXPA"]), (s_dve, FD["LAMD"]), (s_act, FD["XF"]))))

        def elat_ev(e):
            return e.copy(scr_h[:, 0:16 * NCH],
                          pt2[0:64, 0:16 * NCH])
        th.append(lambda: FD.__setitem__("ELATA", SE.g(elat_ev, (s_te, FD["ELATTE"]))))

        def dup_mm(e):
            for c in range(NCH):
                last = e.matmul(pb[4][:, c * 16:(c + 1) * 16], dupc[:],
                                scr_h[:, c * 16:(c + 1) * 16],
                                start=(c == 0), stop=(c == NCH - 1))
            return last
        th.append(lambda: FD.__setitem__("DUPT", TE.g(
            dup_mm, (s_act, FD["ELATA"]), (s_dve, FD["LAMD"]))))

        def dup_ev(e):
            return e.tensor_copy(eld_v, pb[4][:, 0:16 * NCH])

        def _dupev():
            dw = [(s_te, FD["DUPT"])]
            if "SPG" in PV:
                dw.append((s_sp, PV["SPG"]))
            FD["DUPD"] = VE.g(dup_ev, *dw)
        th.append(_dupev)
        return th, FD

    def _run_front0():
        th, FD = build_front(0, {})
        for t in th:
            t()
        return FD
    FRONTS = {0: _run_front0()}
    PREV_SPG = None
    for fr in range(NFR):
        frow = fr * F
        par = fr % 2
        FR_ = FRONTS[fr]
        XN_DVE, XF_ACT, LN_SQ, SIG_ACT = FR_["XN"], FR_["XF"], FR_["LNSQ"], FR_["SIG"]
        BEM_DVE, EXP_ACT = FR_["BEMD"], FR_["EXPA"]
        LAM_DVE, ELAT_ACT, DUP_TE, DUP_DVE = (FR_["LAMD"], FR_["ELATA"],
                                              FR_["DUPT"], FR_["DUPD"])
        if fr + 1 < NFR:
            pv = dict(TEV3=lambda: TEV[NCH - 1], GR3=lambda: GRDVE[NCH - 1])
            if PREV_SPG is not None:
                pv["SPG"] = PREV_SPG
            NEXT_TH, FRONTS[fr + 1] = build_front(fr + 1, pv)
        else:
            NEXT_TH, pv = [], None
        beta_vf = beta_hb[:, par * F:(par + 1) * F]
        bem_vf = bem_h[:, par * F:(par + 1) * F]
        lam_allv = lam_all[:, par * 8 * NCH:(par + 1) * 8 * NCH]
        ela_dupv = ela_dup[:, par * 16 * NCH:(par + 1) * 16 * NCH]

        def q_amax(e):
            for g in range(NG):
                last = e.tensor_reduce(qs_s[g][:], xn_t[:, g * 1024:(g + 1) * 1024],
                                       axis=AX.X, op=ALU.max, apply_absolute_value=True)
            return last
        VE.g(q_amax)

        def q_scale(e):
            for g in range(NG):
                last = e.tensor_scalar(qs_s[g][:], qs_s[g][:], 1e-6, 1.0 / 127.0,
                                       op0=ALU.max, op1=ALU.mult)
            return last
        VE.g(q_scale)

        def q_recip(e):
            for g in range(NG):
                last = e.reciprocal(qsr_s[g][:], qs_s[g][:])
            return last
        QS_DVE = VE.g(q_recip)

        XQ_DVE = QS_DVE
        M32 = 12582912.0
        M52 = 6755399441055744.0
        for g in range(NG):
            sl = slice(g * 1024, (g + 1) * 1024)
            qw = [(s_act, SIG_ACT), (s_te, DUP_TE)]
            if fr > 0:
                qw.append((s_act, PREV_GEV))
            else:
                qw.append((s_dve, GP_LAST_CAST))
            VE.g(lambda e, sl=sl, g=g: e.tensor_scalar(
                sq_scr[:], xn_t[:, sl], qsr_s[g][:], None,
                op0=ALU.mult), *qw)
            VE.g(lambda e: e.tensor_scalar(s_scr[:], sq_scr[:], M32, None, op0=ALU.add))
            VE.g(lambda e: e.tensor_scalar(s_scr[:], s_scr[:], M32, None,
                                           op0=ALU.subtract))
            VE.g(lambda e: e.tensor_scalar(r2_scr[:], sq_scr[:], M52, None, op0=ALU.add))
            VE.g(lambda e: e.tensor_scalar(r2_scr[:], r2_scr[:], M52, None,
                                           op0=ALU.subtract))
            VE.g(lambda e: e.tensor_tensor(r2_scr[:], r2_scr[:], s_scr[:],
                                           op=ALU.subtract))
            XQ_DVE = VE.g(lambda e, sl=sl: e.scalar_tensor_tensor(
                xq_t[:, sl], r2_scr[:], flag1[:], s_scr[:],
                op0=ALU.mult, op1=ALU.add))
        if fr + 1 < NFR:
            pv["XQ"] = XQ_DVE


        QEV = {}
        for p in range(8):
            def qmm(e, p=p):
                for kc in range(8):
                    last = e.matmul(pb[p % 2][:, 0:FT],
                                    wq[:, kc * 1024 + p * 128:kc * 1024 + (p + 1) * 128],
                                    xn_f[:, kc * FT:(kc + 1) * FT],
                                    start=(kc == 0), stop=(kc == 7))
                return last
            waits = [(s_act, XF_ACT)]
            if p >= 2:
                waits.append((s_act, QEV[p - 2]))
            elif fr > 0:
                waits.append((s_dve, PREV_FEV1))
            else:
                waits.append((s_dve, SY_LAST_CAST))
            qte = TE.g(qmm, *waits)

            def qev(e, p=p):
                return e.activation(q_f[:, p * FT:(p + 1) * FT], pb[p % 2][:, 0:FT], AF.Silu)
            QEV[p] = SE.g(qev, (s_te, qte))

        KEV = {}
        for p in range(8):
            def kmm(e, p=p):
                for kc in range(8):
                    last = e.matmul(pb[p % 2][:, 0:FT],
                                    wk[:, kc * 1024 + p * 128:kc * 1024 + (p + 1) * 128],
                                    xn_f[:, kc * FT:(kc + 1) * FT],
                                    start=(kc == 0), stop=(kc == 7))
                return last
            waits = [(s_act, QEV[7] if p < 2 else KEV[p - 2])]
            kte = TE.g(kmm, *waits)

            def kev(e, p=p):
                return e.activation(k_f[:, p * FT:(p + 1) * FT], pb[p % 2][:, 0:FT], AF.Silu)
            KEV[p] = SE.g(kev, (s_te, kte))

        XQTR = []
        for g in range(NG):
            def trq(e, g=g):
                for dc in range(8):
                    col = dc * 128
                    last = e.transpose(pt2[:, col:col + 128],
                                       xq_t[:, g * 1024 + dc * 128:g * 1024 + (dc + 1) * 128],
                                       iden[:])
                return last
            waits = [(s_dve, XQ_DVE)]
            waits.append((s_act, XQTR[-1]) if XQTR else (s_act, s_act.v))
            trq_te = TE.g(trq, *waits)

            def trqev(e, g=g):
                for dc in range(8):
                    col = dc * 128
                    last = e.copy(xq_f[:, dc * FT + g * 128: dc * FT + (g + 1) * 128],
                                  pt2[:, col:col + 128])
                return last
            XQTR.append(SE.g(trqev, (s_te, trq_te)))
        XQF_ACT = XQTR[-1]

        RSQ = {}
        RTE = {}
        for p in range(8):
            def rsq(e, p=p):
                return e.activation(ksq[:, 0:FT], k_f[:, p * FT:(p + 1) * FT], AF.Square)
            waits = [(s_act, KEV[7])] if p == 0 else [(s_te, RTE[p - 1])]
            RSQ[p] = SE.g(rsq, *waits)

            def rmm(e, p=p):
                return e.matmul(pb[5][0:16, 0:FT], csel[:, p * 16:(p + 1) * 16],
                                ksq[:, 0:FT], start=(p == 0), stop=(p == 7))
            rw = [(s_act, RSQ[p])]
            if p == 0 and fr > 0:
                rw.append((s_dve, PREV_QHEV3))
            RTE[p] = TE.g(rmm, *rw)

        VE.g(lambda e: e.tensor_scalar(rn_f32[:], pb[5][0:16, 0:FT], 1e-12, None,
                                       op0=ALU.add), (s_te, RTE[7]))
        RN_DVE = VE.g(lambda e: e.reciprocal(rn_f32[:], rn_f32[:]))

        def rn_sqrt(e):
            return e.activation(rn_h[:], rn_f32[:], AF.Sqrt)
        RN_ACT = SE.g(rn_sqrt, (s_dve, RN_DVE))

        VEV = {}
        for p in range(8):
            def vmm(e, p=p, bem_vf=bem_vf):
                for kc in range(8):
                    e.matmul(pb[2 + p % 2][:, 0:FT],
                             wv[:, kc * 1024 + p * 128:kc * 1024 + (p + 1) * 128],
                             xn_f[:, kc * FT:(kc + 1) * FT],
                             start=(kc == 0), stop=(kc == 7))
                return e.matmul(pb[4][:, 0:FT], ep[:, p * 128:(p + 1) * 128],
                                bem_vf, start=True, stop=True)
            waits = [(s_dve, BEM_DVE), (s_act, XF_ACT), (s_dve, DUP_DVE)]
            if p >= 1:
                waits.append((s_dve, VEV[p - 1]))
                waits.append((s_act, FCP[p - 1]))
            elif fr == 0:
                waits.append((s_dve, GP_LAST_CAST))
            vte = TE.g(vmm, *waits)

            if p == 0:
                FCP = {}
            FCP[p] = fcp = SE.g(lambda e: e.copy(field_s[:, 0:FT], pb[4][:, 0:FT]),
                                (s_te, vte))

            def vev(e, p=p):
                return e.tensor_tensor(v_f[:, p * FT:(p + 1) * FT], pb[2 + p % 2][:, 0:FT],
                                       field_s[:, 0:FT], op=ALU.mult)
            VEV[p] = VE.g(vev, (s_te, vte), (s_act, fcp))
        if fr + 1 < NFR:
            pv["VTE7"] = vte

        KNB = {}
        for p in range(8):
            def fmm(e, p=p, beta_vf=beta_vf):
                e.matmul(pb[4][:, 0:FT], ep[:, p * 128:(p + 1) * 128],
                         rn_h[:], start=True, stop=True)
                return e.matmul(pb[5][:, 0:FT], ep[:, p * 128:(p + 1) * 128],
                                beta_vf, start=True, stop=True)
            waits = [(s_act, RN_ACT), (s_dve, VEV[7]), (s_dve, FR_["BETAD"])]
            if p > 0:
                waits.append((s_dve, KNB[p - 1]))
            fte = TE.g(fmm, *waits)

            VE.g(lambda e, p=p: e.tensor_tensor(
                kn_f[:, p * FT:(p + 1) * FT], k_f[:, p * FT:(p + 1) * FT],
                pb[4][:, 0:FT], op=ALU.mult), (s_te, fte))
            KNB[p] = VE.g(lambda e, p=p: e.tensor_tensor(
                knb_f[:, p * FT:(p + 1) * FT], kn_f[:, p * FT:(p + 1) * FT],
                pb[5][:, 0:FT], op=ALU.mult))
        FIELDS = KNB[7]

        GRDVE, SQEV, SQ_FIN, SQ_TE_LAST = {}, {}, {}, {}
        CH_TE, CHEV, CH_FIN = {}, {}, {}
        SEQ1, SEQ2, DS_TE, SUPD, OEVD, WT_ACT, TEV = {}, {}, {}, {}, {}, {}, {}
        QH_TE, QHEV, SPILL_AT, OSPILL_AT = {}, {}, {}, {}

        def ksl_f(t_, h, ci):
            p, r0 = h // 2, 64 * (h % 2)
            return t_[r0:r0 + 64, p * FT + ci * CK: p * FT + ci * CK + CK]

        def build_T(ci):
            rungs = []

            def t_kn(e, ci=ci):
                for h in range(H):
                    r0 = 64 * (h % 2)
                    last = trt(e, pt2[r0:r0 + 64, h * 64:(h + 1) * 64],
                               ksl_f(kn_f, h, ci), iden[r0:r0 + 64, r0:r0 + 64], (r0, r0))
                return last
            w = [(s_dve, FIELDS), (s_te, DUP_TE), (s_act, ELAT_ACT)]
            w.append((s_act, TEV[ci - 1]) if ci > 0 else (s_act, XQF_ACT))
            if fr == 0:
                w.append((s_dve, CAST_FFN))
            rungs.append(("tensor", t_kn, w, None))

            def t_knev(e, ci=ci):
                return e.copy(knt_all[:, ci * 1024:(ci + 1) * 1024], pt2[:, 0:1024])
            rungs.append(("scalar", t_knev, "AFTER_PREV_TE", None))

            def t_knb(e, ci=ci):
                for h in range(H):
                    r0 = 64 * (h % 2)
                    last = trt(e, pt2[r0:r0 + 64, h * 64:(h + 1) * 64],
                               ksl_f(knb_f, h, ci), iden[r0:r0 + 64, r0:r0 + 64], (r0, r0))
                return last
            rungs.append(("tensor", t_knb, "AFTER_PREV_ACT", None))

            def t_knbev(e, ci=ci):
                ra = r_all[:, ci * 1024:(ci + 1) * 1024]
                e.copy(ra[0:64, :].rearrange("p (u w) -> p u w", w=128)[:, :, 64:128],
                       pt2[0:64, :].rearrange("p (u w) -> p u w", w=128)[:, :, 0:64])
                return e.copy(ra[64:128, :].rearrange("p (u w) -> p u w", w=128)[:, :, 64:128],
                              pt2[64:128, :].rearrange("p (u w) -> p u w", w=128)[:, :, 64:128])
            rungs.append(("scalar", t_knbev, "AFTER_PREV_TE", None))

            def t_v(e, ci=ci):
                for h in range(H):
                    r0 = 64 * (h % 2)
                    last = trt(e, pt2[r0:r0 + 64, h * 64:(h + 1) * 64],
                               ksl_f(v_f, h, ci), iden[r0:r0 + 64, r0:r0 + 64], (r0, r0))
                return last
            rungs.append(("tensor", t_v, "AFTER_PREV_ACT", None))

            def t_vev(e, ci=ci):
                ra = r_all[:, ci * 1024:(ci + 1) * 1024]
                e.copy(ra[0:64, :].rearrange("p (u w) -> p u w", w=128)[:, :, 0:64],
                       pt2[0:64, :].rearrange("p (u w) -> p u w", w=128)[:, :, 0:64])
                return e.copy(ra[64:128, :].rearrange("p (u w) -> p u w", w=128)[:, :, 0:64],
                              pt2[64:128, :].rearrange("p (u w) -> p u w", w=128)[:, :, 64:128])
            rungs.append(("scalar", t_vev, "AFTER_PREV_TE", ("TEV", ci)))
            return rungs

        def build_M(ci):
            rungs = []
            sl2, sl3 = ci % 2, ci % 3

            def m_gram(e, ci=ci):
                last = None
                for h in range(H):
                    r0 = 64 * (h % 2)
                    tp = (r0, r0)
                    cc = (h // 2) * 64
                    mmt(e, pb[2][r0:r0 + 64, cc:cc + 64], ksl_f(kn_f, h, ci),
                        ksl_f(knb_f, h, ci), (h < 2), (h >= H - 2), tp)
                    mmt(e, pb[3][r0:r0 + 64, cc:cc + 64], ksl_f(kn_f, h, ci),
                        ksl_f(q_f, h, ci), (h < 2), (h >= H - 2), tp)
                    last = mmt(e, pb[4][r0:r0 + 64, cc:cc + 64], ksl_f(knb_f, h, ci),
                               ksl_f(kn_f, h, ci), (h < 2), (h >= H - 2), tp)
                return last
            w = [(s_dve, FIELDS), (s_dve, DUP_DVE)]
            if ci > 0:
                w += [(s_dve, SQ_FIN[ci - 1]), (s_dve, GRDVE[ci - 1])]
            rungs.append(("tensor", m_gram, w, ("GR_TE", ci)))

            def m_gramev(e, ci=ci, sl2=sl2, sl3=sl3):
                v3 = lambda t: t.rearrange("p (a b) -> p a b", b=64)
                e.scalar_tensor_tensor(v3(t0_d[sl2][:]), v3(pb[2][:]), -1.0, mup,
                                       op0=ALU.mult, op1=ALU.mult)
                e.tensor_tensor(v3(dt_d[sl3][:]), v3(pb[3][:]), mui, op=ALU.mult)
                return e.scalar_tensor_tensor(v3(p0_s[:]), v3(pb[4][:]), -1.0, mlo,
                                              op0=ALU.mult, op1=ALU.mult)
            w = ["AFTER_PREV_TE"]
            if ci >= 2:
                w.append((s_te, CH_TE[(ci - 2, 5)]))
            if ci >= 1:
                w.append((s_te, SQ_TE_LAST[ci - 1]))
            if ci >= 3:
                w.append((s_te, QH_TE[ci - 3]))
            rungs.append(("vector", m_gramev, w, ("GRDVE", ci)))

            for lev in range(5):
                def m_sq(e, ci=ci, lev=lev, sl2=sl2):
                    tprev = t0_d[sl2] if lev == 0 else tl_d[sl2][lev - 1]
                    pprev = p0_s if lev == 0 else pl_s[lev - 1]
                    last = None
                    for h in range(H):
                        r0 = 64 * (h % 2)
                        tp = (r0, r0)
                        cc = (h // 2) * 64
                        rsl, csl = slice(r0, r0 + 64), slice(cc, cc + 64)
                        last = mmt(e, pb[2][rsl, csl], pprev[rsl, csl],
                                   tprev[rsl, csl], (h < 2), (h >= H - 2), tp)
                        if lev < 4:
                            last = mmt(e, pb[3][rsl, csl], tprev[rsl, csl],
                                       pprev[rsl, csl], (h < 2), (h >= H - 2), tp)
                    return last
                tag = ("SQ_TE_LAST", ci) if lev == 4 else None
                rungs.append(("tensor", m_sq, ["AFTER_PREV_DVE"], tag))

                def m_sqev(e, ci=ci, lev=lev, sl2=sl2):
                    last = e.tensor_copy(tl_d[sl2][lev][:], pb[2][:])
                    if lev < 4:
                        last = e.tensor_copy(pl_s[lev][:], pb[3][:])
                    return last
                w = ["AFTER_PREV_TE"]
                if ci >= 2:
                    w.append((s_te, CH_TE[(ci - 2, 5)]))
                tag = ("SQEV", (ci, lev))
                if lev == 4:
                    rungs.append(("vector", m_sqev, w, [tag, ("SQ_FIN", ci)]))
                else:
                    rungs.append(("vector", m_sqev, w, tag))
            return rungs

        def build_C(ci):
            rungs = []
            sl2 = ci % 2
            for lev in range(6):
                def c_mm(e, ci=ci, lev=lev, sl2=sl2):
                    tlev = t0_d[sl2] if lev == 0 else tl_d[sl2][lev - 1]
                    rsrc = r_all[:, ci * 1024:(ci + 1) * 1024] if lev == 0 else xch_s
                    last = None
                    for h in range(H):
                        r0 = 64 * (h % 2)
                        tp = (r0, r0)
                        cc = (h // 2) * 128
                        bank = pb[0] if cc < 512 else pb[1]
                        ccc = cc % 512
                        last = mmt(e, bank[r0:r0 + 64, ccc:ccc + 128],
                                   tlev[r0:r0 + 64, (h // 2) * 64:(h // 2) * 64 + 64],
                                   rsrc[r0:r0 + 64, cc:cc + 128],
                                   (h % 8 < 2), (h % 8 >= 6), tp)
                    return last
                if lev == 0:
                    w = [(s_act, TEV[ci]), (s_dve, GRDVE[ci])]
                    if ci > 0:
                        w.append((s_dve, CH_FIN[ci - 1]))
                        w.append((s_te, OT_TE[ci - 1]) if False else (s_dve, CH_FIN[ci - 1]))
                else:
                    w = ["AFTER_PREV_DVE", (s_dve, SQEV[(ci, lev - 1)])]
                rungs.append(("tensor", c_mm, w, ("CH_TE", (ci, lev))))

                if lev == 0:
                    def c_ev0(e, ci=ci):
                        ra = r_all[:, ci * 1024:(ci + 1) * 1024]
                        e.tensor_tensor(xch_s[:, 0:512], ra[:, 0:512], pb[0][:],
                                        op=ALU.add)
                        return e.tensor_tensor(xch_s[:, 512:1024], ra[:, 512:1024],
                                               pb[1][:], op=ALU.add)
                    w = ["AFTER_PREV_TE"]
                    if ci >= 1:
                        w.append((s_te, CH_TE[(ci - 1, 5)]))
                    rungs.append(("vector", c_ev0, w, ("CHEV", (ci, 0))))
                elif lev < 5:
                    def c_ev(e, ci=ci, lev=lev):
                        e.tensor_tensor(xch_s[:, 0:512], xch_s[:, 0:512], pb[0][:],
                                        op=ALU.add)
                        return e.tensor_tensor(xch_s[:, 512:1024], xch_s[:, 512:1024],
                                               pb[1][:], op=ALU.add)
                    rungs.append(("vector", c_ev, ["AFTER_PREV_TE"], ("CHEV", (ci, lev))))
                else:
                    def c_split(e, ci=ci, sl2=sl2):
                        xv = xch_s.rearrange("p (u w) -> p u w", w=128)
                        e.tensor_tensor(vt_d[sl2][:, 0:256].rearrange("p (u w) -> p u w", w=64),
                                        xv[:, 0:4, 0:64],
                                        pb[0][:].rearrange("p (u w) -> p u w", w=128)[:, :, 0:64],
                                        op=ALU.add)
                        e.tensor_tensor(wt_d[sl2][:, 0:256].rearrange("p (u w) -> p u w", w=64),
                                        xv[:, 0:4, 64:128],
                                        pb[0][:].rearrange("p (u w) -> p u w", w=128)[:, :, 64:128],
                                        op=ALU.add)
                        e.tensor_tensor(vt_d[sl2][:, 256:512].rearrange("p (u w) -> p u w", w=64),
                                        xv[:, 4:8, 0:64],
                                        pb[1][:].rearrange("p (u w) -> p u w", w=128)[:, :, 0:64],
                                        op=ALU.add)
                        return e.tensor_tensor(wt_d[sl2][:, 256:512].rearrange("p (u w) -> p u w", w=64),
                                               xv[:, 4:8, 64:128],
                                               pb[1][:].rearrange("p (u w) -> p u w", w=128)[:, :, 64:128],
                                               op=ALU.add)
                    w = ["AFTER_PREV_TE"]
                    if ci >= 2:
                        w.append((s_te, QH_TE[ci - 2]))
                    rungs.append(("vector", c_split, w, ("CH_FIN", ci)))
            return rungs

        def build_S(ci):
            rungs = []
            sl2, sl3 = ci % 2, ci % 3

            def s_wt(e, ci=ci, sl2=sl2):
                for h in range(H):
                    r0 = 64 * (h % 2)
                    last = trt(e, pt[r0:r0 + 64, h * 64:(h + 1) * 64],
                               wt_d[sl2][r0:r0 + 64, (h // 2) * 64:(h // 2) * 64 + 64],
                               iden[r0:r0 + 64, r0:r0 + 64], (r0, r0))
                return last
            w = [(s_dve, CH_FIN[ci])]
            if ci > 0:
                w.append((s_act, WT_ACT[ci - 1]))
            rungs.append(("tensor", s_wt, w, None))

            def s_wtev(e, ci=ci, sl2=sl2):
                e.copy(wfm_d[sl2][0:64, :].rearrange("p (u w) -> p u w", w=64),
                       pt[0:64, 0:1024].rearrange("p (u w) -> p u w", w=128)[:, :, 0:64])
                return e.copy(wfm_d[sl2][64:128, :].rearrange("p (u w) -> p u w", w=64),
                              pt[64:128, 0:1024].rearrange("p (u w) -> p u w", w=128)[:, :, 64:128])
            rungs.append(("scalar", s_wtev, "AFTER_PREV_TE", ("WT_ACT", ci)))

            def s_vcorr(e, ci=ci, sl2=sl2):
                last = None
                for h in range(H):
                    r0 = 64 * (h % 2)
                    tp = (r0, r0)
                    cc = (h // 2) * 64
                    rsl, csl = slice(r0, r0 + 64), slice(cc, cc + 64)
                    last = mmt(e, pb[5][rsl, csl], wfm_d[sl2][rsl, csl],
                               state_b[rsl, csl], (h < 2), (h >= H - 2), tp)
                return last
            w = ["AFTER_PREV_ACT"]
            if ci > 0:
                w.append((s_dve, QHEV[ci - 1]))
            else:
                w.append((s_dve, W_DVE))
            rungs.append(("tensor", s_vcorr, w, ("SEQ1", ci)))

            def s_vnew(e, ci=ci, sl2=sl2):
                return e.tensor_tensor(vt_d[sl2][:], vt_d[sl2][:], pb[5][:],
                                       op=ALU.subtract)
            rungs.append(("vector", s_vnew, "AFTER_PREV_TE", None))

            def s_omm(e, ci=ci, sl2=sl2, sl3=sl3):
                last = None
                for h in range(H):
                    r0 = 64 * (h % 2)
                    tp = (r0, r0)
                    cc = (h // 2) * 64
                    rsl, csl = slice(r0, r0 + 64), slice(cc, cc + 64)
                    mmt(e, pb[5][rsl, csl], ksl_f(q_f, h, ci),
                        state_b[rsl, csl], (h < 2), False, tp)
                    last = mmt(e, pb[5][rsl, csl], dt_d[sl3][rsl, csl],
                               vt_d[sl2][rsl, csl], False, (h >= H - 2), tp)
                return last
            rungs.append(("tensor", s_omm, "AFTER_PREV_DVE", ("SEQ2", ci)))

            def s_oev(e, ci=ci, ela_dupv=ela_dupv):
                ed = ela_dupv[:, ci * 16:(ci + 1) * 16].rearrange("p (u t) -> p u t", t=2)
                e.tensor_tensor(
                    osc_s[0:64, :].rearrange("p (u w) -> p u w", w=64),
                    pb[5][0:64, :].rearrange("p (u w) -> p u w", w=64),
                    ed[0:64, :, 0:1].broadcast_to((64, 8, 64)), op=ALU.mult)
                return e.tensor_tensor(
                    osc_s[64:128, :].rearrange("p (u w) -> p u w", w=64),
                    pb[5][64:128, :].rearrange("p (u w) -> p u w", w=64),
                    ed[64:128, :, 1:2].broadcast_to((64, 8, 64)), op=ALU.mult)
            w = ["AFTER_PREV_TE", (s_dve, DUP_DVE)]
            if ci > 0:
                w.append((s_spo, OSPILL_AT[ci - 1]))
            elif s_spo.v:
                w.append((s_spo, s_spo.v))
            rungs.append(("vector", s_oev, w, ("OEVD", ci)))

            def s_ds(e, ci=ci, sl2=sl2):
                last = None
                for h in range(H):
                    r0 = 64 * (h % 2)
                    tp = (r0, r0)
                    cc = (h // 2) * 64
                    rsl, csl = slice(r0, r0 + 64), slice(cc, cc + 64)
                    last = mmt(e, pb[5][rsl, csl],
                               knt_all[rsl, ci * 1024 + h * 64: ci * 1024 + (h + 1) * 64],
                               vt_d[sl2][rsl, csl], (h < 2), (h >= H - 2), tp)
                return last
            rungs.append(("tensor", s_ds, "AFTER_PREV_DVE", ("DS_TE", ci)))

            def s_sup1(e):
                return e.tensor_tensor(state[:], state[:], pb[5][:], op=ALU.add)
            rungs.append(("vector", s_sup1, ["AFTER_PREV_TE", (s_dve, LAM_DVE)], None))

            def s_sup2(e, ci=ci, lam_allv=lam_allv):
                lam_c = lam_allv.rearrange("p (u c) -> p u c", c=NCH)[:, :, ci:ci + 1]
                return e.tensor_tensor(
                    state[:].rearrange("p (u w) -> p u w", w=64),
                    state[:].rearrange("p (u w) -> p u w", w=64),
                    lam_c.broadcast_to((128, 8, 64)), op=ALU.mult)
            rungs.append(("vector", s_sup2, "AFTER_PREV_DVE", None))

            def s_sup3(e):
                return e.tensor_copy(state_b[:], state[:])
            rungs.append(("vector", s_sup3, "AFTER_PREV_DVE", ("SUPD", ci)))

            gci = fr * NCH + ci

            def s_ospill(e, gci=gci):
                return [e.dma_start(out=oz_d[gci], in_=osc_s[:])]
            s_ospill.n_dma = 1
            w = ["AFTER_PREV_DVE"] + ([(s_spo, s_spo.v)] if s_spo.v else [])
            rungs.append(("spill_o", s_ospill, w, ("OSPILL_AT", ci)))

            def s_mh(e, ci=ci, sl2=sl2):
                last = None
                for h in range(H):
                    r0 = 64 * (h % 2)
                    tp = (r0, r0)
                    cc = (h // 2) * 64
                    rsl, csl = slice(r0, r0 + 64), slice(cc, cc + 64)
                    last = mmt(e, pb[5][rsl, csl], wt_d[sl2][rsl, csl],
                               knt_all[rsl, ci * 1024 + h * 64: ci * 1024 + (h + 1) * 64],
                               (h < 2), (h >= H - 2), tp)
                return last
            rungs.append(("tensor", s_mh, "AFTER_PREV_DVE", None))

            def s_mhev(e, ci=ci):
                v3 = lambda t: t.rearrange("p (a b) -> p a b", b=64)
                e.tensor_tensor(v3(o_f[0:64, 0:512]),
                                iden[0:64, 0:64].unsqueeze(1).broadcast_to((64, 8, 64)),
                                v3(pb[5][0:64, :]), op=ALU.subtract)
                return e.tensor_tensor(v3(o_f[64:128, 0:512]),
                                       iden[64:128, 64:128].unsqueeze(1).broadcast_to((64, 8, 64)),
                                       v3(pb[5][64:128, :]), op=ALU.subtract)
            w = ["AFTER_PREV_TE", (s_dve, LAM_DVE)]
            if ci > 0:
                w.append((s_spq, SPILL_AT[ci - 1]))
            elif s_spq.v:
                w.append((s_spq, s_spq.v))
            rungs.append(("vector", s_mhev, w, None))

            def s_mhev2(e, ci=ci, lam_allv=lam_allv):
                v3 = lambda t: t.rearrange("p (a b) -> p a b", b=64)
                lam_c = lam_allv.rearrange("p (u c) -> p u c", c=NCH)[:, :, ci:ci + 1]
                return e.tensor_tensor(v3(o_f[:, 0:512]), v3(o_f[:, 0:512]),
                                       lam_c.broadcast_to((128, 8, 64)), op=ALU.mult)
            rungs.append(("vector", s_mhev2, "AFTER_PREV_DVE", None))

            def s_qh(e, ci=ci, sl2=sl2, sl3=sl3):
                last = None
                for h in range(H):
                    r0 = 64 * (h % 2)
                    tp = (r0, r0)
                    cc = (h // 2) * 64
                    rsl, csl = slice(r0, r0 + 64), slice(cc, cc + 64)
                    last = mmt(e, pb[5][rsl, csl], wt_d[sl2][rsl, csl],
                               dt_d[sl3][rsl, csl], (h < 2), (h >= H - 2), tp)
                return last
            rungs.append(("tensor", s_qh, "AFTER_PREV_DVE", ("QH_TE", ci)))

            def s_qhev(e, ci=ci):
                qv = q_f.rearrange("p (u t) -> p u t", t=FT)[:, :, ci * CK:(ci + 1) * CK]
                return e.tensor_tensor(o_f[:, 512:1024].rearrange("p (a b) -> p a b", b=64),
                                       qv, pb[5][:].rearrange("p (a b) -> p a b", b=64),
                                       op=ALU.subtract)
            rungs.append(("vector", s_qhev, "AFTER_PREV_TE", ("QHEV", ci)))

            def s_mqspill(e, gci=gci):
                return [e.dma_start(out=mh_d[gci], in_=o_f[:, 0:512]),
                        e.dma_start(out=qh_d[gci], in_=o_f[:, 512:1024])]
            s_mqspill.n_dma = 2
            w = ["AFTER_PREV_DVE"] + ([(s_spq, s_spq.v)] if s_spq.v else [])
            rungs.append(("spill_q", s_mqspill, w, ("SPILL_AT", ci)))
            return rungs

        def emit_rungs(rungs, own):
            kind, fn, waits, tag = rungs
            if isinstance(waits, str):
                waits = [waits]
            rw = []
            for x in waits:
                if x == "AFTER_PREV_TE":
                    if own.get("tensor"):
                        rw.append((s_te, own["tensor"]))
                elif x == "AFTER_PREV_DVE":
                    if own.get("vector"):
                        rw.append((s_dve, own["vector"]))
                elif x == "AFTER_PREV_ACT":
                    if own.get("scalar"):
                        rw.append((s_act, own["scalar"]))
                else:
                    rw.append(x)
            if kind in ("spill", "spill_o", "spill_q"):
                sem = {"spill": s_sp, "spill_o": s_spo, "spill_q": s_spq}[kind]
                val = SY.g(fn, *rw, sem=sem)
                own[kind] = val
            else:
                eng = {"tensor": TE, "vector": VE, "scalar": SE}[kind]
                val = eng.g(fn, *rw)
                own[kind] = val
            if tag is not None:
                tags = tag if isinstance(tag, list) else [tag]
                for tname, tkey in tags:
                    globals_map[tname][tkey] = val
            return val

        globals_map = dict(GR_TE={}, GRDVE=GRDVE, SQEV=SQEV, SQ_FIN=SQ_FIN,
                           SQ_TE_LAST=SQ_TE_LAST, CH_TE=CH_TE, CHEV=CHEV,
                           CH_FIN=CH_FIN, SEQ1=SEQ1, SEQ2=SEQ2, DS_TE=DS_TE,
                           SUPD=SUPD, OEVD=OEVD, WT_ACT=WT_ACT, TEV=TEV,
                           QH_TE=QH_TE, QHEV=QHEV, SPILL_AT=SPILL_AT, OSPILL_AT=OSPILL_AT)

        th_i = 0
        for it in range(NCH + 3):
            stages = []
            if it < NCH:
                stages.append(("T", it, build_T(it), {}))
            if 0 <= it - 1 < NCH:
                stages.append(("M", it - 1, build_M(it - 1), {}))
            if 0 <= it - 2 < NCH:
                stages.append(("C", it - 2, build_C(it - 2), {}))
            if 0 <= it - 3 < NCH:
                stages.append(("S", it - 3, build_S(it - 3), {}))
            mx = max(len(s[2]) for s in stages) if stages else 0
            for r in range(mx):
                for (_, _, l, own) in stages:
                    if r < len(l):
                        emit_rungs(l[r], own)
                if it >= NCH + 1 and th_i < len(NEXT_TH):
                    NEXT_TH[th_i]()
                    th_i += 1
        while th_i < len(NEXT_TH):
            NEXT_TH[th_i]()
            th_i += 1

        GEV = {}
        for g in range(NG):
            def ffn_mm(e, g=g):
                for hh in range(2):
                    for kc in range(8):
                        last = e.matmul(pb[hh][:, 0:512],
                                        xq_f[:, kc * FT + g * 128: kc * FT + (g + 1) * 128],
                                        wfn[:, kc * 1024 + hh * 512: kc * 1024 + (hh + 1) * 512],
                                        start=(kc == 0), stop=(kc == 7))
                return last
            waits = [(s_act, XQF_ACT), (s_dve, CAST_FFN)]
            if g > 0:
                waits.append((s_dve, FEV1[g - 1]))
            else:
                waits.append((s_dve, CH_FIN[NCH - 1]))
            FTE = TE.g(ffn_mm, *waits)

            def ffn_ev1(e, g=g):
                for hh in range(2):
                    sl = slice(hh * 512, (hh + 1) * 512)
                    last = e.scalar_tensor_tensor(s_scr[:, sl], pb[hh][:, 0:512],
                                                  qs_s[g][:], fws_b[:, sl],
                                                  op0=ALU.mult, op1=ALU.mult)
                return last
            fdv_waits = [(s_te, FTE)]
            if g > 0:
                fdv_waits.append((s_act, GEV[g - 1]))
            if g == 0:
                FEV1 = {}
            FEV1[g] = VE.g(ffn_ev1, *fdv_waits)

            def ffn_ev2(e, g=g):
                for hh in range(2):
                    sl = slice(hh * 512, (hh + 1) * 512)
                    last = e.tensor_tensor(s_scr[:, sl], s_scr[:, sl], fbias_b[:, sl],
                                           op=ALU.add)
                return last
            FDV = VE.g(ffn_ev2)

            def ffn_silu(e, g=g):
                return e.activation(g_t[:, g * 1024:(g + 1) * 1024], s_scr[:], AF.Silu)
            sw = [(s_dve, FDV)]
            if g == 0 and fr > 0:
                sw.append((s_sp, PREV_SPG))
            GEV[g] = SE.g(ffn_silu, *sw)


        @dma_group(NG + 1)
        def sp_g(e, fr=fr, frow=frow, ela_dupv=ela_dupv):
            a = [e.dma_start(out=gz_d[frow + g * 128: frow + (g + 1) * 128, :],
                             in_=g_t[:, g * 1024:(g + 1) * 1024]) for g in range(NG)]
            a.append(e.dma_start(out=eld_d[fr], in_=ela_dupv))
            return a
        w = [(s_act, GEV[NG - 1]), (s_dve, DUP_DVE)] + ([(s_sp, s_sp.v)] if s_sp.v else [])
        SPG = SY.g(sp_g, *w, sem=s_sp)
        PREV_SPG = SPG
        PREV_FEV1 = FEV1[NG - 1]
        PREV_GEV = GEV[NG - 1]
        PREV_QHEV3 = QHEV[NCH - 1]
        LAST_SUPD = SUPD[NCH - 1]
        LAST_SPILL = (s_sp.v, s_spo.v, s_spq.v)

    sc_t = k_f[:, 0:512]
    P1_TE_END = s_te.v
    P1_DVE_END = s_dve.v
    P1_ACT_END = s_act.v

    @dma_group(1)
    def st_dma1(e):
        return [e.dma_start(out=scin_d, in_=state_b[:])]
    STD1 = SY.g(st_dma1, (s_dve, LAST_SUPD), sem=s_st)

    def cc_seq(e, STD1=STD1):
        e.wait_ge(s_st.h, STD1)
        e.collective_compute(
            "AllGather", ALU.bypass, replica_groups=groups,
            ins=[scin_d.opt()], outs=[scout_d.opt()]).then_inc(s_cc.h, 1)
        return e.nop()
    GP.g(cc_seq)
    s_cc.v += 1
    CC_GP = s_cc.v

    @dma_group(1)
    def st_dma2(e):
        return [e.dma_start(out=sc_t, in_=scout_d[0:128, :])]
    STD2 = SY.g(st_dma2, (s_cc, CC_GP), sem=s_st)

    def steer1(e):
        return e.tensor_scalar(state[:], sc_t, shalf[:], None, op0=ALU.mult)
    ST1 = VE.g(steer1, (s_st, STD2))

    def steer2(e):
        return e.tensor_copy(state_b[:], state[:])
    ST_DVE = VE.g(steer2, (s_dve, ST1))

    x2v = wq.bitcast(F32)
    g2v = wk
    x_p = lambda sl: x2v[:, sl * 2048:(sl + 1) * 2048]
    g_p = lambda sl: g2v[:, sl * 2048:(sl + 1) * 2048]
    ela_p = lambda sl: g2v[:, 4096 + sl * 64: 4096 + (sl + 1) * 64]
    for fr in range(NFR):
        frow = fr * F
        sl = fr % 2
        mh_t = lambda s: kn_f[:, s * 1024: s * 1024 + 1024]
        qh_t = lambda s: knb_f[:, s * 1024: s * 1024 + 1024]
        oz_t = lambda s: v_f[:, s * 1024: s * 1024 + 1024]
        mh_t = lambda s: kn_f[:, s * 512: (s + 1) * 512]
        qh_t = lambda s: knb_f[:, s * 512: (s + 1) * 512]
        oz_t = lambda s: v_f[:, s * 512: (s + 1) * 512]

        @dma_group(NG + NG + 1)
        def ld2(e, fr=fr, frow=frow, sl=sl):
            a = [e.dma_start(out=g_p(sl)[:, g * 1024:(g + 1) * 1024],
                             in_=gz_d[frow + g * 128: frow + (g + 1) * 128, :])
                 for g in range(NG)]
            a += [e.dma_start(out=x_p(sl)[:, g * 1024:(g + 1) * 1024],
                              in_=x_d[frow + g * 128: frow + (g + 1) * 128, :])
                  for g in range(NG)]
            a.append(e.dma_start(out=ela_p(sl), in_=eld_d[fr]))
            return a
        w2 = [(s_sp, s_sp.v), (s_spo, LAST_SPILL[1]), (s_spq, LAST_SPILL[2])]
        if fr == 0:
            w2.append((s_te, P1_TE_END))
            w2.append((s_dve, P1_DVE_END))
            w2.append((s_act, P1_ACT_END))
        elif fr >= 2:
            w2.append((s_dve, P2FE[fr - 2]))
        LD2 = SY.g(ld2, *w2, sem=s_sp)

        CHLD = {}
        HOMS = {}
        P2SUP = {}
        P2OEV = {}
        P2OT = {}
        P2OA = {}
        for ci in range(NCH):
            gci = fr * NCH + ci
            s2 = ci % 2

            @dma_group(3)
            def ldc(e, gci=gci, s2=s2):
                return [e.dma_start(out=mh_t(s2), in_=mh_d[gci]),
                        e.dma_start(out=qh_t(s2), in_=qh_d[gci]),
                        e.dma_start(out=oz_t(s2), in_=oz_d[gci])]
            w = [(s_sp, s_sp.v), (s_spo, LAST_SPILL[1]), (s_spq, LAST_SPILL[2])]
            if ci >= 2:
                w.append((s_te, P2OT[ci - 2]))
            elif fr > 0:
                w.append((s_te, PREV_HOM[ci + 2]))
                w.append((s_dve, PREV_OEV[ci + 2]))
            CHLD[ci] = SY.g(ldc, *w, sem=s_sp)

            def hom_mm(e, ci=ci, s2=s2):
                last = None
                for h in range(H):
                    r0 = 64 * (h % 2)
                    tp = (r0, r0)
                    cc = (h // 2) * 64
                    rsl, csl = slice(r0, r0 + 64), slice(cc, cc + 64)
                    mmt(e, pb[4][rsl, csl], mh_t(s2)[rsl, csl], state_b[rsl, csl],
                        (h < 2), (h >= H - 2), tp)
                    last = mmt(e, pb[5][rsl, csl], qh_t(s2)[rsl, csl],
                               state_b[rsl, csl], (h < 2), (h >= H - 2), tp)
                return last
            w = [(s_sp, CHLD[ci])]
            if ci > 0:
                w.append((s_dve, P2SUP[ci - 1]))
                w.append((s_dve, P2OEV[ci - 1]))
            elif fr == 0:
                w.append((s_dve, ST_DVE))
            else:
                w.append((s_dve, PREV_SUP3))
                w.append((s_dve, PREV_OEV[3]))
            HOM = TE.g(hom_mm, *w)
            HOMS[ci] = HOM

            def p2_oc(e, ci=ci, s2=s2, sl=sl):
                ed = ela_p(sl)[:, ci * 16:(ci + 1) * 16].rearrange("p (u t) -> p u t", t=2)
                e.scalar_tensor_tensor(
                    osc_s[0:64, :].rearrange("p (u w) -> p u w", w=64),
                    pb[5][0:64, :].rearrange("p (u w) -> p u w", w=64),
                    1.0, ed[0:64, :, 0:1].broadcast_to((64, 8, 64)),
                    op0=ALU.mult, op1=ALU.mult)
                return e.scalar_tensor_tensor(
                    osc_s[64:128, :].rearrange("p (u w) -> p u w", w=64),
                    pb[5][64:128, :].rearrange("p (u w) -> p u w", w=64),
                    1.0, ed[64:128, :, 1:2].broadcast_to((64, 8, 64)),
                    op0=ALU.mult, op1=ALU.mult)
            w = [(s_te, HOM)]
            if ci >= 1:
                w.append((s_te, P2OT[ci - 1]))
            P2OC1 = VE.g(p2_oc, *w)

            def p2_oc2(e, s2=s2):
                return e.tensor_tensor(osc_s[:], osc_s[:], oz_t(s2), op=ALU.add)
            P2OEV[ci] = VE.g(p2_oc2, (s_dve, P2OC1))

            def p2_sup(e):
                return e.tensor_copy(state_b[:], pb[4][:])
            P2SUP[ci] = VE.g(p2_sup, (s_te, HOM))

            def p2_ot(e, ci=ci):
                for h in range(H):
                    r0 = 64 * (h % 2)
                    last = trt(e, pt[r0:r0 + 64, h * 64:(h + 1) * 64],
                               osc_s[r0:r0 + 64, (h // 2) * 64:(h // 2) * 64 + 64],
                               iden[r0:r0 + 64, r0:r0 + 64], (r0, r0))
                return last
            w = [(s_dve, P2OEV[ci])]
            if ci >= 1:
                w.append((s_act, P2OA[ci - 1]))
            elif fr > 0:
                w.append((s_act, PREV_P2OA3))
            P2OT[ci] = TE.g(p2_ot, *w)

            def p2_oa(e, ci=ci):
                tk0 = ci * CK
                e.copy(o_f[0:64, :].rearrange("p (u w) -> p u w", w=FT)[:, :, tk0:tk0 + 64],
                       pt[0:64, 0:1024].rearrange("p (u w) -> p u w", w=128)[:, :, 0:64])
                return e.copy(o_f[64:128, :].rearrange("p (u w) -> p u w", w=FT)[:, :, tk0:tk0 + 64],
                              pt[64:128, 0:1024].rearrange("p (u w) -> p u w", w=128)[:, :, 64:128])
            aw = [(s_te, P2OT[ci])]
            if ci == 0 and fr > 0:
                aw.append((s_dve, PREV_ONF7))
                aw.append((s_te, PREV_WTE))
            P2OA[ci] = SE.g(p2_oa, *aw)

        ksqb = [ksq, field_s]
        OSQ = {}
        OTE = {}
        for p in range(8):
            def osq(e, p=p):
                return e.activation(ksqb[p % 2][:, 0:FT], o_f[:, p * FT:(p + 1) * FT],
                                    AF.Square)
            waits = [(s_act, P2OA[NCH - 1])] if p == 0 else                     ([(s_te, OTE[p - 2])] if p >= 2 else [])
            OSQ[p] = SE.g(osq, *waits)

            def omm(e, p=p):
                return e.matmul(pb[0][0:16, 0:FT], csel[:, p * 16:(p + 1) * 16],
                                ksqb[p % 2][:, 0:FT], start=(p == 0), stop=(p == 7))
            ow = [(s_act, OSQ[p])]
            if p == 0:
                ow.append((s_dve, ST_DVE if fr == 0 else PREV_ONF7))
            OTE[p] = TE.g(omm, *ow)

        VE.g(lambda e: e.tensor_scalar(rms_h[:], pb[0][0:16, 0:FT], 1.0 / HD, 1e-6,
                                       op0=ALU.mult, op1=ALU.add), (s_te, OTE[7]))
        RMS_DVE = VE.g(lambda e: e.reciprocal(rms_h[:], rms_h[:]))

        def rms_sqrt(e):
            return e.activation(rms_hb[:], rms_h[:], AF.Sqrt)
        RMS_ACT = SE.g(rms_sqrt, (s_dve, RMS_DVE))

        ONF = {}
        for p in range(8):
            def rfmm(e, p=p):
                bank = pb[0] if p % 2 == 0 else pb[1]
                return e.matmul(bank[:, 0:FT], ep[:, p * 128:(p + 1) * 128],
                                rms_hb[:], start=True, stop=True)
            waits = [(s_act, RMS_ACT)]
            if p >= 2:
                waits.append((s_dve, ONF[p - 2]))
            elif p == 1:
                waits.append((s_dve, ST_DVE if fr == 0 else P2FE[fr - 1]))
            rte = TE.g(rfmm, *waits)

            def onev(e, p=p):
                bank = pb[0] if p % 2 == 0 else pb[1]
                return e.scalar_tensor_tensor(o_f[:, p * FT:(p + 1) * FT],
                                              o_f[:, p * FT:(p + 1) * FT],
                                              normw_f[p][:], bank[:, 0:FT],
                                              op0=ALU.mult, op1=ALU.mult)
            ONF[p] = VE.g(onev, (s_te, rte))

        pt2f = pt2.bitcast(F32)
        WOBK = [(pb[2], pb[3]), (pb[1], pt2f)]
        g32 = g_t.bitcast(F32)
        ODST = [out_t, g32]
        for g in range(NG):
            for kc in range(8):
                def wo_mm(e, g=g, kc=kc):
                    for hh in range(2):
                        last = e.matmul(WOBK[g][hh][:, 0:512],
                                        o_f[:, kc * FT + g * 128: kc * FT + (g + 1) * 128],
                                        wo[:, kc * 1024 + hh * 512: kc * 1024 + (hh + 1) * 512],
                                        start=(kc == 0), stop=(kc == 7))
                    return last
                waits = [(s_dve, ONF[kc])]
                if kc == 0 and fr > 0:
                    waits.append((s_out, PREV_OUT[g]))
                WTE = TE.g(wo_mm, *waits)

            def final_ev1(e, g=g, slf=sl):
                for hh in range(2):
                    last = e.tensor_tensor(
                        ODST[g][:, hh * 512:(hh + 1) * 512], WOBK[g][hh][:, 0:512],
                        g_p(slf)[:, g * 1024 + hh * 512:g * 1024 + (hh + 1) * 512],
                        op=ALU.mult)
                return last
            ew1 = [(s_te, WTE), (s_sp, CHLD[NCH - 1])]
            if fr > 0:
                ew1.append((s_out, PREV_OUT[g]))
            VE.g(final_ev1, *ew1)

            def final_ev2(e, g=g, slf=sl):
                for hh in range(2):
                    last = e.tensor_tensor(
                        ODST[g][:, hh * 512:(hh + 1) * 512], ODST[g][:, hh * 512:(hh + 1) * 512],
                        x_p(slf)[:, g * 1024 + hh * 512:g * 1024 + (hh + 1) * 512],
                        op=ALU.add)
                return last
            FEV = VE.g(final_ev2)

            @dma_group(1)
            def st_out(e, g=g, frow=frow):
                return [e.dma_start(out=out_d[frow + g * 128: frow + (g + 1) * 128, :],
                                    in_=ODST[g][:])]
            if fr == 0 and g == 0:
                PREV_OUT = {}
            ow2 = [(s_dve, FEV)] + ([(s_out, s_out.v)] if s_out.v else [])
            PREV_OUT[g] = GPD.g(st_out, *ow2)
        if fr == 0:
            P2FE = {}
        P2FE[fr] = FEV
        PREV_HOM = dict(HOMS)
        PREV_OEV = dict(P2OEV)
        PREV_SUP3 = P2SUP[NCH - 1]
        PREV_RMS = RMS_DVE
        PREV_ONF7 = ONF[7]
        PREV_WTE = WTE
        PREV_P2OA3 = P2OA[NCH - 1]

    FINAL_OUT = s_out.v

    with nc.Block() as block:
        @block.sync
        def _(e):
            _emit(e, SY)
            e.wait_ge(s_out.h, FINAL_OUT)

        @block.tensor
        def _(e):
            _emit(e, TE)

        @block.vector
        def _(e):
            _emit(e, VE)

        @block.scalar
        def _(e):
            _emit(e, SE)

        @block.gpsimd
        def _(e):
            _emit(e, GPSTG)
            _emit(e, GP)
            _emit(e, GPD)

    return nc



def _consts(F):
    bf = ml_dtypes.bfloat16
    iden = np.eye(128, dtype=bf)
    r = np.arange(64)
    pat_up = (r[:, None] < r[None, :]).astype(np.float32)
    pat_lo = (r[:, None] > r[None, :]).astype(np.float32)
    pat_ui = (r[:, None] <= r[None, :]).astype(np.float32)
    mup = np.tile(pat_up, (2, 1)).astype(bf)
    mlo = np.tile(pat_lo, (2, 1)).astype(bf)
    mui = np.tile(pat_ui, (2, 1)).astype(bf)
    ep = np.zeros((16, 1024), np.float32)
    for h in range(16):
        p, hi = h // 2, h % 2
        ep[h, p * 128 + hi * 64: p * 128 + (hi + 1) * 64] = 1.0
    csel = np.zeros((128, 128), np.float32)
    for p in range(8):
        csel[0:64, p * 16 + 2 * p] = 1.0
        csel[64:128, p * 16 + 2 * p + 1] = 1.0
    dup = np.zeros((64, 128), np.float32)
    for t in range(64):
        dup[t, t] = 1.0
        dup[t, 64 + t] = 1.0
    reset = np.ones((16, F), np.float32)
    reset[:, 0::CK] = 0.0
    return dict(iden=iden, mup=mup, mlo=mlo, mui=mui, ep=ep.astype(bf),
                csel=csel.astype(bf), dup=dup.astype(bf), reset=reset)


def _in_map(xb, inputs, F):
    c = _consts(F)
    return dict(
        x=np.ascontiguousarray(xb),
        wq=np.asarray(inputs["L0_attn_Wq"], np.float32),
        wk=np.asarray(inputs["L0_attn_Wk"], np.float32),
        wv=np.asarray(inputs["L0_attn_Wv"], np.float32),
        wo=np.asarray(inputs["L0_attn_Wo"], np.float32),
        wbf=np.ascontiguousarray(np.concatenate(
            [np.asarray(inputs["L0_attn_Wbet"], np.float32),
             np.asarray(inputs["L0_attn_Wf"], np.float32)], axis=1)),
        normw=np.asarray(inputs["L0_attn_norm_w"], np.float32).reshape(1, D),
        fqt=np.ascontiguousarray(np.asarray(inputs["L0_ffn_weight_q"]).T),
        fscale=np.asarray(inputs["L0_ffn_weight_scale"], np.float32).reshape(1, D),
        fbias=np.asarray(inputs["L0_ffn_bias"], np.float32).reshape(1, D),
        **c,
    )


def kernel(L0_attn_x, L0_attn_Wq, L0_attn_Wk, L0_attn_Wv, L0_attn_Wo,
           L0_attn_Wbet, L0_attn_Wf, L0_attn_norm_w, L0_ffn_weight_q,
           L0_ffn_weight_scale, L0_ffn_bias):
    inputs = dict(L0_attn_Wq=L0_attn_Wq, L0_attn_Wk=L0_attn_Wk,
                  L0_attn_Wv=L0_attn_Wv, L0_attn_Wo=L0_attn_Wo,
                  L0_attn_Wbet=L0_attn_Wbet, L0_attn_Wf=L0_attn_Wf,
                  L0_attn_norm_w=L0_attn_norm_w, L0_ffn_weight_q=L0_ffn_weight_q,
                  L0_ffn_weight_scale=L0_ffn_weight_scale, L0_ffn_bias=L0_ffn_bias)
    x = np.asarray(L0_attn_x, np.float32)
    B, N, _ = x.shape
    T, F = N // 2, 256
    groups = [[b, b + B] for b in range(B)]
    nc = _build(T, F, groups)
    in_maps = []
    for c in range(2 * B):
        beta, j = c % B, c // B
        im = _in_map(x[beta, j * T:(j + 1) * T], inputs, F)
        im["shalf"] = np.full((128, 1), float(j), np.float32)
        in_maps.append(im)
    res = run_bass_kernel_spmd(nc, in_maps, core_ids=list(range(2 * B)))
    out = np.stack([np.concatenate([np.asarray(res.results[b]["out"]),
                                    np.asarray(res.results[b + B]["out"])], axis=0)
                    for b in range(B)], axis=0)
    return out.astype(np.float32)


# version 25
# speedup vs baseline: 1.0276x; 1.0276x over previous
import numpy as np
import ml_dtypes
import concourse.bass as bass
import concourse.mybir as mybir
from concourse.bass_utils import run_bass_kernel_spmd

F32 = mybir.dt.float32
BF16 = mybir.dt.bfloat16
I8 = mybir.dt.int8
AF = mybir.ActivationFunctionType
ALU = mybir.AluOpType
AX = mybir.AxisListType

D = 1024
H = 16
HD = 64
CK = 64


class Sem:
    def __init__(self, nc, name):
        self.h = nc.alloc_semaphore(name)
        self.v = 0


class Eng:

    def __init__(self, sem, dma=False):
        self.sem = sem
        self.dma = dma
        self.groups = []

    def g(self, fn, *waits, sem=None, dma=None):
        sem = sem or self.sem
        dmaf = self.dma if dma is None else dma
        if not dmaf and sem.v > 0:
            waits = waits + ((sem, sem.v),)
        self.groups.append((tuple(waits), fn, sem, dmaf))
        if dmaf:
            n = getattr(fn, "n_dma", 1)
            sem.v += 16 * n
        else:
            sem.v += 1
        return sem.v


def _emit(engine, eng_sched):
    for waits, fn, sem, dmaf in eng_sched.groups:
        for (wsem, val) in waits:
            if val > 0:
                engine.wait_ge(wsem.h, val)
        last = fn(engine)
        if dmaf:
            for instr in (last if isinstance(last, (list, tuple)) else [last]):
                instr.then_inc(sem.h, 16)
        else:
            last.then_inc(sem.h, 1)


def dma_group(n):
    def deco(fn):
        fn.n_dma = n
        return fn
    return deco


def mmt(e, out, lhsT, rhs, start, stop, tp):
    return e.matmul(out, lhsT, rhs, start=start, stop=stop, tile_position=tp,
                    skip_group_check=(tp[1] == 64))


def trt(e, out, in_, idn, tp):
    return e.matmul(out, in_, idn, is_transpose=True, tile_position=tp,
                    skip_group_check=True)


def _build(T, F, groups):
    assert T % F == 0 and F % 128 == 0 and 128 <= F <= 512
    NFR = T // F
    NG = F // 128
    NCH = F // CK
    FT = F

    nc = bass.Bass("TRN2", target_bir_lowering=False)

    x_d = nc.dram_tensor("x", [T, D], F32, kind="ExternalInput").ap()
    wq_d = nc.dram_tensor("wq", [D, D], F32, kind="ExternalInput").ap()
    wk_d = nc.dram_tensor("wk", [D, D], F32, kind="ExternalInput").ap()
    wv_d = nc.dram_tensor("wv", [D, D], F32, kind="ExternalInput").ap()
    wo_d = nc.dram_tensor("wo", [D, D], F32, kind="ExternalInput").ap()
    wbf_d = nc.dram_tensor("wbf", [D, 32], F32, kind="ExternalInput").ap()
    normw_d = nc.dram_tensor("normw", [1, D], F32, kind="ExternalInput").ap()
    fqt_d = nc.dram_tensor("fqt", [D, D], I8, kind="ExternalInput").ap()
    fscale_d = nc.dram_tensor("fscale", [1, D], F32, kind="ExternalInput").ap()
    fbias_d = nc.dram_tensor("fbias", [1, D], F32, kind="ExternalInput").ap()
    iden_d = nc.dram_tensor("iden", [128, 128], BF16, kind="ExternalInput").ap()
    mup_d = nc.dram_tensor("mup", [128, 64], BF16, kind="ExternalInput").ap()
    mlo_d = nc.dram_tensor("mlo", [128, 64], BF16, kind="ExternalInput").ap()
    mui_d = nc.dram_tensor("mui", [128, 64], BF16, kind="ExternalInput").ap()
    ep_d = nc.dram_tensor("ep", [16, 1024], BF16, kind="ExternalInput").ap()
    csel_d = nc.dram_tensor("csel", [128, 128], BF16, kind="ExternalInput").ap()
    dup_d = nc.dram_tensor("dup", [64, 128], BF16, kind="ExternalInput").ap()
    reset_d = nc.dram_tensor("reset", [16, F], F32, kind="ExternalInput").ap()
    shalf_d = nc.dram_tensor("shalf", [128, 1], F32, kind="ExternalInput").ap()
    out_d = nc.dram_tensor("out", [T, D], F32, kind="ExternalOutput").ap()
    oz_d = nc.dram_tensor("oz_i", [T // CK, 128, 512], BF16).ap()
    mh_d = nc.dram_tensor("mh_i", [T // CK, 128, 512], BF16).ap()
    qh_d = nc.dram_tensor("qh_i", [T // CK, 128, 512], BF16).ap()
    gz_d = nc.dram_tensor("gz_i", [T, D], BF16).ap()
    eld_d = nc.dram_tensor("eld_i", [T // F, 128, 64], BF16).ap()
    scin_d = nc.dram_tensor("scin_i", [128, 512], BF16).ap()
    scout_d = nc.dram_tensor("scout_i", [256, 512], BF16).ap()

    def sb(name, shape, dtype):
        return nc.alloc_sbuf_tensor(name, shape, dtype).ap()

    wq = sb("wq_s", [128, 8192], BF16)
    wk = sb("wk_s", [128, 8192], BF16)
    wv = sb("wv_s", [128, 8192], BF16)
    wo = sb("wo_s", [128, 8192], BF16)
    wfn = sb("wfn_s", [128, 8192], BF16)
    wbf = sb("wbf_s", [128, 256], BF16)
    wstage8 = None
    iden = sb("iden_s", [128, 128], BF16)
    mup_t = sb("mup_s", [128, 64], BF16)
    mlo_t = sb("mlo_s", [128, 64], BF16)
    mui_t = sb("mui_s", [128, 64], BF16)
    mup = mup_t.unsqueeze(1).broadcast_to((128, 8, 64)).rearrange("p a b -> p (a b)") if False else None
    mup = mup_t.unsqueeze(1).broadcast_to((128, 8, 64))
    mlo = mlo_t.unsqueeze(1).broadcast_to((128, 8, 64))
    mui = mui_t.unsqueeze(1).broadcast_to((128, 8, 64))
    ep = sb("ep_s", [16, 1024], BF16)
    csel = sb("csel_s", [128, 128], BF16)
    dupc = sb("dup_s", [64, 128], BF16)
    resetm = sb("reset_s", [16, F], F32)
    normw_f = [sb(f"normw_f{p}", [128, 1], F32) for p in range(8)]
    fbias_b = sb("fbias_b", [128, D], BF16)
    fws_b = sb("fws_b", [128, D], BF16)

    x_t = sb("x_t", [128, NG * 1024], F32)
    wstage = x_t[:, 0:2048]
    xn_t = sb("xn_t", [128, NG * 1024], BF16)
    xn_f = sb("xn_f", [128, 8 * FT], BF16)
    xq_t = sb("xq_t", [128, NG * 1024], BF16)
    xq_f = sb("xq_f", [128, 8 * FT], BF16)
    q_f = sb("q_f", [128, 8 * FT], BF16)
    k_f = sb("k_f", [128, 8 * FT], BF16)
    kn_f = sb("kn_f", [128, 8 * FT], BF16)
    knb_f = sb("knb_f", [128, 8 * FT], BF16)
    v_f = sb("v_f", [128, 8 * FT], BF16)
    o_f = sb("o_f", [128, 8 * FT], BF16)
    ksq = sb("ksq", [128, FT], BF16)
    field_s = sb("field_s", [128, FT], BF16)
    g_t = sb("g_t", [128, NG * 1024], BF16)
    s_scr = sb("s_scr", [128, 1024], F32)
    out_t = sb("out_t", [128, 1024], F32)
    r2_scr = out_t
    sq_scr = sb("sq_scr", [128, 1024], F32)
    zbf = sq_scr[0:16, 0:2 * F]
    beta_hb = sb("beta_hb", [16, 2 * F], BF16)
    la_h = sb("la_h", [16, F], F32)
    ela_h = sb("ela_h", [16, F], BF16)
    elam_hb = sb("elam_hb", [16, NCH], BF16)
    bem_h = sb("bem_h", [16, 2 * F], BF16)
    rn_h = sb("rn_h", [16, F], BF16)
    rn_f32 = sb("rn_f32", [16, F], F32)
    rms_h = rn_f32
    zero16 = sb("zero16", [16, 1], F32)
    ones1 = sb("ones1", [128, 1], F32)
    flag1 = sb("flag1", [128, 1], F32)
    lam_all = sb("lam_all", [128, 2 * 8 * NCH], F32)
    ela_dup = sb("ela_dup", [128, 2 * 16 * NCH], BF16)

    ln_m = [sb(f"ln_m{g}", [128, 1], F32) for g in range(NG)]
    ln_r = [sb(f"ln_r{g}", [128, 1], F32) for g in range(NG)]
    ln_s2 = [sb(f"ln_s2{g}", [128, 1], F32) for g in range(NG)]
    qs_s = [sb(f"qs_s{g}", [128, 1], F32) for g in range(NG)]
    qsr_s = [sb(f"qsr_s{g}", [128, 1], F32) for g in range(NG)]
    state = sb("state", [128, 512], F32)
    state_b = sb("state_b", [128, 512], BF16)
    t0_d = [sb(f"t0_{s}", [128, 512], BF16) for s in range(2)]
    dt_d = [sb(f"dt_{s}", [128, 512], BF16) for s in range(3)]
    p0_s = sb("p0_s", [128, 512], BF16)
    tl_d = [[sb(f"tl{k}_{s}", [128, 512], BF16) for k in range(5)] for s in range(2)]
    pl_s = [sb(f"pl{k}", [128, 512], BF16) for k in range(4)]
    r_all = sb("r_all", [128, NCH * 1024], BF16)
    knt_all = sb("knt_all", [128, NCH * 1024], BF16)
    xch_s = sb("xch_s", [128, 1024], BF16)
    vt_d = [sb(f"vt_{s}", [128, 512], BF16) for s in range(2)]
    wt_d = [sb(f"wt_{s}", [128, 512], BF16) for s in range(2)]
    wfm_d = [sb(f"wfm_{s}", [128, 512], BF16) for s in range(2)]
    osc_s = sb("osc_s", [128, 512], BF16)
    shalf = sb("shalf_s", [128, 1], F32)
    rms_hb = beta_hb[:, 0:F]
    scr_h = sq_scr.bitcast(BF16)[0:64, 1024:1024 + 16 * NCH]
    wstage8 = r_all.bitcast(I8)[:, 0:8192]

    pb = [nc.alloc_psum_tensor(f"pb{i}", [128, 512], F32).ap() for i in range(6)]
    pt = nc.alloc_psum_tensor("pt6", [128, 1024], BF16).ap()
    pt2 = nc.alloc_psum_tensor("pt7", [128, 1024], BF16).ap()

    s_in = Sem(nc, "s_in")
    s_c = Sem(nc, "s_c")
    s_w = Sem(nc, "s_w")
    s_w2 = Sem(nc, "s_w2")
    s_wb = Sem(nc, "s_wb")
    s_wb2 = Sem(nc, "s_wb2")
    s_te = Sem(nc, "s_te")
    s_dve = Sem(nc, "s_dve")
    s_act = Sem(nc, "s_act")
    s_gp = Sem(nc, "s_gp")
    s_cc = Sem(nc, "s_cc")
    s_st = Sem(nc, "s_st")
    s_sp = Sem(nc, "s_sp")
    s_spo = Sem(nc, "s_spo")
    s_spq = Sem(nc, "s_spq")
    s_out = Sem(nc, "s_out")

    SY = Eng(s_in, dma=True)
    SW = Eng(s_w, dma=True)
    TE = Eng(s_te)
    VE = Eng(s_dve)
    SE = Eng(s_act)
    GP = Eng(s_cc)
    GPD = Eng(s_out, dma=True)
    GPSTG = Eng(s_wb, dma=True)


    @dma_group(19)
    def ld_consts(e):
        a = []
        a.append(e.dma_start(out=iden, in_=iden_d))
        a.append(e.dma_start(out=mup_t, in_=mup_d))
        a.append(e.dma_start(out=mlo_t, in_=mlo_d))
        a.append(e.dma_start(out=mui_t, in_=mui_d))
        a.append(e.dma_start(out=ep, in_=ep_d))
        a.append(e.dma_start(out=csel, in_=csel_d))
        a.append(e.dma_start(out=dupc, in_=dup_d))
        a.append(e.dma_start(out=resetm, in_=reset_d))
        a.append(e.dma_start(out=shalf, in_=shalf_d))
        for p8 in range(8):
            a.append(e.dma_start(out=normw_f[p8],
                                 in_=normw_d[:, p8 * 128:(p8 + 1) * 128].rearrange("o b -> b o")))
        a.append(e.dma_start(out=sq_scr[:, 0:1024], in_=fbias_d.squeeze(0).partition_broadcast(128)))
        a.append(e.dma_start(out=s_scr[:, 0:1024], in_=fscale_d.squeeze(0).partition_broadcast(128)))
        return a
    CONSTS_IN = GPSTG.g(ld_consts, sem=s_c)

    def init_zero(e):
        e.tensor_copy(fbias_b[:], sq_scr[:, 0:1024])
        e.tensor_copy(fws_b[:], s_scr[:, 0:1024])
        e.memset(ones1[:], 1.0)
        e.memset(flag1[:], 2.0 ** -30)
        e.memset(zero16[:], 0.0)
        e.memset(state_b[:], 0.0)
        return e.memset(state[:], 0.0)
    IZ = VE.g(init_zero, (s_c, CONSTS_IN))
    VE.g(lambda e: e.tensor_tensor(flag1[:], ones1[:], flag1[:], op=ALU.add))
    W_DVE = VE.g(lambda e: e.tensor_scalar(flag1[:], flag1[:], 1.0, 2.0 ** 30,
                                           op0=ALU.subtract, op1=ALU.mult))

    xq32 = xq_t.bitcast(F32)

    @dma_group(1)
    def ld_wbf(e):
        return [e.dma_start(out=xq32[:, 0:256].rearrange("p (a b) -> p a b", b=32),
                            in_=wbf_d.rearrange("(a p) b -> p a b", p=128))]
    WBF_IN = SY.g(ld_wbf, sem=s_w)
    CAST_WBF = VE.g(lambda e: e.tensor_copy(wbf[:], xq32[:, 0:256]), (s_w, WBF_IN))

    r32 = r_all.bitcast(F32)
    kn32 = knt_all.bitcast(F32)
    sy_pieces = [(wk_d, wk, kc0) for kc0 in (0, 2, 4, 6)] + \
                [(wq_d, wq, kc0) for kc0 in (0, 2, 4, 6)]
    sy_cast = {}
    for i, (wd, wsb, kc0) in enumerate(sy_pieces):
        slot = x_t[:, 0:2048] if i % 2 == 0 else r32[:, 0:2048]
        sem = s_w if i % 2 == 0 else s_w2

        @dma_group(2)
        def ld(e, wd=wd, kc0=kc0, slot=slot):
            return [e.dma_start(out=slot[:, j * 1024:(j + 1) * 1024],
                                in_=wd[(kc0 + j) * 128:(kc0 + j + 1) * 128, :])
                    for j in range(2)]
        w = [(s_dve, CAST_WBF)] if i < 2 else [(s_dve, sy_cast[i - 2])]
        ld_v = SY.g(ld, *w, sem=sem)

        def cast(e, wsb=wsb, kc0=kc0, slot=slot):
            return e.tensor_copy(wsb[:, kc0 * 1024:(kc0 + 2) * 1024], slot[:])
        sy_cast[i] = VE.g(cast, (sem, ld_v))
    SY_XT_CAST = sy_cast[len(sy_pieces) - 2]
    SY_LAST_CAST = sy_cast[len(sy_pieces) - 1]

    gp_pieces = [(wv_d, wv, kc0) for kc0 in (0, 2, 4, 6)] + \
                [(wo_d, wo, kc0) for kc0 in (0, 2, 4, 6)]
    gp_cast = {}
    for i, (wd, wsb, kc0) in enumerate(gp_pieces):
        sem = s_wb if i % 2 == 0 else s_wb2

        @dma_group(2)
        def ldg(e, wd=wd, kc0=kc0, i=i):
            if i % 2 == 0:
                return [e.dma_start(out=out_t[:, 0:1024], in_=wd[kc0 * 128:(kc0 + 1) * 128, :]),
                        e.dma_start(out=s_scr[:, 0:1024], in_=wd[(kc0 + 1) * 128:(kc0 + 2) * 128, :])]
            return [e.dma_start(out=kn32[:, j * 1024:(j + 1) * 1024],
                                in_=wd[(kc0 + j) * 128:(kc0 + j + 1) * 128, :])
                    for j in range(2)]
        w = [(s_dve, IZ)] if i < 2 else [(s_dve, gp_cast[i - 2])]
        ld_v = GPSTG.g(ldg, *w, sem=sem)

        def castg(e, wsb=wsb, kc0=kc0, i=i):
            if i % 2 == 0:
                e.tensor_copy(wsb[:, kc0 * 1024:(kc0 + 1) * 1024], out_t[:, 0:1024])
                return e.tensor_copy(wsb[:, (kc0 + 1) * 1024:(kc0 + 2) * 1024], s_scr[:, 0:1024])
            return e.tensor_copy(wsb[:, kc0 * 1024:(kc0 + 2) * 1024], kn32[:])
        gp_cast[i] = VE.g(castg, (sem, ld_v))
    GP_LAST_CAST = gp_cast[len(gp_pieces) - 1]

    LDX = {}

    @dma_group(NG)
    def ld_x0(e):
        return [e.dma_start(out=x_t[:, g * 1024:(g + 1) * 1024],
                            in_=x_d[g * 128:(g + 1) * 128, :]) for g in range(NG)]
    LDX[0] = SY.g(ld_x0, (s_dve, SY_XT_CAST))

    @dma_group(8)
    def ld_fqt(e):
        return [e.dma_start(out=wstage8[:, kc * 1024:(kc + 1) * 1024],
                            in_=fqt_d[kc * 128:(kc + 1) * 128, :]) for kc in range(8)]
    FW_IN = SY.g(ld_fqt, (s_dve, SY_LAST_CAST), sem=s_w)
    CAST_FFN = VE.g(lambda e: e.tensor_copy(wfn[:], wstage8[:]), (s_w, FW_IN))

    def sched_ldx(fr, *waits):
        frow = fr * F

        @dma_group(NG)
        def ld_x(e, frow=frow):
            return [e.dma_start(out=x_t[:, g * 1024:(g + 1) * 1024],
                                in_=x_d[frow + g * 128: frow + (g + 1) * 128, :])
                    for g in range(NG)]
        LDX[fr] = SY.g(ld_x, *waits)

    def build_front(fr, PV):
        par = fr % 2
        X_IN = LDX[fr]
        beta_v = beta_hb[:, par * F:(par + 1) * F]
        la_v = la_h[:]
        ela_v = ela_h[:]
        elam_v = elam_hb[:]
        bem_v = bem_h[:, par * F:(par + 1) * F]
        lam_v = lam_all[:, par * 8 * NCH:(par + 1) * 8 * NCH]
        eld_v = ela_dup[:, par * 16 * NCH:(par + 1) * 16 * NCH]
        FD = {}
        th = []

        def ln_sum(e):
            for g in range(NG):
                last = e.tensor_reduce(ln_m[g][:], x_t[:, g * 1024:(g + 1) * 1024],
                                       axis=AX.X, op=ALU.add)
            return last
        th.append(lambda: VE.g(ln_sum, (s_in, X_IN)))

        def ln_sq(e):
            for g in range(NG):
                scr = sq_scr[:] if g % 2 == 0 else xn_t[:, 0:1024]
                last = e.activation(scr, x_t[:, g * 1024:(g + 1) * 1024],
                                    AF.Square, accum_out=ln_s2[g][:])
            return last

        def _lnsq():
            lw = [(s_in, X_IN), (s_dve, W_DVE)]
            if "XQ" in PV:
                lw.append((s_dve, PV["XQ"]))
            FD["LNSQ"] = SE.g(ln_sq, *lw)
        th.append(_lnsq)

        def ln_stats1(e):
            for g in range(NG):
                e.tensor_scalar(ln_m[g][:], ln_m[g][:], 1.0 / D, None, op0=ALU.mult)
                last = e.tensor_scalar(ln_s2[g][:], ln_s2[g][:], 1.0 / D, None,
                                       op0=ALU.mult)
            return last
        th.append(lambda: VE.g(ln_stats1, (s_act, FD["LNSQ"])))

        def ln_stats2(e):
            for g in range(NG):
                last = e.tensor_tensor(ln_r[g][:], ln_m[g][:], ln_m[g][:], op=ALU.mult)
            return last
        th.append(lambda: VE.g(ln_stats2))

        def ln_stats3(e):
            for g in range(NG):
                last = e.tensor_tensor(ln_r[g][:], ln_s2[g][:], ln_r[g][:],
                                       op=ALU.subtract)
            return last
        th.append(lambda: VE.g(ln_stats3))

        def ln_stats4(e):
            for g in range(NG):
                last = e.tensor_scalar(ln_r[g][:], ln_r[g][:], 1e-5, None, op0=ALU.add)
            return last
        th.append(lambda: FD.__setitem__("LNVAR", VE.g(ln_stats4)))

        def ln_sqrt(e):
            for g in range(NG):
                last = e.activation(ln_r[g][:], ln_r[g][:], AF.Sqrt)
            return last
        th.append(lambda: FD.__setitem__("LNSTD", SE.g(ln_sqrt, (s_dve, FD["LNVAR"]))))

        def ln_recip(e):
            for g in range(NG):
                last = e.reciprocal(ln_r[g][:], ln_r[g][:])
            return last
        th.append(lambda: VE.g(ln_recip, (s_act, FD["LNSTD"])))

        def ln_apply(e):
            for g in range(NG):
                last = e.scalar_tensor_tensor(
                    xn_t[:, g * 1024:(g + 1) * 1024], x_t[:, g * 1024:(g + 1) * 1024],
                    ln_m[g][:], ln_r[g][:].broadcast_to((128, 1024)),
                    op0=ALU.subtract, op1=ALU.mult)
            return last

        def _lnapply():
            aw = []
            if "XQ" in PV:
                aw.append((s_dve, PV["XQ"]))
            FD["XN"] = VE.g(ln_apply, *aw)
            if fr + 1 < NFR:
                sched_ldx(fr + 1, (s_dve, FD["XN"]), (s_act, FD["LNSQ"]))
        th.append(_lnapply)

        for g in range(NG):
            def tr(e, g=g):
                for dc in range(8):
                    col = dc * 128
                    last = e.transpose(pt2[:, col:col + 128],
                                       xn_t[:, g * 1024 + dc * 128:g * 1024 + (dc + 1) * 128],
                                       iden[:])
                return last

            def _tr(g=g, tr=tr):
                waits = [(s_dve, FD["XN"]), (s_c, CONSTS_IN)]
                if g > 0:
                    waits.append((s_act, FD["TREV0"]))
                elif "TEV3" in PV:
                    waits.append((s_act, PV["TEV3"]()))
                else:
                    waits.append((s_act, s_act.v))
                FD["TRTE"] = TE.g(tr, *waits)
            th.append(_tr)

            def trev(e, g=g):
                for dc in range(8):
                    col = dc * 128
                    last = e.copy(xn_f[:, dc * FT + g * 128: dc * FT + (g + 1) * 128],
                                  pt2[:, col:col + 128])
                return last

            def _trev(g=g, trev=trev):
                tw = [(s_te, FD["TRTE"])]
                if g == 0 and "VTE7" in PV:
                    tw.append((s_te, PV["VTE7"]))
                FD["TREV0" if g == 0 else "XF"] = SE.g(trev, *tw)
            th.append(_trev)

        def zbf_mm(e):
            for kc in range(8):
                e.matmul(pb[4][0:16, 0:FT], wbf[:, kc * 32:kc * 32 + 16],
                         xn_f[:, kc * FT:(kc + 1) * FT],
                         start=(kc == 0), stop=False)
                last = e.matmul(pb[4][0:16, FT:2 * FT], wbf[:, kc * 32 + 16:(kc + 1) * 32],
                                xn_f[:, kc * FT:(kc + 1) * FT],
                                start=False, stop=(kc == 7))
            return last

        def _zbf():
            zw = [(s_act, FD["XF"])]
            if "GR3" in PV:
                zw.append((s_dve, PV["GR3"]()))
            else:
                zw.append((s_dve, CAST_WBF))
            FD["ZBFTE"] = TE.g(zbf_mm, *zw)
        th.append(_zbf)

        def zbf_ev(e):
            return e.tensor_copy(zbf[:], pb[4][0:16, 0:2 * FT])
        th.append(lambda: FD.__setitem__("ZBFD", VE.g(zbf_ev, (s_te, FD["ZBFTE"]))))

        def sig_ops(e):
            e.activation(beta_v, zbf[:, 0:F], AF.Sigmoid)
            return e.activation(la_v, zbf[:, F:2 * F], AF.Sigmoid)
        th.append(lambda: SE.g(sig_ops, (s_dve, FD["ZBFD"])))
        th.append(lambda: FD.__setitem__("SIG", SE.g(
            lambda e: e.activation(la_v, la_v, AF.Ln))))

        def la_ops(e):
            e.tensor_scalar(beta_v, beta_v, 2.0, None, op0=ALU.mult)
            return e.tensor_tensor_scan(la_v, resetm[:], la_v, zero16[:],
                                        op0=ALU.mult, op1=ALU.add)
        th.append(lambda: FD.__setitem__("BETAD", VE.g(la_ops, (s_act, FD["SIG"]))))
        th.append(lambda: FD.__setitem__("LAD", VE.g(
            lambda e: e.tensor_scalar(la_v, la_v, -60.0, None, op0=ALU.max))))

        def exp_ops(e):
            e.activation(ela_v, la_v, AF.Exp)
            e.activation(elam_v, la_v[:, CK - 1::CK], AF.Exp)
            return e.activation(bem_v, la_v, AF.Exp, scale=-1.0)
        th.append(lambda: FD.__setitem__("EXPA", SE.g(exp_ops, (s_dve, FD["LAD"]))))

        def bem_mul(e):
            return e.tensor_tensor(bem_v, beta_v, bem_v, op=ALU.mult)
        th.append(lambda: FD.__setitem__("BEMD", VE.g(bem_mul, (s_act, FD["EXPA"]))))

        def lam_mm(e):
            for p in range(8):
                last = e.matmul(pb[4][:, p * NCH:(p + 1) * NCH],
                                ep[:, p * 128:(p + 1) * 128], elam_v,
                                start=(p == 0), stop=(p == 7))
            return last
        th.append(lambda: FD.__setitem__("LAMTE", TE.g(
            lam_mm, (s_act, FD["EXPA"]), (s_dve, FD["ZBFD"]))))

        def lam_ev(e):
            return e.tensor_copy(lam_v, pb[4][:, 0:8 * NCH])
        th.append(lambda: FD.__setitem__("LAMD", VE.g(lam_ev, (s_te, FD["LAMTE"]))))

        def elat_mm(e):
            for c in range(NCH):
                last = e.transpose(pt2[0:64, c * 16:(c + 1) * 16],
                                   ela_v[:, c * CK:(c + 1) * CK], iden[0:16, 0:16])
            return last
        th.append(lambda: FD.__setitem__("ELATTE", TE.g(
            elat_mm, (s_act, FD["EXPA"]), (s_dve, FD["LAMD"]), (s_act, FD["XF"]))))

        def elat_ev(e):
            return e.copy(scr_h[:, 0:16 * NCH],
                          pt2[0:64, 0:16 * NCH])
        th.append(lambda: FD.__setitem__("ELATA", SE.g(elat_ev, (s_te, FD["ELATTE"]))))

        def dup_mm(e):
            for c in range(NCH):
                last = e.matmul(pb[4][:, c * 16:(c + 1) * 16], dupc[:],
                                scr_h[:, c * 16:(c + 1) * 16],
                                start=(c == 0), stop=(c == NCH - 1))
            return last
        th.append(lambda: FD.__setitem__("DUPT", TE.g(
            dup_mm, (s_act, FD["ELATA"]), (s_dve, FD["LAMD"]))))

        def dup_ev(e):
            return e.tensor_copy(eld_v, pb[4][:, 0:16 * NCH])

        def _dupev():
            dw = [(s_te, FD["DUPT"])]
            if "SPG" in PV:
                dw.append((s_sp, PV["SPG"]))
            FD["DUPD"] = VE.g(dup_ev, *dw)
        th.append(_dupev)
        return th, FD

    def _run_front0():
        th, FD = build_front(0, {})
        for t in th:
            t()
        return FD
    FRONTS = {0: _run_front0()}
    PREV_SPG = None
    for fr in range(NFR):
        frow = fr * F
        par = fr % 2
        FR_ = FRONTS[fr]
        XN_DVE, XF_ACT, LN_SQ, SIG_ACT = FR_["XN"], FR_["XF"], FR_["LNSQ"], FR_["SIG"]
        BEM_DVE, EXP_ACT = FR_["BEMD"], FR_["EXPA"]
        LAM_DVE, ELAT_ACT, DUP_TE, DUP_DVE = (FR_["LAMD"], FR_["ELATA"],
                                              FR_["DUPT"], FR_["DUPD"])
        if fr + 1 < NFR:
            pv = dict(TEV3=lambda: TEV[NCH - 1], GR3=lambda: GRDVE[NCH - 1])
            if PREV_SPG is not None:
                pv["SPG"] = PREV_SPG
            NEXT_TH, FRONTS[fr + 1] = build_front(fr + 1, pv)
        else:
            NEXT_TH, pv = [], None
        beta_vf = beta_hb[:, par * F:(par + 1) * F]
        bem_vf = bem_h[:, par * F:(par + 1) * F]
        lam_allv = lam_all[:, par * 8 * NCH:(par + 1) * 8 * NCH]
        ela_dupv = ela_dup[:, par * 16 * NCH:(par + 1) * 16 * NCH]

        def q_amax(e):
            for g in range(NG):
                last = e.tensor_reduce(qs_s[g][:], xn_t[:, g * 1024:(g + 1) * 1024],
                                       axis=AX.X, op=ALU.max, apply_absolute_value=True)
            return last
        VE.g(q_amax)

        def q_scale(e):
            for g in range(NG):
                last = e.tensor_scalar(qs_s[g][:], qs_s[g][:], 1e-6, 1.0 / 127.0,
                                       op0=ALU.max, op1=ALU.mult)
            return last
        VE.g(q_scale)

        def q_recip(e):
            for g in range(NG):
                last = e.reciprocal(qsr_s[g][:], qs_s[g][:])
            return last
        QS_DVE = VE.g(q_recip)

        XQ_DVE = QS_DVE
        M32 = 12582912.0
        M52 = 6755399441055744.0
        for g in range(NG):
            sl = slice(g * 1024, (g + 1) * 1024)
            qw = [(s_act, SIG_ACT), (s_te, DUP_TE)]
            if fr > 0:
                qw.append((s_act, PREV_GEV))
            else:
                qw.append((s_dve, GP_LAST_CAST))
            VE.g(lambda e, sl=sl, g=g: e.tensor_scalar(
                sq_scr[:], xn_t[:, sl], qsr_s[g][:], None,
                op0=ALU.mult), *qw)
            VE.g(lambda e: e.tensor_scalar(s_scr[:], sq_scr[:], M32, None, op0=ALU.add))
            VE.g(lambda e: e.tensor_scalar(s_scr[:], s_scr[:], M32, None,
                                           op0=ALU.subtract))
            VE.g(lambda e: e.tensor_scalar(r2_scr[:], sq_scr[:], M52, None, op0=ALU.add))
            VE.g(lambda e: e.tensor_scalar(r2_scr[:], r2_scr[:], M52, None,
                                           op0=ALU.subtract))
            VE.g(lambda e: e.tensor_tensor(r2_scr[:], r2_scr[:], s_scr[:],
                                           op=ALU.subtract))
            XQ_DVE = VE.g(lambda e, sl=sl: e.scalar_tensor_tensor(
                xq_t[:, sl], r2_scr[:], flag1[:], s_scr[:],
                op0=ALU.mult, op1=ALU.add))
        if fr + 1 < NFR:
            pv["XQ"] = XQ_DVE


        QEV = {}
        for p in range(8):
            def qmm(e, p=p):
                for kc in range(8):
                    last = e.matmul(pb[p % 2][:, 0:FT],
                                    wq[:, kc * 1024 + p * 128:kc * 1024 + (p + 1) * 128],
                                    xn_f[:, kc * FT:(kc + 1) * FT],
                                    start=(kc == 0), stop=(kc == 7))
                return last
            waits = [(s_act, XF_ACT)]
            if p >= 2:
                waits.append((s_act, QEV[p - 2]))
            elif fr > 0:
                waits.append((s_dve, PREV_FEV1))
            else:
                waits.append((s_dve, SY_LAST_CAST))
            qte = TE.g(qmm, *waits)

            def qev(e, p=p):
                return e.activation(q_f[:, p * FT:(p + 1) * FT], pb[p % 2][:, 0:FT], AF.Silu)
            QEV[p] = SE.g(qev, (s_te, qte))

        KEV = {}
        for p in range(8):
            def kmm(e, p=p):
                for kc in range(8):
                    last = e.matmul(pb[p % 2][:, 0:FT],
                                    wk[:, kc * 1024 + p * 128:kc * 1024 + (p + 1) * 128],
                                    xn_f[:, kc * FT:(kc + 1) * FT],
                                    start=(kc == 0), stop=(kc == 7))
                return last
            waits = [(s_act, QEV[7] if p < 2 else KEV[p - 2])]
            kte = TE.g(kmm, *waits)

            def kev(e, p=p):
                return e.activation(k_f[:, p * FT:(p + 1) * FT], pb[p % 2][:, 0:FT], AF.Silu)
            KEV[p] = SE.g(kev, (s_te, kte))

        XQTR = []
        for g in range(NG):
            def trq(e, g=g):
                for dc in range(8):
                    col = dc * 128
                    last = e.transpose(pt2[:, col:col + 128],
                                       xq_t[:, g * 1024 + dc * 128:g * 1024 + (dc + 1) * 128],
                                       iden[:])
                return last
            waits = [(s_dve, XQ_DVE)]
            waits.append((s_act, XQTR[-1]) if XQTR else (s_act, s_act.v))
            trq_te = TE.g(trq, *waits)

            def trqev(e, g=g):
                for dc in range(8):
                    col = dc * 128
                    last = e.copy(xq_f[:, dc * FT + g * 128: dc * FT + (g + 1) * 128],
                                  pt2[:, col:col + 128])
                return last
            XQTR.append(SE.g(trqev, (s_te, trq_te)))
        XQF_ACT = XQTR[-1]

        RSQ = {}
        RTE = {}
        for p in range(8):
            def rsq(e, p=p):
                return e.activation(ksq[:, 0:FT], k_f[:, p * FT:(p + 1) * FT], AF.Square)
            waits = [(s_act, KEV[7])] if p == 0 else [(s_te, RTE[p - 1])]
            RSQ[p] = SE.g(rsq, *waits)

            def rmm(e, p=p):
                return e.matmul(pb[5][0:16, 0:FT], csel[:, p * 16:(p + 1) * 16],
                                ksq[:, 0:FT], start=(p == 0), stop=(p == 7))
            rw = [(s_act, RSQ[p])]
            if p == 0 and fr > 0:
                rw.append((s_dve, PREV_QHEV3))
            RTE[p] = TE.g(rmm, *rw)

        VE.g(lambda e: e.tensor_scalar(rn_f32[:], pb[5][0:16, 0:FT], 1e-12, None,
                                       op0=ALU.add), (s_te, RTE[7]))
        RN_DVE = VE.g(lambda e: e.reciprocal(rn_f32[:], rn_f32[:]))

        def rn_sqrt(e):
            return e.activation(rn_h[:], rn_f32[:], AF.Sqrt)
        RN_ACT = SE.g(rn_sqrt, (s_dve, RN_DVE))

        VEV = {}
        for p in range(8):
            def vmm(e, p=p, bem_vf=bem_vf):
                for kc in range(8):
                    e.matmul(pb[2 + p % 2][:, 0:FT],
                             wv[:, kc * 1024 + p * 128:kc * 1024 + (p + 1) * 128],
                             xn_f[:, kc * FT:(kc + 1) * FT],
                             start=(kc == 0), stop=(kc == 7))
                return e.matmul(pb[4][:, 0:FT], ep[:, p * 128:(p + 1) * 128],
                                bem_vf, start=True, stop=True)
            waits = [(s_dve, BEM_DVE), (s_act, XF_ACT), (s_dve, DUP_DVE)]
            if p >= 1:
                waits.append((s_dve, VEV[p - 1]))
                waits.append((s_act, FCP[p - 1]))
            elif fr == 0:
                waits.append((s_dve, GP_LAST_CAST))
            vte = TE.g(vmm, *waits)

            if p == 0:
                FCP = {}
            FCP[p] = fcp = SE.g(lambda e: e.copy(field_s[:, 0:FT], pb[4][:, 0:FT]),
                                (s_te, vte))

            def vev(e, p=p):
                return e.tensor_tensor(v_f[:, p * FT:(p + 1) * FT], pb[2 + p % 2][:, 0:FT],
                                       field_s[:, 0:FT], op=ALU.mult)
            VEV[p] = VE.g(vev, (s_te, vte), (s_act, fcp))
        if fr + 1 < NFR:
            pv["VTE7"] = vte

        KNB = {}
        for p in range(8):
            def fmm(e, p=p, beta_vf=beta_vf):
                e.matmul(pb[4][:, 0:FT], ep[:, p * 128:(p + 1) * 128],
                         rn_h[:], start=True, stop=True)
                return e.matmul(pb[5][:, 0:FT], ep[:, p * 128:(p + 1) * 128],
                                beta_vf, start=True, stop=True)
            waits = [(s_act, RN_ACT), (s_dve, VEV[7]), (s_dve, FR_["BETAD"])]
            if p > 0:
                waits.append((s_dve, KNB[p - 1]))
            fte = TE.g(fmm, *waits)

            VE.g(lambda e, p=p: e.tensor_tensor(
                kn_f[:, p * FT:(p + 1) * FT], k_f[:, p * FT:(p + 1) * FT],
                pb[4][:, 0:FT], op=ALU.mult), (s_te, fte))
            KNB[p] = VE.g(lambda e, p=p: e.tensor_tensor(
                knb_f[:, p * FT:(p + 1) * FT], kn_f[:, p * FT:(p + 1) * FT],
                pb[5][:, 0:FT], op=ALU.mult))
        FIELDS = KNB[7]

        GRDVE, SQEV, SQ_FIN, SQ_TE_LAST = {}, {}, {}, {}
        CH_TE, CHEV, CH_FIN = {}, {}, {}
        SEQ1, SEQ2, DS_TE, SUPD, OEVD, WT_ACT, TEV = {}, {}, {}, {}, {}, {}, {}
        QH_TE, QHEV, SPILL_AT, OSPILL_AT = {}, {}, {}, {}

        def ksl_f(t_, h, ci):
            p, r0 = h // 2, 64 * (h % 2)
            return t_[r0:r0 + 64, p * FT + ci * CK: p * FT + ci * CK + CK]

        def build_T(ci):
            rungs = []

            def t_kn(e, ci=ci):
                for h in range(H):
                    r0 = 64 * (h % 2)
                    last = trt(e, pt2[r0:r0 + 64, h * 64:(h + 1) * 64],
                               ksl_f(kn_f, h, ci), iden[r0:r0 + 64, r0:r0 + 64], (r0, r0))
                return last
            w = [(s_dve, FIELDS), (s_te, DUP_TE), (s_act, ELAT_ACT)]
            w.append((s_act, TEV[ci - 1]) if ci > 0 else (s_act, XQF_ACT))
            if fr == 0:
                w.append((s_dve, CAST_FFN))
            rungs.append(("tensor", t_kn, w, None))

            def t_knev(e, ci=ci):
                return e.copy(knt_all[:, ci * 1024:(ci + 1) * 1024], pt2[:, 0:1024])
            rungs.append(("scalar", t_knev, "AFTER_PREV_TE", None))

            def t_knb(e, ci=ci):
                for h in range(H):
                    r0 = 64 * (h % 2)
                    last = trt(e, pt2[r0:r0 + 64, h * 64:(h + 1) * 64],
                               ksl_f(knb_f, h, ci), iden[r0:r0 + 64, r0:r0 + 64], (r0, r0))
                return last
            rungs.append(("tensor", t_knb, "AFTER_PREV_ACT", None))

            def t_knbev(e, ci=ci):
                ra = r_all[:, ci * 1024:(ci + 1) * 1024]
                e.copy(ra[0:64, :].rearrange("p (u w) -> p u w", w=128)[:, :, 64:128],
                       pt2[0:64, :].rearrange("p (u w) -> p u w", w=128)[:, :, 0:64])
                return e.copy(ra[64:128, :].rearrange("p (u w) -> p u w", w=128)[:, :, 64:128],
                              pt2[64:128, :].rearrange("p (u w) -> p u w", w=128)[:, :, 64:128])
            rungs.append(("scalar", t_knbev, "AFTER_PREV_TE", None))

            def t_v(e, ci=ci):
                for h in range(H):
                    r0 = 64 * (h % 2)
                    last = trt(e, pt2[r0:r0 + 64, h * 64:(h + 1) * 64],
                               ksl_f(v_f, h, ci), iden[r0:r0 + 64, r0:r0 + 64], (r0, r0))
                return last
            rungs.append(("tensor", t_v, "AFTER_PREV_ACT", None))

            def t_vev(e, ci=ci):
                ra = r_all[:, ci * 1024:(ci + 1) * 1024]
                e.copy(ra[0:64, :].rearrange("p (u w) -> p u w", w=128)[:, :, 0:64],
                       pt2[0:64, :].rearrange("p (u w) -> p u w", w=128)[:, :, 0:64])
                return e.copy(ra[64:128, :].rearrange("p (u w) -> p u w", w=128)[:, :, 0:64],
                              pt2[64:128, :].rearrange("p (u w) -> p u w", w=128)[:, :, 64:128])
            rungs.append(("scalar", t_vev, "AFTER_PREV_TE", ("TEV", ci)))
            return rungs

        def build_M(ci):
            rungs = []
            sl2, sl3 = ci % 2, ci % 3

            def m_gram(e, ci=ci):
                last = None
                for h in range(H):
                    r0 = 64 * (h % 2)
                    tp = (r0, r0)
                    cc = (h // 2) * 64
                    mmt(e, pb[2][r0:r0 + 64, cc:cc + 64], ksl_f(kn_f, h, ci),
                        ksl_f(knb_f, h, ci), (h < 2), (h >= H - 2), tp)
                    mmt(e, pb[3][r0:r0 + 64, cc:cc + 64], ksl_f(kn_f, h, ci),
                        ksl_f(q_f, h, ci), (h < 2), (h >= H - 2), tp)
                    last = mmt(e, pb[4][r0:r0 + 64, cc:cc + 64], ksl_f(knb_f, h, ci),
                               ksl_f(kn_f, h, ci), (h < 2), (h >= H - 2), tp)
                return last
            w = [(s_dve, FIELDS), (s_dve, DUP_DVE)]
            if ci > 0:
                w += [(s_act, SQ_FIN[ci - 1]), (s_dve, GRDVE[ci - 1])]
            rungs.append(("tensor", m_gram, w, ("GR_TE", ci)))

            def m_gramev(e, ci=ci, sl2=sl2, sl3=sl3):
                v3 = lambda t: t.rearrange("p (a b) -> p a b", b=64)
                e.scalar_tensor_tensor(v3(t0_d[sl2][:]), v3(pb[2][:]), -1.0, mup,
                                       op0=ALU.mult, op1=ALU.mult)
                e.tensor_tensor(v3(dt_d[sl3][:]), v3(pb[3][:]), mui, op=ALU.mult)
                return e.scalar_tensor_tensor(v3(p0_s[:]), v3(pb[4][:]), -1.0, mlo,
                                              op0=ALU.mult, op1=ALU.mult)
            w = ["AFTER_PREV_TE"]
            if ci >= 2:
                w.append((s_te, CH_TE[(ci - 2, 5)]))
            if ci >= 1:
                w.append((s_te, SQ_TE_LAST[ci - 1]))
            if ci >= 3:
                w.append((s_te, QH_TE[ci - 3]))
            rungs.append(("vector", m_gramev, w, ("GRDVE", ci)))

            for lev in range(5):
                def m_sq(e, ci=ci, lev=lev, sl2=sl2):
                    tprev = t0_d[sl2] if lev == 0 else tl_d[sl2][lev - 1]
                    pprev = p0_s if lev == 0 else pl_s[lev - 1]
                    last = None
                    for h in range(H):
                        r0 = 64 * (h % 2)
                        tp = (r0, r0)
                        cc = (h // 2) * 64
                        rsl, csl = slice(r0, r0 + 64), slice(cc, cc + 64)
                        last = mmt(e, pb[2][rsl, csl], pprev[rsl, csl],
                                   tprev[rsl, csl], (h < 2), (h >= H - 2), tp)
                        if lev < 4:
                            last = mmt(e, pb[3][rsl, csl], tprev[rsl, csl],
                                       pprev[rsl, csl], (h < 2), (h >= H - 2), tp)
                    return last
                tag = ("SQ_TE_LAST", ci) if lev == 4 else None
                rungs.append(("tensor", m_sq, ["AFTER_PREV_DVE", "AFTER_PREV_ACT"], tag))

                def m_sqev(e, ci=ci, lev=lev, sl2=sl2):
                    last = e.copy(tl_d[sl2][lev][:], pb[2][:])
                    if lev < 4:
                        last = e.copy(pl_s[lev][:], pb[3][:])
                    return last
                w = ["AFTER_PREV_TE"]
                if ci >= 2:
                    w.append((s_te, CH_TE[(ci - 2, 5)]))
                tag = ("SQEV", (ci, lev))
                if lev == 4:
                    rungs.append(("scalar", m_sqev, w, [tag, ("SQ_FIN", ci)]))
                else:
                    rungs.append(("scalar", m_sqev, w, tag))
            return rungs

        def build_C(ci):
            rungs = []
            sl2 = ci % 2
            for lev in range(6):
                def c_mm(e, ci=ci, lev=lev, sl2=sl2):
                    tlev = t0_d[sl2] if lev == 0 else tl_d[sl2][lev - 1]
                    rsrc = r_all[:, ci * 1024:(ci + 1) * 1024] if lev == 0 else xch_s
                    last = None
                    for h in range(H):
                        r0 = 64 * (h % 2)
                        tp = (r0, r0)
                        cc = (h // 2) * 128
                        bank = pb[0] if cc < 512 else pb[1]
                        ccc = cc % 512
                        last = mmt(e, bank[r0:r0 + 64, ccc:ccc + 128],
                                   tlev[r0:r0 + 64, (h // 2) * 64:(h // 2) * 64 + 64],
                                   rsrc[r0:r0 + 64, cc:cc + 128],
                                   (h % 8 < 2), (h % 8 >= 6), tp)
                    return last
                if lev == 0:
                    w = [(s_act, TEV[ci]), (s_dve, GRDVE[ci])]
                    if ci > 0:
                        w.append((s_dve, CH_FIN[ci - 1]))
                        w.append((s_te, OT_TE[ci - 1]) if False else (s_dve, CH_FIN[ci - 1]))
                else:
                    w = ["AFTER_PREV_DVE", (s_act, SQEV[(ci, lev - 1)])]
                rungs.append(("tensor", c_mm, w, ("CH_TE", (ci, lev))))

                if lev == 0:
                    def c_ev0(e, ci=ci):
                        ra = r_all[:, ci * 1024:(ci + 1) * 1024]
                        e.tensor_tensor(xch_s[:, 0:512], ra[:, 0:512], pb[0][:],
                                        op=ALU.add)
                        return e.tensor_tensor(xch_s[:, 512:1024], ra[:, 512:1024],
                                               pb[1][:], op=ALU.add)
                    w = ["AFTER_PREV_TE"]
                    if ci >= 1:
                        w.append((s_te, CH_TE[(ci - 1, 5)]))
                    rungs.append(("vector", c_ev0, w, ("CHEV", (ci, 0))))
                elif lev < 5:
                    def c_ev(e, ci=ci, lev=lev):
                        e.tensor_tensor(xch_s[:, 0:512], xch_s[:, 0:512], pb[0][:],
                                        op=ALU.add)
                        return e.tensor_tensor(xch_s[:, 512:1024], xch_s[:, 512:1024],
                                               pb[1][:], op=ALU.add)
                    rungs.append(("vector", c_ev, ["AFTER_PREV_TE"], ("CHEV", (ci, lev))))
                else:
                    def c_split(e, ci=ci, sl2=sl2):
                        xv = xch_s.rearrange("p (u w) -> p u w", w=128)
                        e.tensor_tensor(vt_d[sl2][:, 0:256].rearrange("p (u w) -> p u w", w=64),
                                        xv[:, 0:4, 0:64],
                                        pb[0][:].rearrange("p (u w) -> p u w", w=128)[:, :, 0:64],
                                        op=ALU.add)
                        e.tensor_tensor(wt_d[sl2][:, 0:256].rearrange("p (u w) -> p u w", w=64),
                                        xv[:, 0:4, 64:128],
                                        pb[0][:].rearrange("p (u w) -> p u w", w=128)[:, :, 64:128],
                                        op=ALU.add)
                        e.tensor_tensor(vt_d[sl2][:, 256:512].rearrange("p (u w) -> p u w", w=64),
                                        xv[:, 4:8, 0:64],
                                        pb[1][:].rearrange("p (u w) -> p u w", w=128)[:, :, 0:64],
                                        op=ALU.add)
                        return e.tensor_tensor(wt_d[sl2][:, 256:512].rearrange("p (u w) -> p u w", w=64),
                                               xv[:, 4:8, 64:128],
                                               pb[1][:].rearrange("p (u w) -> p u w", w=128)[:, :, 64:128],
                                               op=ALU.add)
                    w = ["AFTER_PREV_TE"]
                    if ci >= 2:
                        w.append((s_te, QH_TE[ci - 2]))
                    rungs.append(("vector", c_split, w, ("CH_FIN", ci)))
            return rungs

        def build_S(ci):
            rungs = []
            sl2, sl3 = ci % 2, ci % 3

            def s_wt(e, ci=ci, sl2=sl2):
                for h in range(H):
                    r0 = 64 * (h % 2)
                    last = trt(e, pt[r0:r0 + 64, h * 64:(h + 1) * 64],
                               wt_d[sl2][r0:r0 + 64, (h // 2) * 64:(h // 2) * 64 + 64],
                               iden[r0:r0 + 64, r0:r0 + 64], (r0, r0))
                return last
            w = [(s_dve, CH_FIN[ci])]
            if ci > 0:
                w.append((s_act, WT_ACT[ci - 1]))
            rungs.append(("tensor", s_wt, w, None))

            def s_wtev(e, ci=ci, sl2=sl2):
                e.copy(wfm_d[sl2][0:64, :].rearrange("p (u w) -> p u w", w=64),
                       pt[0:64, 0:1024].rearrange("p (u w) -> p u w", w=128)[:, :, 0:64])
                return e.copy(wfm_d[sl2][64:128, :].rearrange("p (u w) -> p u w", w=64),
                              pt[64:128, 0:1024].rearrange("p (u w) -> p u w", w=128)[:, :, 64:128])
            rungs.append(("scalar", s_wtev, "AFTER_PREV_TE", ("WT_ACT", ci)))

            def s_vcorr(e, ci=ci, sl2=sl2):
                last = None
                for h in range(H):
                    r0 = 64 * (h % 2)
                    tp = (r0, r0)
                    cc = (h // 2) * 64
                    rsl, csl = slice(r0, r0 + 64), slice(cc, cc + 64)
                    last = mmt(e, pb[5][rsl, csl], wfm_d[sl2][rsl, csl],
                               state_b[rsl, csl], (h < 2), (h >= H - 2), tp)
                return last
            w = ["AFTER_PREV_ACT"]
            if ci > 0:
                w.append((s_dve, QHEV[ci - 1]))
            else:
                w.append((s_dve, W_DVE))
            rungs.append(("tensor", s_vcorr, w, ("SEQ1", ci)))

            def s_vnew(e, ci=ci, sl2=sl2):
                return e.tensor_tensor(vt_d[sl2][:], vt_d[sl2][:], pb[5][:],
                                       op=ALU.subtract)
            rungs.append(("vector", s_vnew, "AFTER_PREV_TE", None))

            def s_omm(e, ci=ci, sl2=sl2, sl3=sl3):
                last = None
                for h in range(H):
                    r0 = 64 * (h % 2)
                    tp = (r0, r0)
                    cc = (h // 2) * 64
                    rsl, csl = slice(r0, r0 + 64), slice(cc, cc + 64)
                    mmt(e, pb[5][rsl, csl], ksl_f(q_f, h, ci),
                        state_b[rsl, csl], (h < 2), False, tp)
                    last = mmt(e, pb[5][rsl, csl], dt_d[sl3][rsl, csl],
                               vt_d[sl2][rsl, csl], False, (h >= H - 2), tp)
                return last
            rungs.append(("tensor", s_omm, "AFTER_PREV_DVE", ("SEQ2", ci)))

            def s_oev(e, ci=ci, ela_dupv=ela_dupv):
                ed = ela_dupv[:, ci * 16:(ci + 1) * 16].rearrange("p (u t) -> p u t", t=2)
                e.tensor_tensor(
                    osc_s[0:64, :].rearrange("p (u w) -> p u w", w=64),
                    pb[5][0:64, :].rearrange("p (u w) -> p u w", w=64),
                    ed[0:64, :, 0:1].broadcast_to((64, 8, 64)), op=ALU.mult)
                return e.tensor_tensor(
                    osc_s[64:128, :].rearrange("p (u w) -> p u w", w=64),
                    pb[5][64:128, :].rearrange("p (u w) -> p u w", w=64),
                    ed[64:128, :, 1:2].broadcast_to((64, 8, 64)), op=ALU.mult)
            w = ["AFTER_PREV_TE", (s_dve, DUP_DVE)]
            if ci > 0:
                w.append((s_spo, OSPILL_AT[ci - 1]))
            elif s_spo.v:
                w.append((s_spo, s_spo.v))
            rungs.append(("vector", s_oev, w, ("OEVD", ci)))

            def s_ds(e, ci=ci, sl2=sl2):
                last = None
                for h in range(H):
                    r0 = 64 * (h % 2)
                    tp = (r0, r0)
                    cc = (h // 2) * 64
                    rsl, csl = slice(r0, r0 + 64), slice(cc, cc + 64)
                    last = mmt(e, pb[5][rsl, csl],
                               knt_all[rsl, ci * 1024 + h * 64: ci * 1024 + (h + 1) * 64],
                               vt_d[sl2][rsl, csl], (h < 2), (h >= H - 2), tp)
                return last
            rungs.append(("tensor", s_ds, "AFTER_PREV_DVE", ("DS_TE", ci)))

            def s_sup1(e):
                return e.tensor_tensor(state[:], state[:], pb[5][:], op=ALU.add)
            rungs.append(("vector", s_sup1, ["AFTER_PREV_TE", (s_dve, LAM_DVE)], None))

            def s_sup2(e, ci=ci, lam_allv=lam_allv):
                lam_c = lam_allv.rearrange("p (u c) -> p u c", c=NCH)[:, :, ci:ci + 1]
                return e.tensor_tensor(
                    state[:].rearrange("p (u w) -> p u w", w=64),
                    state[:].rearrange("p (u w) -> p u w", w=64),
                    lam_c.broadcast_to((128, 8, 64)), op=ALU.mult)
            rungs.append(("vector", s_sup2, "AFTER_PREV_DVE", None))

            def s_sup3(e):
                return e.tensor_copy(state_b[:], state[:])
            rungs.append(("vector", s_sup3, "AFTER_PREV_DVE", ("SUPD", ci)))

            gci = fr * NCH + ci

            def s_ospill(e, gci=gci):
                return [e.dma_start(out=oz_d[gci], in_=osc_s[:])]
            s_ospill.n_dma = 1
            w = ["AFTER_PREV_DVE"] + ([(s_spo, s_spo.v)] if s_spo.v else [])
            rungs.append(("spill_o", s_ospill, w, ("OSPILL_AT", ci)))

            def s_mh(e, ci=ci, sl2=sl2):
                last = None
                for h in range(H):
                    r0 = 64 * (h % 2)
                    tp = (r0, r0)
                    cc = (h // 2) * 64
                    rsl, csl = slice(r0, r0 + 64), slice(cc, cc + 64)
                    last = mmt(e, pb[5][rsl, csl], wt_d[sl2][rsl, csl],
                               knt_all[rsl, ci * 1024 + h * 64: ci * 1024 + (h + 1) * 64],
                               (h < 2), (h >= H - 2), tp)
                return last
            rungs.append(("tensor", s_mh, "AFTER_PREV_DVE", None))

            def s_mhev(e, ci=ci):
                v3 = lambda t: t.rearrange("p (a b) -> p a b", b=64)
                e.tensor_tensor(v3(o_f[0:64, 0:512]),
                                iden[0:64, 0:64].unsqueeze(1).broadcast_to((64, 8, 64)),
                                v3(pb[5][0:64, :]), op=ALU.subtract)
                return e.tensor_tensor(v3(o_f[64:128, 0:512]),
                                       iden[64:128, 64:128].unsqueeze(1).broadcast_to((64, 8, 64)),
                                       v3(pb[5][64:128, :]), op=ALU.subtract)
            w = ["AFTER_PREV_TE", (s_dve, LAM_DVE)]
            if ci > 0:
                w.append((s_spq, SPILL_AT[ci - 1]))
            elif s_spq.v:
                w.append((s_spq, s_spq.v))
            rungs.append(("vector", s_mhev, w, None))

            def s_mhev2(e, ci=ci, lam_allv=lam_allv):
                v3 = lambda t: t.rearrange("p (a b) -> p a b", b=64)
                lam_c = lam_allv.rearrange("p (u c) -> p u c", c=NCH)[:, :, ci:ci + 1]
                return e.tensor_tensor(v3(o_f[:, 0:512]), v3(o_f[:, 0:512]),
                                       lam_c.broadcast_to((128, 8, 64)), op=ALU.mult)
            rungs.append(("vector", s_mhev2, "AFTER_PREV_DVE", None))

            def s_qh(e, ci=ci, sl2=sl2, sl3=sl3):
                last = None
                for h in range(H):
                    r0 = 64 * (h % 2)
                    tp = (r0, r0)
                    cc = (h // 2) * 64
                    rsl, csl = slice(r0, r0 + 64), slice(cc, cc + 64)
                    last = mmt(e, pb[5][rsl, csl], wt_d[sl2][rsl, csl],
                               dt_d[sl3][rsl, csl], (h < 2), (h >= H - 2), tp)
                return last
            rungs.append(("tensor", s_qh, "AFTER_PREV_DVE", ("QH_TE", ci)))

            def s_qhev(e, ci=ci):
                qv = q_f.rearrange("p (u t) -> p u t", t=FT)[:, :, ci * CK:(ci + 1) * CK]
                return e.tensor_tensor(o_f[:, 512:1024].rearrange("p (a b) -> p a b", b=64),
                                       qv, pb[5][:].rearrange("p (a b) -> p a b", b=64),
                                       op=ALU.subtract)
            rungs.append(("vector", s_qhev, "AFTER_PREV_TE", ("QHEV", ci)))

            def s_mqspill(e, gci=gci):
                return [e.dma_start(out=mh_d[gci], in_=o_f[:, 0:512]),
                        e.dma_start(out=qh_d[gci], in_=o_f[:, 512:1024])]
            s_mqspill.n_dma = 2
            w = ["AFTER_PREV_DVE"] + ([(s_spq, s_spq.v)] if s_spq.v else [])
            rungs.append(("spill_q", s_mqspill, w, ("SPILL_AT", ci)))
            return rungs

        def emit_rungs(rungs, own):
            kind, fn, waits, tag = rungs
            if isinstance(waits, str):
                waits = [waits]
            rw = []
            for x in waits:
                if x == "AFTER_PREV_TE":
                    if own.get("tensor"):
                        rw.append((s_te, own["tensor"]))
                elif x == "AFTER_PREV_DVE":
                    if own.get("vector"):
                        rw.append((s_dve, own["vector"]))
                elif x == "AFTER_PREV_ACT":
                    if own.get("scalar"):
                        rw.append((s_act, own["scalar"]))
                else:
                    rw.append(x)
            if kind in ("spill", "spill_o", "spill_q"):
                sem = {"spill": s_sp, "spill_o": s_spo, "spill_q": s_spq}[kind]
                val = SY.g(fn, *rw, sem=sem)
                own[kind] = val
            else:
                eng = {"tensor": TE, "vector": VE, "scalar": SE}[kind]
                val = eng.g(fn, *rw)
                own[kind] = val
            if tag is not None:
                tags = tag if isinstance(tag, list) else [tag]
                for tname, tkey in tags:
                    globals_map[tname][tkey] = val
            return val

        globals_map = dict(GR_TE={}, GRDVE=GRDVE, SQEV=SQEV, SQ_FIN=SQ_FIN,
                           SQ_TE_LAST=SQ_TE_LAST, CH_TE=CH_TE, CHEV=CHEV,
                           CH_FIN=CH_FIN, SEQ1=SEQ1, SEQ2=SEQ2, DS_TE=DS_TE,
                           SUPD=SUPD, OEVD=OEVD, WT_ACT=WT_ACT, TEV=TEV,
                           QH_TE=QH_TE, QHEV=QHEV, SPILL_AT=SPILL_AT, OSPILL_AT=OSPILL_AT)

        th_i = 0
        for it in range(NCH + 3):
            stages = []
            if it < NCH:
                stages.append(("T", it, build_T(it), {}))
            if 0 <= it - 1 < NCH:
                stages.append(("M", it - 1, build_M(it - 1), {}))
            if 0 <= it - 2 < NCH:
                stages.append(("C", it - 2, build_C(it - 2), {}))
            if 0 <= it - 3 < NCH:
                stages.append(("S", it - 3, build_S(it - 3), {}))
            mx = max(len(s[2]) for s in stages) if stages else 0
            for r in range(mx):
                for (_, _, l, own) in stages:
                    if r < len(l):
                        emit_rungs(l[r], own)
                if it >= NCH + 1 and th_i < len(NEXT_TH):
                    NEXT_TH[th_i]()
                    th_i += 1
        while th_i < len(NEXT_TH):
            NEXT_TH[th_i]()
            th_i += 1

        GEV = {}
        for g in range(NG):
            def ffn_mm(e, g=g):
                for hh in range(2):
                    for kc in range(8):
                        last = e.matmul(pb[hh][:, 0:512],
                                        xq_f[:, kc * FT + g * 128: kc * FT + (g + 1) * 128],
                                        wfn[:, kc * 1024 + hh * 512: kc * 1024 + (hh + 1) * 512],
                                        start=(kc == 0), stop=(kc == 7))
                return last
            waits = [(s_act, XQF_ACT), (s_dve, CAST_FFN)]
            if g > 0:
                waits.append((s_dve, FEV1[g - 1]))
            else:
                waits.append((s_dve, CH_FIN[NCH - 1]))
            FTE = TE.g(ffn_mm, *waits)

            def ffn_ev1(e, g=g):
                for hh in range(2):
                    sl = slice(hh * 512, (hh + 1) * 512)
                    last = e.scalar_tensor_tensor(s_scr[:, sl], pb[hh][:, 0:512],
                                                  qs_s[g][:], fws_b[:, sl],
                                                  op0=ALU.mult, op1=ALU.mult)
                return last
            fdv_waits = [(s_te, FTE)]
            if g > 0:
                fdv_waits.append((s_act, GEV[g - 1]))
            if g == 0:
                FEV1 = {}
            FEV1[g] = VE.g(ffn_ev1, *fdv_waits)

            def ffn_ev2(e, g=g):
                for hh in range(2):
                    sl = slice(hh * 512, (hh + 1) * 512)
                    last = e.tensor_tensor(s_scr[:, sl], s_scr[:, sl], fbias_b[:, sl],
                                           op=ALU.add)
                return last
            FDV = VE.g(ffn_ev2)

            def ffn_silu(e, g=g):
                return e.activation(g_t[:, g * 1024:(g + 1) * 1024], s_scr[:], AF.Silu)
            sw = [(s_dve, FDV)]
            if g == 0 and fr > 0:
                sw.append((s_sp, PREV_SPG))
            GEV[g] = SE.g(ffn_silu, *sw)


        @dma_group(NG + 1)
        def sp_g(e, fr=fr, frow=frow, ela_dupv=ela_dupv):
            a = [e.dma_start(out=gz_d[frow + g * 128: frow + (g + 1) * 128, :],
                             in_=g_t[:, g * 1024:(g + 1) * 1024]) for g in range(NG)]
            a.append(e.dma_start(out=eld_d[fr], in_=ela_dupv))
            return a
        w = [(s_act, GEV[NG - 1]), (s_dve, DUP_DVE)] + ([(s_sp, s_sp.v)] if s_sp.v else [])
        SPG = SY.g(sp_g, *w, sem=s_sp)
        PREV_SPG = SPG
        PREV_FEV1 = FEV1[NG - 1]
        PREV_GEV = GEV[NG - 1]
        PREV_QHEV3 = QHEV[NCH - 1]
        LAST_SUPD = SUPD[NCH - 1]
        LAST_SPILL = (s_sp.v, s_spo.v, s_spq.v)

    sc_t = k_f[:, 0:512]
    P1_TE_END = s_te.v
    P1_DVE_END = s_dve.v
    P1_ACT_END = s_act.v

    @dma_group(1)
    def st_dma1(e):
        return [e.dma_start(out=scin_d, in_=state_b[:])]
    STD1 = SY.g(st_dma1, (s_dve, LAST_SUPD), sem=s_st)

    def cc_seq(e, STD1=STD1):
        e.wait_ge(s_st.h, STD1)
        e.collective_compute(
            "AllGather", ALU.bypass, replica_groups=groups,
            ins=[scin_d.opt()], outs=[scout_d.opt()]).then_inc(s_cc.h, 1)
        return e.nop()
    GP.g(cc_seq)
    s_cc.v += 1
    CC_GP = s_cc.v

    @dma_group(1)
    def st_dma2(e):
        return [e.dma_start(out=sc_t, in_=scout_d[0:128, :])]
    STD2 = SY.g(st_dma2, (s_cc, CC_GP), sem=s_st)

    def steer1(e):
        return e.tensor_scalar(state[:], sc_t, shalf[:], None, op0=ALU.mult)
    ST1 = VE.g(steer1, (s_st, STD2))

    def steer2(e):
        return e.tensor_copy(state_b[:], state[:])
    ST_DVE = VE.g(steer2, (s_dve, ST1))

    x2v = wq.bitcast(F32)
    g2v = wk
    x_p = lambda sl: x2v[:, sl * 2048:(sl + 1) * 2048]
    g_p = lambda sl: g2v[:, sl * 2048:(sl + 1) * 2048]
    ela_p = lambda sl: g2v[:, 4096 + sl * 64: 4096 + (sl + 1) * 64]
    for fr in range(NFR):
        frow = fr * F
        sl = fr % 2
        mh_t = lambda s: kn_f[:, s * 1024: s * 1024 + 1024]
        qh_t = lambda s: knb_f[:, s * 1024: s * 1024 + 1024]
        oz_t = lambda s: v_f[:, s * 1024: s * 1024 + 1024]
        mh_t = lambda s: kn_f[:, s * 512: (s + 1) * 512]
        qh_t = lambda s: knb_f[:, s * 512: (s + 1) * 512]
        oz_t = lambda s: v_f[:, s * 512: (s + 1) * 512]

        @dma_group(NG + NG + 1)
        def ld2(e, fr=fr, frow=frow, sl=sl):
            a = [e.dma_start(out=g_p(sl)[:, g * 1024:(g + 1) * 1024],
                             in_=gz_d[frow + g * 128: frow + (g + 1) * 128, :])
                 for g in range(NG)]
            a += [e.dma_start(out=x_p(sl)[:, g * 1024:(g + 1) * 1024],
                              in_=x_d[frow + g * 128: frow + (g + 1) * 128, :])
                  for g in range(NG)]
            a.append(e.dma_start(out=ela_p(sl), in_=eld_d[fr]))
            return a
        w2 = [(s_sp, s_sp.v), (s_spo, LAST_SPILL[1]), (s_spq, LAST_SPILL[2])]
        if fr == 0:
            w2.append((s_te, P1_TE_END))
            w2.append((s_dve, P1_DVE_END))
            w2.append((s_act, P1_ACT_END))
        elif fr >= 2:
            w2.append((s_dve, P2FE[fr - 2]))
        LD2 = SY.g(ld2, *w2, sem=s_sp)

        CHLD = {}
        HOMS = {}
        P2SUP = {}
        P2OEV = {}
        P2OT = {}
        P2OA = {}
        for ci in range(NCH):
            gci = fr * NCH + ci
            s2 = ci % 2

            @dma_group(3)
            def ldc(e, gci=gci, s2=s2):
                return [e.dma_start(out=mh_t(s2), in_=mh_d[gci]),
                        e.dma_start(out=qh_t(s2), in_=qh_d[gci]),
                        e.dma_start(out=oz_t(s2), in_=oz_d[gci])]
            w = [(s_sp, s_sp.v), (s_spo, LAST_SPILL[1]), (s_spq, LAST_SPILL[2])]
            if ci >= 2:
                w.append((s_te, P2OT[ci - 2]))
            elif fr > 0:
                w.append((s_te, PREV_HOM[ci + 2]))
                w.append((s_dve, PREV_OEV[ci + 2]))
            CHLD[ci] = SY.g(ldc, *w, sem=s_sp)

            def hom_mm(e, ci=ci, s2=s2):
                last = None
                for h in range(H):
                    r0 = 64 * (h % 2)
                    tp = (r0, r0)
                    cc = (h // 2) * 64
                    rsl, csl = slice(r0, r0 + 64), slice(cc, cc + 64)
                    mmt(e, pb[4][rsl, csl], mh_t(s2)[rsl, csl], state_b[rsl, csl],
                        (h < 2), (h >= H - 2), tp)
                    last = mmt(e, pb[5][rsl, csl], qh_t(s2)[rsl, csl],
                               state_b[rsl, csl], (h < 2), (h >= H - 2), tp)
                return last
            w = [(s_sp, CHLD[ci])]
            if ci > 0:
                w.append((s_dve, P2SUP[ci - 1]))
                w.append((s_dve, P2OEV[ci - 1]))
            elif fr == 0:
                w.append((s_dve, ST_DVE))
            else:
                w.append((s_dve, PREV_SUP3))
                w.append((s_dve, PREV_OEV[3]))
            HOM = TE.g(hom_mm, *w)
            HOMS[ci] = HOM

            def p2_oc(e, ci=ci, s2=s2, sl=sl):
                ed = ela_p(sl)[:, ci * 16:(ci + 1) * 16].rearrange("p (u t) -> p u t", t=2)
                e.scalar_tensor_tensor(
                    osc_s[0:64, :].rearrange("p (u w) -> p u w", w=64),
                    pb[5][0:64, :].rearrange("p (u w) -> p u w", w=64),
                    1.0, ed[0:64, :, 0:1].broadcast_to((64, 8, 64)),
                    op0=ALU.mult, op1=ALU.mult)
                return e.scalar_tensor_tensor(
                    osc_s[64:128, :].rearrange("p (u w) -> p u w", w=64),
                    pb[5][64:128, :].rearrange("p (u w) -> p u w", w=64),
                    1.0, ed[64:128, :, 1:2].broadcast_to((64, 8, 64)),
                    op0=ALU.mult, op1=ALU.mult)
            w = [(s_te, HOM)]
            if ci >= 1:
                w.append((s_te, P2OT[ci - 1]))
            P2OC1 = VE.g(p2_oc, *w)

            def p2_oc2(e, s2=s2):
                return e.tensor_tensor(osc_s[:], osc_s[:], oz_t(s2), op=ALU.add)
            P2OEV[ci] = VE.g(p2_oc2, (s_dve, P2OC1))

            def p2_sup(e):
                return e.tensor_copy(state_b[:], pb[4][:])
            P2SUP[ci] = VE.g(p2_sup, (s_te, HOM))

            def p2_ot(e, ci=ci):
                for h in range(H):
                    r0 = 64 * (h % 2)
                    last = trt(e, pt[r0:r0 + 64, h * 64:(h + 1) * 64],
                               osc_s[r0:r0 + 64, (h // 2) * 64:(h // 2) * 64 + 64],
                               iden[r0:r0 + 64, r0:r0 + 64], (r0, r0))
                return last
            w = [(s_dve, P2OEV[ci])]
            if ci >= 1:
                w.append((s_act, P2OA[ci - 1]))
            elif fr > 0:
                w.append((s_act, PREV_P2OA3))
            P2OT[ci] = TE.g(p2_ot, *w)

            def p2_oa(e, ci=ci):
                tk0 = ci * CK
                e.copy(o_f[0:64, :].rearrange("p (u w) -> p u w", w=FT)[:, :, tk0:tk0 + 64],
                       pt[0:64, 0:1024].rearrange("p (u w) -> p u w", w=128)[:, :, 0:64])
                return e.copy(o_f[64:128, :].rearrange("p (u w) -> p u w", w=FT)[:, :, tk0:tk0 + 64],
                              pt[64:128, 0:1024].rearrange("p (u w) -> p u w", w=128)[:, :, 64:128])
            aw = [(s_te, P2OT[ci])]
            if ci == 0 and fr > 0:
                aw.append((s_dve, PREV_ONF7))
                aw.append((s_te, PREV_WTE))
            P2OA[ci] = SE.g(p2_oa, *aw)

        ksqb = [ksq, field_s]
        OSQ = {}
        OTE = {}
        for p in range(8):
            def osq(e, p=p):
                return e.activation(ksqb[p % 2][:, 0:FT], o_f[:, p * FT:(p + 1) * FT],
                                    AF.Square)
            waits = [(s_act, P2OA[NCH - 1])] if p == 0 else                     ([(s_te, OTE[p - 2])] if p >= 2 else [])
            OSQ[p] = SE.g(osq, *waits)

            def omm(e, p=p):
                return e.matmul(pb[0][0:16, 0:FT], csel[:, p * 16:(p + 1) * 16],
                                ksqb[p % 2][:, 0:FT], start=(p == 0), stop=(p == 7))
            ow = [(s_act, OSQ[p])]
            if p == 0:
                ow.append((s_dve, ST_DVE if fr == 0 else PREV_ONF7))
            OTE[p] = TE.g(omm, *ow)

        VE.g(lambda e: e.tensor_scalar(rms_h[:], pb[0][0:16, 0:FT], 1.0 / HD, 1e-6,
                                       op0=ALU.mult, op1=ALU.add), (s_te, OTE[7]))
        RMS_DVE = VE.g(lambda e: e.reciprocal(rms_h[:], rms_h[:]))

        def rms_sqrt(e):
            return e.activation(rms_hb[:], rms_h[:], AF.Sqrt)
        RMS_ACT = SE.g(rms_sqrt, (s_dve, RMS_DVE))

        ONF = {}
        for p in range(8):
            def rfmm(e, p=p):
                bank = pb[0] if p % 2 == 0 else pb[1]
                return e.matmul(bank[:, 0:FT], ep[:, p * 128:(p + 1) * 128],
                                rms_hb[:], start=True, stop=True)
            waits = [(s_act, RMS_ACT)]
            if p >= 2:
                waits.append((s_dve, ONF[p - 2]))
            elif p == 1:
                waits.append((s_dve, ST_DVE if fr == 0 else P2FE[fr - 1]))
            rte = TE.g(rfmm, *waits)

            def onev(e, p=p):
                bank = pb[0] if p % 2 == 0 else pb[1]
                return e.scalar_tensor_tensor(o_f[:, p * FT:(p + 1) * FT],
                                              o_f[:, p * FT:(p + 1) * FT],
                                              normw_f[p][:], bank[:, 0:FT],
                                              op0=ALU.mult, op1=ALU.mult)
            ONF[p] = VE.g(onev, (s_te, rte))

        pt2f = pt2.bitcast(F32)
        WOBK = [(pb[2], pb[3]), (pb[1], pt2f)]
        g32 = g_t.bitcast(F32)
        ODST = [out_t, g32]
        for g in range(NG):
            def wo_mm(e, g=g):
                for hh in range(2):
                    for kc in range(8):
                        last = e.matmul(WOBK[g][hh][:, 0:512],
                                        o_f[:, kc * FT + g * 128: kc * FT + (g + 1) * 128],
                                        wo[:, kc * 1024 + hh * 512: kc * 1024 + (hh + 1) * 512],
                                        start=(kc == 0), stop=(kc == 7))
                return last
            waits = [(s_dve, ONF[7])]
            if fr > 0:
                waits.append((s_out, PREV_OUT[g]))
            WTE = TE.g(wo_mm, *waits)

            def final_ev1(e, g=g, slf=sl):
                for hh in range(2):
                    last = e.tensor_tensor(
                        ODST[g][:, hh * 512:(hh + 1) * 512], WOBK[g][hh][:, 0:512],
                        g_p(slf)[:, g * 1024 + hh * 512:g * 1024 + (hh + 1) * 512],
                        op=ALU.mult)
                return last
            ew1 = [(s_te, WTE), (s_sp, CHLD[NCH - 1])]
            if fr > 0:
                ew1.append((s_out, PREV_OUT[g]))
            VE.g(final_ev1, *ew1)

            def final_ev2(e, g=g, slf=sl):
                for hh in range(2):
                    last = e.tensor_tensor(
                        ODST[g][:, hh * 512:(hh + 1) * 512], ODST[g][:, hh * 512:(hh + 1) * 512],
                        x_p(slf)[:, g * 1024 + hh * 512:g * 1024 + (hh + 1) * 512],
                        op=ALU.add)
                return last
            FEV = VE.g(final_ev2)

            @dma_group(1)
            def st_out(e, g=g, frow=frow):
                return [e.dma_start(out=out_d[frow + g * 128: frow + (g + 1) * 128, :],
                                    in_=ODST[g][:])]
            if fr == 0 and g == 0:
                PREV_OUT = {}
            ow2 = [(s_dve, FEV)] + ([(s_out, s_out.v)] if s_out.v else [])
            PREV_OUT[g] = GPD.g(st_out, *ow2)
        if fr == 0:
            P2FE = {}
        P2FE[fr] = FEV
        PREV_HOM = dict(HOMS)
        PREV_OEV = dict(P2OEV)
        PREV_SUP3 = P2SUP[NCH - 1]
        PREV_RMS = RMS_DVE
        PREV_ONF7 = ONF[7]
        PREV_WTE = WTE
        PREV_P2OA3 = P2OA[NCH - 1]

    FINAL_OUT = s_out.v

    with nc.Block() as block:
        @block.sync
        def _(e):
            _emit(e, SY)
            e.wait_ge(s_out.h, FINAL_OUT)

        @block.tensor
        def _(e):
            _emit(e, TE)

        @block.vector
        def _(e):
            _emit(e, VE)

        @block.scalar
        def _(e):
            _emit(e, SE)

        @block.gpsimd
        def _(e):
            _emit(e, GPSTG)
            _emit(e, GP)
            _emit(e, GPD)

    return nc



def _consts(F):
    bf = ml_dtypes.bfloat16
    iden = np.eye(128, dtype=bf)
    r = np.arange(64)
    pat_up = (r[:, None] < r[None, :]).astype(np.float32)
    pat_lo = (r[:, None] > r[None, :]).astype(np.float32)
    pat_ui = (r[:, None] <= r[None, :]).astype(np.float32)
    mup = np.tile(pat_up, (2, 1)).astype(bf)
    mlo = np.tile(pat_lo, (2, 1)).astype(bf)
    mui = np.tile(pat_ui, (2, 1)).astype(bf)
    ep = np.zeros((16, 1024), np.float32)
    for h in range(16):
        p, hi = h // 2, h % 2
        ep[h, p * 128 + hi * 64: p * 128 + (hi + 1) * 64] = 1.0
    csel = np.zeros((128, 128), np.float32)
    for p in range(8):
        csel[0:64, p * 16 + 2 * p] = 1.0
        csel[64:128, p * 16 + 2 * p + 1] = 1.0
    dup = np.zeros((64, 128), np.float32)
    for t in range(64):
        dup[t, t] = 1.0
        dup[t, 64 + t] = 1.0
    reset = np.ones((16, F), np.float32)
    reset[:, 0::CK] = 0.0
    return dict(iden=iden, mup=mup, mlo=mlo, mui=mui, ep=ep.astype(bf),
                csel=csel.astype(bf), dup=dup.astype(bf), reset=reset)


def _in_map(xb, inputs, F):
    c = _consts(F)
    return dict(
        x=np.ascontiguousarray(xb),
        wq=np.asarray(inputs["L0_attn_Wq"], np.float32),
        wk=np.asarray(inputs["L0_attn_Wk"], np.float32),
        wv=np.asarray(inputs["L0_attn_Wv"], np.float32),
        wo=np.asarray(inputs["L0_attn_Wo"], np.float32),
        wbf=np.ascontiguousarray(np.concatenate(
            [np.asarray(inputs["L0_attn_Wbet"], np.float32),
             np.asarray(inputs["L0_attn_Wf"], np.float32)], axis=1)),
        normw=np.asarray(inputs["L0_attn_norm_w"], np.float32).reshape(1, D),
        fqt=np.ascontiguousarray(np.asarray(inputs["L0_ffn_weight_q"]).T),
        fscale=np.asarray(inputs["L0_ffn_weight_scale"], np.float32).reshape(1, D),
        fbias=np.asarray(inputs["L0_ffn_bias"], np.float32).reshape(1, D),
        **c,
    )


def kernel(L0_attn_x, L0_attn_Wq, L0_attn_Wk, L0_attn_Wv, L0_attn_Wo,
           L0_attn_Wbet, L0_attn_Wf, L0_attn_norm_w, L0_ffn_weight_q,
           L0_ffn_weight_scale, L0_ffn_bias):
    inputs = dict(L0_attn_Wq=L0_attn_Wq, L0_attn_Wk=L0_attn_Wk,
                  L0_attn_Wv=L0_attn_Wv, L0_attn_Wo=L0_attn_Wo,
                  L0_attn_Wbet=L0_attn_Wbet, L0_attn_Wf=L0_attn_Wf,
                  L0_attn_norm_w=L0_attn_norm_w, L0_ffn_weight_q=L0_ffn_weight_q,
                  L0_ffn_weight_scale=L0_ffn_weight_scale, L0_ffn_bias=L0_ffn_bias)
    x = np.asarray(L0_attn_x, np.float32)
    B, N, _ = x.shape
    T, F = N // 2, 256
    groups = [[b, b + B] for b in range(B)]
    nc = _build(T, F, groups)
    in_maps = []
    for c in range(2 * B):
        beta, j = c % B, c // B
        im = _in_map(x[beta, j * T:(j + 1) * T], inputs, F)
        im["shalf"] = np.full((128, 1), float(j), np.float32)
        in_maps.append(im)
    res = run_bass_kernel_spmd(nc, in_maps, core_ids=list(range(2 * B)))
    out = np.stack([np.concatenate([np.asarray(res.results[b]["out"]),
                                    np.asarray(res.results[b + B]["out"])], axis=0)
                    for b in range(B)], axis=0)
    return out.astype(np.float32)


# version 26
# speedup vs baseline: 1.0367x; 1.0088x over previous
import numpy as np
import ml_dtypes
import concourse.bass as bass
import concourse.mybir as mybir
from concourse.bass_utils import run_bass_kernel_spmd

F32 = mybir.dt.float32
BF16 = mybir.dt.bfloat16
I8 = mybir.dt.int8
AF = mybir.ActivationFunctionType
ALU = mybir.AluOpType
AX = mybir.AxisListType

D = 1024
H = 16
HD = 64
CK = 64


class Sem:
    def __init__(self, nc, name):
        self.h = nc.alloc_semaphore(name)
        self.v = 0


class Eng:

    def __init__(self, sem, dma=False):
        self.sem = sem
        self.dma = dma
        self.groups = []

    def g(self, fn, *waits, sem=None, dma=None):
        sem = sem or self.sem
        dmaf = self.dma if dma is None else dma
        if not dmaf and sem.v > 0:
            waits = waits + ((sem, sem.v),)
        self.groups.append((tuple(waits), fn, sem, dmaf))
        if dmaf:
            n = getattr(fn, "n_dma", 1)
            sem.v += 16 * n
        else:
            sem.v += 1
        return sem.v


def _emit(engine, eng_sched):
    for waits, fn, sem, dmaf in eng_sched.groups:
        for (wsem, val) in waits:
            if val > 0:
                engine.wait_ge(wsem.h, val)
        last = fn(engine)
        if dmaf:
            for instr in (last if isinstance(last, (list, tuple)) else [last]):
                instr.then_inc(sem.h, 16)
        else:
            last.then_inc(sem.h, 1)


def dma_group(n):
    def deco(fn):
        fn.n_dma = n
        return fn
    return deco


def mmt(e, out, lhsT, rhs, start, stop, tp):
    return e.matmul(out, lhsT, rhs, start=start, stop=stop, tile_position=tp,
                    skip_group_check=(tp[1] == 64))


def trt(e, out, in_, idn, tp):
    return e.matmul(out, in_, idn, is_transpose=True, tile_position=tp,
                    skip_group_check=True)


def _build(T, F, groups):
    assert T % F == 0 and F % 128 == 0 and 128 <= F <= 512
    NFR = T // F
    NG = F // 128
    NCH = F // CK
    FT = F

    nc = bass.Bass("TRN2", target_bir_lowering=False)

    x_d = nc.dram_tensor("x", [T, D], F32, kind="ExternalInput").ap()
    wq_d = nc.dram_tensor("wq", [D, D], F32, kind="ExternalInput").ap()
    wk_d = nc.dram_tensor("wk", [D, D], F32, kind="ExternalInput").ap()
    wv_d = nc.dram_tensor("wv", [D, D], F32, kind="ExternalInput").ap()
    wo_d = nc.dram_tensor("wo", [D, D], F32, kind="ExternalInput").ap()
    wbf_d = nc.dram_tensor("wbf", [D, 32], F32, kind="ExternalInput").ap()
    normw_d = nc.dram_tensor("normw", [1, D], F32, kind="ExternalInput").ap()
    fqt_d = nc.dram_tensor("fqt", [D, D], I8, kind="ExternalInput").ap()
    fscale_d = nc.dram_tensor("fscale", [1, D], F32, kind="ExternalInput").ap()
    fbias_d = nc.dram_tensor("fbias", [1, D], F32, kind="ExternalInput").ap()
    iden_d = nc.dram_tensor("iden", [128, 128], BF16, kind="ExternalInput").ap()
    mup_d = nc.dram_tensor("mup", [128, 64], BF16, kind="ExternalInput").ap()
    mlo_d = nc.dram_tensor("mlo", [128, 64], BF16, kind="ExternalInput").ap()
    mui_d = nc.dram_tensor("mui", [128, 64], BF16, kind="ExternalInput").ap()
    ep_d = nc.dram_tensor("ep", [16, 1024], BF16, kind="ExternalInput").ap()
    csel_d = nc.dram_tensor("csel", [128, 128], BF16, kind="ExternalInput").ap()
    dup_d = nc.dram_tensor("dup", [64, 128], BF16, kind="ExternalInput").ap()
    reset_d = nc.dram_tensor("reset", [16, F], F32, kind="ExternalInput").ap()
    shalf_d = nc.dram_tensor("shalf", [128, 1], F32, kind="ExternalInput").ap()
    out_d = nc.dram_tensor("out", [T, D], F32, kind="ExternalOutput").ap()
    oz_d = nc.dram_tensor("oz_i", [T // CK, 128, 512], BF16).ap()
    mh_d = nc.dram_tensor("mh_i", [T // CK, 128, 512], BF16).ap()
    qh_d = nc.dram_tensor("qh_i", [T // CK, 128, 512], BF16).ap()
    gz_d = nc.dram_tensor("gz_i", [T, D], BF16).ap()
    eld_d = nc.dram_tensor("eld_i", [T // F, 128, 64], BF16).ap()
    scin_d = nc.dram_tensor("scin_i", [128, 512], BF16).ap()
    scout_d = nc.dram_tensor("scout_i", [256, 512], BF16).ap()

    def sb(name, shape, dtype):
        return nc.alloc_sbuf_tensor(name, shape, dtype).ap()

    wq = sb("wq_s", [128, 8192], BF16)
    wk = sb("wk_s", [128, 8192], BF16)
    wv = sb("wv_s", [128, 8192], BF16)
    wo = sb("wo_s", [128, 8192], BF16)
    wfn = sb("wfn_s", [128, 8192], BF16)
    wbf = sb("wbf_s", [128, 256], BF16)
    wstage8 = None
    iden = sb("iden_s", [128, 128], BF16)
    mup_t = sb("mup_s", [128, 64], BF16)
    mlo_t = sb("mlo_s", [128, 64], BF16)
    mui_t = sb("mui_s", [128, 64], BF16)
    mup = mup_t.unsqueeze(1).broadcast_to((128, 8, 64)).rearrange("p a b -> p (a b)") if False else None
    mup = mup_t.unsqueeze(1).broadcast_to((128, 8, 64))
    mlo = mlo_t.unsqueeze(1).broadcast_to((128, 8, 64))
    mui = mui_t.unsqueeze(1).broadcast_to((128, 8, 64))
    ep = sb("ep_s", [16, 1024], BF16)
    csel = sb("csel_s", [128, 128], BF16)
    dupc = sb("dup_s", [64, 128], BF16)
    resetm = sb("reset_s", [16, F], F32)
    normw_f = [sb(f"normw_f{p}", [128, 1], F32) for p in range(8)]
    fbias_b = sb("fbias_b", [128, D], BF16)
    fws_b = sb("fws_b", [128, D], BF16)

    x_t = sb("x_t", [128, NG * 1024], F32)
    wstage = x_t[:, 0:2048]
    xn_t = sb("xn_t", [128, NG * 1024], BF16)
    xn_f = sb("xn_f", [128, 8 * FT], BF16)
    xq_t = sb("xq_t", [128, NG * 1024], BF16)
    xq_f = sb("xq_f", [128, 8 * FT], BF16)
    q_f = sb("q_f", [128, 8 * FT], BF16)
    k_f = sb("k_f", [128, 8 * FT], BF16)
    kn_f = sb("kn_f", [128, 8 * FT], BF16)
    knb_f = sb("knb_f", [128, 8 * FT], BF16)
    v_f = sb("v_f", [128, 8 * FT], BF16)
    o_f = sb("o_f", [128, 8 * FT], BF16)
    ksq = sb("ksq", [128, FT], BF16)
    field_s = sb("field_s", [128, FT], BF16)
    g_t = sb("g_t", [128, NG * 1024], BF16)
    s_scr = sb("s_scr", [128, 1024], F32)
    out_t = sb("out_t", [128, 1024], F32)
    r2_scr = out_t
    sq_scr = sb("sq_scr", [128, 1024], F32)
    zbf = sq_scr[0:16, 0:2 * F]
    beta_hb = sb("beta_hb", [16, 2 * F], BF16)
    la_h = sb("la_h", [16, F], F32)
    ela_h = sb("ela_h", [16, F], BF16)
    elam_hb = sb("elam_hb", [16, NCH], BF16)
    bem_h = sb("bem_h", [16, 2 * F], BF16)
    rn_h = sb("rn_h", [16, F], BF16)
    rn_f32 = sb("rn_f32", [16, F], F32)
    rms_h = rn_f32
    zero16 = sb("zero16", [16, 1], F32)
    ones1 = sb("ones1", [128, 1], F32)
    flag1 = sb("flag1", [128, 1], F32)
    lam_all = sb("lam_all", [128, 2 * 8 * NCH], F32)
    ela_dup = sb("ela_dup", [128, 2 * 16 * NCH], BF16)

    ln_m = [sb(f"ln_m{g}", [128, 1], F32) for g in range(NG)]
    ln_r = [sb(f"ln_r{g}", [128, 1], F32) for g in range(NG)]
    ln_s2 = [sb(f"ln_s2{g}", [128, 1], F32) for g in range(NG)]
    qs_s = [sb(f"qs_s{g}", [128, 1], F32) for g in range(NG)]
    qsr_s = [sb(f"qsr_s{g}", [128, 1], F32) for g in range(NG)]
    state = sb("state", [128, 512], F32)
    state_b = sb("state_b", [128, 512], BF16)
    t0_d = [sb(f"t0_{s}", [128, 512], BF16) for s in range(2)]
    dt_d = [sb(f"dt_{s}", [128, 512], BF16) for s in range(3)]
    p0_s = sb("p0_s", [128, 512], BF16)
    tl_d = [[sb(f"tl{k}_{s}", [128, 512], BF16) for k in range(5)] for s in range(2)]
    pl_s = [sb(f"pl{k}", [128, 512], BF16) for k in range(4)]
    r_all = sb("r_all", [128, NCH * 1024], BF16)
    knt_all = sb("knt_all", [128, NCH * 1024], BF16)
    xch_s = sb("xch_s", [128, 1024], BF16)
    vt_d = [sb(f"vt_{s}", [128, 512], BF16) for s in range(2)]
    wt_d = [sb(f"wt_{s}", [128, 512], BF16) for s in range(2)]
    wfm_d = [sb(f"wfm_{s}", [128, 512], BF16) for s in range(2)]
    osc_s = sb("osc_s", [128, 512], BF16)
    shalf = sb("shalf_s", [128, 1], F32)
    rms_hb = beta_hb[:, 0:F]
    scr_h = sq_scr.bitcast(BF16)[0:64, 1024:1024 + 16 * NCH]
    wstage8 = r_all.bitcast(I8)[:, 0:8192]

    pb = [nc.alloc_psum_tensor(f"pb{i}", [128, 512], F32).ap() for i in range(6)]
    pt = nc.alloc_psum_tensor("pt6", [128, 1024], BF16).ap()
    pt2 = nc.alloc_psum_tensor("pt7", [128, 1024], BF16).ap()

    s_in = Sem(nc, "s_in")
    s_c = Sem(nc, "s_c")
    s_w = Sem(nc, "s_w")
    s_w2 = Sem(nc, "s_w2")
    s_wb = Sem(nc, "s_wb")
    s_wb2 = Sem(nc, "s_wb2")
    s_te = Sem(nc, "s_te")
    s_dve = Sem(nc, "s_dve")
    s_act = Sem(nc, "s_act")
    s_gp = Sem(nc, "s_gp")
    s_cc = Sem(nc, "s_cc")
    s_st = Sem(nc, "s_st")
    s_sp = Sem(nc, "s_sp")
    s_spo = Sem(nc, "s_spo")
    s_spq = Sem(nc, "s_spq")
    s_out = Sem(nc, "s_out")

    SY = Eng(s_in, dma=True)
    SW = Eng(s_w, dma=True)
    TE = Eng(s_te)
    VE = Eng(s_dve)
    SE = Eng(s_act)
    GP = Eng(s_cc)
    GPD = Eng(s_out, dma=True)
    GPSTG = Eng(s_wb, dma=True)


    @dma_group(19)
    def ld_consts(e):
        a = []
        a.append(e.dma_start(out=iden, in_=iden_d))
        a.append(e.dma_start(out=mup_t, in_=mup_d))
        a.append(e.dma_start(out=mlo_t, in_=mlo_d))
        a.append(e.dma_start(out=mui_t, in_=mui_d))
        a.append(e.dma_start(out=ep, in_=ep_d))
        a.append(e.dma_start(out=csel, in_=csel_d))
        a.append(e.dma_start(out=dupc, in_=dup_d))
        a.append(e.dma_start(out=resetm, in_=reset_d))
        a.append(e.dma_start(out=shalf, in_=shalf_d))
        for p8 in range(8):
            a.append(e.dma_start(out=normw_f[p8],
                                 in_=normw_d[:, p8 * 128:(p8 + 1) * 128].rearrange("o b -> b o")))
        a.append(e.dma_start(out=sq_scr[:, 0:1024], in_=fbias_d.squeeze(0).partition_broadcast(128)))
        a.append(e.dma_start(out=s_scr[:, 0:1024], in_=fscale_d.squeeze(0).partition_broadcast(128)))
        return a
    CONSTS_IN = GPSTG.g(ld_consts, sem=s_c)

    def init_zero(e):
        e.tensor_copy(fbias_b[:], sq_scr[:, 0:1024])
        e.tensor_copy(fws_b[:], s_scr[:, 0:1024])
        e.memset(ones1[:], 1.0)
        e.memset(flag1[:], 2.0 ** -30)
        e.memset(zero16[:], 0.0)
        e.memset(state_b[:], 0.0)
        return e.memset(state[:], 0.0)
    IZ = VE.g(init_zero, (s_c, CONSTS_IN))
    VE.g(lambda e: e.tensor_tensor(flag1[:], ones1[:], flag1[:], op=ALU.add))
    W_DVE = VE.g(lambda e: e.tensor_scalar(flag1[:], flag1[:], 1.0, 2.0 ** 30,
                                           op0=ALU.subtract, op1=ALU.mult))

    xq32 = xq_t.bitcast(F32)

    @dma_group(1)
    def ld_wbf(e):
        return [e.dma_start(out=xq32[:, 0:256].rearrange("p (a b) -> p a b", b=32),
                            in_=wbf_d.rearrange("(a p) b -> p a b", p=128))]
    WBF_IN = SY.g(ld_wbf, sem=s_w)
    CAST_WBF = VE.g(lambda e: e.tensor_copy(wbf[:], xq32[:, 0:256]), (s_w, WBF_IN))

    r32 = r_all.bitcast(F32)
    kn32 = knt_all.bitcast(F32)
    sy_pieces = [(wk_d, wk, kc0) for kc0 in (0, 2, 4, 6)] + \
                [(wq_d, wq, kc0) for kc0 in (0, 2, 4, 6)]
    sy_cast = {}
    for i, (wd, wsb, kc0) in enumerate(sy_pieces):
        slot = x_t[:, 0:2048] if i % 2 == 0 else r32[:, 0:2048]
        sem = s_w if i % 2 == 0 else s_w2

        @dma_group(2)
        def ld(e, wd=wd, kc0=kc0, slot=slot):
            return [e.dma_start(out=slot[:, j * 1024:(j + 1) * 1024],
                                in_=wd[(kc0 + j) * 128:(kc0 + j + 1) * 128, :])
                    for j in range(2)]
        w = [(s_dve, CAST_WBF)] if i < 2 else [(s_dve, sy_cast[i - 2])]
        ld_v = SY.g(ld, *w, sem=sem)

        def cast(e, wsb=wsb, kc0=kc0, slot=slot):
            return e.tensor_copy(wsb[:, kc0 * 1024:(kc0 + 2) * 1024], slot[:])
        sy_cast[i] = VE.g(cast, (sem, ld_v))
    SY_XT_CAST = sy_cast[len(sy_pieces) - 2]
    SY_LAST_CAST = sy_cast[len(sy_pieces) - 1]

    gp_pieces = [(wv_d, wv, kc0) for kc0 in (0, 2, 4, 6)] + \
                [(wo_d, wo, kc0) for kc0 in (0, 2, 4, 6)]
    gp_cast = {}
    for i, (wd, wsb, kc0) in enumerate(gp_pieces):
        sem = s_wb if i % 2 == 0 else s_wb2

        @dma_group(2)
        def ldg(e, wd=wd, kc0=kc0, i=i):
            if i % 2 == 0:
                return [e.dma_start(out=out_t[:, 0:1024], in_=wd[kc0 * 128:(kc0 + 1) * 128, :]),
                        e.dma_start(out=s_scr[:, 0:1024], in_=wd[(kc0 + 1) * 128:(kc0 + 2) * 128, :])]
            return [e.dma_start(out=kn32[:, j * 1024:(j + 1) * 1024],
                                in_=wd[(kc0 + j) * 128:(kc0 + j + 1) * 128, :])
                    for j in range(2)]
        w = [(s_dve, IZ)] if i < 2 else [(s_dve, gp_cast[i - 2])]
        ld_v = GPSTG.g(ldg, *w, sem=sem)

        def castg(e, wsb=wsb, kc0=kc0, i=i):
            if i % 2 == 0:
                e.tensor_copy(wsb[:, kc0 * 1024:(kc0 + 1) * 1024], out_t[:, 0:1024])
                return e.tensor_copy(wsb[:, (kc0 + 1) * 1024:(kc0 + 2) * 1024], s_scr[:, 0:1024])
            return e.tensor_copy(wsb[:, kc0 * 1024:(kc0 + 2) * 1024], kn32[:])
        gp_cast[i] = VE.g(castg, (sem, ld_v))
    GP_LAST_CAST = gp_cast[len(gp_pieces) - 1]

    LDX = {}

    @dma_group(NG)
    def ld_x0(e):
        return [e.dma_start(out=x_t[:, g * 1024:(g + 1) * 1024],
                            in_=x_d[g * 128:(g + 1) * 128, :]) for g in range(NG)]
    LDX[0] = SY.g(ld_x0, (s_dve, SY_XT_CAST))

    @dma_group(8)
    def ld_fqt(e):
        return [e.dma_start(out=wstage8[:, kc * 1024:(kc + 1) * 1024],
                            in_=fqt_d[kc * 128:(kc + 1) * 128, :]) for kc in range(8)]
    FW_IN = SY.g(ld_fqt, (s_dve, SY_LAST_CAST), sem=s_w)
    CAST_FFN = VE.g(lambda e: e.tensor_copy(wfn[:], wstage8[:]), (s_w, FW_IN))

    def sched_ldx(fr, *waits):
        frow = fr * F

        @dma_group(NG)
        def ld_x(e, frow=frow):
            return [e.dma_start(out=x_t[:, g * 1024:(g + 1) * 1024],
                                in_=x_d[frow + g * 128: frow + (g + 1) * 128, :])
                    for g in range(NG)]
        LDX[fr] = SY.g(ld_x, *waits)

    def build_front(fr, PV):
        par = fr % 2
        X_IN = LDX[fr]
        beta_v = beta_hb[:, par * F:(par + 1) * F]
        la_v = la_h[:]
        ela_v = ela_h[:]
        elam_v = elam_hb[:]
        bem_v = bem_h[:, par * F:(par + 1) * F]
        lam_v = lam_all[:, par * 8 * NCH:(par + 1) * 8 * NCH]
        eld_v = ela_dup[:, par * 16 * NCH:(par + 1) * 16 * NCH]
        FD = {}
        th = []

        def ln_sum(e):
            for g in range(NG):
                last = e.tensor_reduce(ln_m[g][:], x_t[:, g * 1024:(g + 1) * 1024],
                                       axis=AX.X, op=ALU.add)
            return last
        th.append(lambda: VE.g(ln_sum, (s_in, X_IN)))

        def ln_sq(e):
            for g in range(NG):
                scr = sq_scr[:] if g % 2 == 0 else xn_t[:, 0:1024]
                last = e.activation(scr, x_t[:, g * 1024:(g + 1) * 1024],
                                    AF.Square, accum_out=ln_s2[g][:])
            return last

        def _lnsq():
            lw = [(s_in, X_IN), (s_dve, W_DVE)]
            if "XQ" in PV:
                lw.append((s_dve, PV["XQ"]))
            FD["LNSQ"] = SE.g(ln_sq, *lw)
        th.append(_lnsq)

        def ln_stats1(e):
            for g in range(NG):
                e.tensor_scalar(ln_m[g][:], ln_m[g][:], 1.0 / D, None, op0=ALU.mult)
                last = e.tensor_scalar(ln_s2[g][:], ln_s2[g][:], 1.0 / D, None,
                                       op0=ALU.mult)
            return last
        th.append(lambda: VE.g(ln_stats1, (s_act, FD["LNSQ"])))

        def ln_stats2(e):
            for g in range(NG):
                last = e.tensor_tensor(ln_r[g][:], ln_m[g][:], ln_m[g][:], op=ALU.mult)
            return last
        th.append(lambda: VE.g(ln_stats2))

        def ln_stats3(e):
            for g in range(NG):
                last = e.tensor_tensor(ln_r[g][:], ln_s2[g][:], ln_r[g][:],
                                       op=ALU.subtract)
            return last
        th.append(lambda: VE.g(ln_stats3))

        def ln_stats4(e):
            for g in range(NG):
                last = e.tensor_scalar(ln_r[g][:], ln_r[g][:], 1e-5, None, op0=ALU.add)
            return last
        th.append(lambda: FD.__setitem__("LNVAR", VE.g(ln_stats4)))

        def ln_sqrt(e):
            for g in range(NG):
                last = e.activation(ln_r[g][:], ln_r[g][:], AF.Sqrt)
            return last
        th.append(lambda: FD.__setitem__("LNSTD", SE.g(ln_sqrt, (s_dve, FD["LNVAR"]))))

        def ln_recip(e):
            for g in range(NG):
                last = e.reciprocal(ln_r[g][:], ln_r[g][:])
            return last
        th.append(lambda: VE.g(ln_recip, (s_act, FD["LNSTD"])))

        def ln_apply(e):
            for g in range(NG):
                last = e.scalar_tensor_tensor(
                    xn_t[:, g * 1024:(g + 1) * 1024], x_t[:, g * 1024:(g + 1) * 1024],
                    ln_m[g][:], ln_r[g][:].broadcast_to((128, 1024)),
                    op0=ALU.subtract, op1=ALU.mult)
            return last

        def _lnapply():
            aw = []
            if "XQ" in PV:
                aw.append((s_dve, PV["XQ"]))
            FD["XN"] = VE.g(ln_apply, *aw)
            if fr + 1 < NFR:
                sched_ldx(fr + 1, (s_dve, FD["XN"]), (s_act, FD["LNSQ"]))
        th.append(_lnapply)

        for g in range(NG):
            def tr(e, g=g):
                for dc in range(8):
                    col = dc * 128
                    last = e.transpose(pt2[:, col:col + 128],
                                       xn_t[:, g * 1024 + dc * 128:g * 1024 + (dc + 1) * 128],
                                       iden[:])
                return last

            def _tr(g=g, tr=tr):
                waits = [(s_dve, FD["XN"]), (s_c, CONSTS_IN)]
                if g > 0:
                    waits.append((s_act, FD["TREV0"]))
                elif "TEV3" in PV:
                    waits.append((s_act, PV["TEV3"]()))
                else:
                    waits.append((s_act, s_act.v))
                FD["TRTE"] = TE.g(tr, *waits)
            th.append(_tr)

            def trev(e, g=g):
                for dc in range(8):
                    col = dc * 128
                    last = e.copy(xn_f[:, dc * FT + g * 128: dc * FT + (g + 1) * 128],
                                  pt2[:, col:col + 128])
                return last

            def _trev(g=g, trev=trev):
                tw = [(s_te, FD["TRTE"])]
                if g == 0 and "VTE7" in PV:
                    tw.append((s_te, PV["VTE7"]))
                FD["TREV0" if g == 0 else "XF"] = SE.g(trev, *tw)
            th.append(_trev)

        def zbf_mm(e):
            for kc in range(8):
                e.matmul(pb[4][0:16, 0:FT], wbf[:, kc * 32:kc * 32 + 16],
                         xn_f[:, kc * FT:(kc + 1) * FT],
                         start=(kc == 0), stop=False)
                last = e.matmul(pb[4][0:16, FT:2 * FT], wbf[:, kc * 32 + 16:(kc + 1) * 32],
                                xn_f[:, kc * FT:(kc + 1) * FT],
                                start=False, stop=(kc == 7))
            return last

        def _zbf():
            zw = [(s_act, FD["XF"])]
            if "GR3" in PV:
                zw.append((s_dve, PV["GR3"]()))
            else:
                zw.append((s_dve, CAST_WBF))
            FD["ZBFTE"] = TE.g(zbf_mm, *zw)
        th.append(_zbf)

        def zbf_ev(e):
            return e.tensor_copy(zbf[:], pb[4][0:16, 0:2 * FT])
        th.append(lambda: FD.__setitem__("ZBFD", VE.g(zbf_ev, (s_te, FD["ZBFTE"]))))

        def sig_ops(e):
            e.activation(beta_v, zbf[:, 0:F], AF.Sigmoid)
            return e.activation(la_v, zbf[:, F:2 * F], AF.Sigmoid)
        th.append(lambda: SE.g(sig_ops, (s_dve, FD["ZBFD"])))
        th.append(lambda: FD.__setitem__("SIG", SE.g(
            lambda e: e.activation(la_v, la_v, AF.Ln))))

        def la_ops(e):
            e.tensor_scalar(beta_v, beta_v, 2.0, None, op0=ALU.mult)
            return e.tensor_tensor_scan(la_v, resetm[:], la_v, zero16[:],
                                        op0=ALU.mult, op1=ALU.add)
        th.append(lambda: FD.__setitem__("BETAD", VE.g(la_ops, (s_act, FD["SIG"]))))
        th.append(lambda: FD.__setitem__("LAD", VE.g(
            lambda e: e.tensor_scalar(la_v, la_v, -60.0, None, op0=ALU.max))))

        def exp_ops(e):
            e.activation(ela_v, la_v, AF.Exp)
            e.activation(elam_v, la_v[:, CK - 1::CK], AF.Exp)
            return e.activation(bem_v, la_v, AF.Exp, scale=-1.0)
        th.append(lambda: FD.__setitem__("EXPA", SE.g(exp_ops, (s_dve, FD["LAD"]))))

        def bem_mul(e):
            return e.tensor_tensor(bem_v, beta_v, bem_v, op=ALU.mult)
        th.append(lambda: FD.__setitem__("BEMD", VE.g(bem_mul, (s_act, FD["EXPA"]))))

        def lam_mm(e):
            for p in range(8):
                last = e.matmul(pb[4][:, p * NCH:(p + 1) * NCH],
                                ep[:, p * 128:(p + 1) * 128], elam_v,
                                start=(p == 0), stop=(p == 7))
            return last
        th.append(lambda: FD.__setitem__("LAMTE", TE.g(
            lam_mm, (s_act, FD["EXPA"]), (s_dve, FD["ZBFD"]))))

        def lam_ev(e):
            return e.tensor_copy(lam_v, pb[4][:, 0:8 * NCH])
        th.append(lambda: FD.__setitem__("LAMD", VE.g(lam_ev, (s_te, FD["LAMTE"]))))

        def elat_mm(e):
            for c in range(NCH):
                last = e.transpose(pt2[0:64, c * 16:(c + 1) * 16],
                                   ela_v[:, c * CK:(c + 1) * CK], iden[0:16, 0:16])
            return last
        th.append(lambda: FD.__setitem__("ELATTE", TE.g(
            elat_mm, (s_act, FD["EXPA"]), (s_dve, FD["LAMD"]), (s_act, FD["XF"]))))

        def elat_ev(e):
            return e.copy(scr_h[:, 0:16 * NCH],
                          pt2[0:64, 0:16 * NCH])
        th.append(lambda: FD.__setitem__("ELATA", SE.g(elat_ev, (s_te, FD["ELATTE"]))))

        def dup_mm(e):
            for c in range(NCH):
                last = e.matmul(pb[4][:, c * 16:(c + 1) * 16], dupc[:],
                                scr_h[:, c * 16:(c + 1) * 16],
                                start=(c == 0), stop=(c == NCH - 1))
            return last
        th.append(lambda: FD.__setitem__("DUPT", TE.g(
            dup_mm, (s_act, FD["ELATA"]), (s_dve, FD["LAMD"]))))

        def dup_ev(e):
            return e.tensor_copy(eld_v, pb[4][:, 0:16 * NCH])

        def _dupev():
            dw = [(s_te, FD["DUPT"])]
            if "SPG" in PV:
                dw.append((s_sp, PV["SPG"]))
            FD["DUPD"] = VE.g(dup_ev, *dw)
        th.append(_dupev)
        return th, FD

    def _run_front0():
        th, FD = build_front(0, {})
        for t in th:
            t()
        return FD
    FRONTS = {0: _run_front0()}
    PREV_SPG = None
    for fr in range(NFR):
        frow = fr * F
        par = fr % 2
        FR_ = FRONTS[fr]
        XN_DVE, XF_ACT, LN_SQ, SIG_ACT = FR_["XN"], FR_["XF"], FR_["LNSQ"], FR_["SIG"]
        BEM_DVE, EXP_ACT = FR_["BEMD"], FR_["EXPA"]
        LAM_DVE, ELAT_ACT, DUP_TE, DUP_DVE = (FR_["LAMD"], FR_["ELATA"],
                                              FR_["DUPT"], FR_["DUPD"])
        if fr + 1 < NFR:
            pv = dict(TEV3=lambda: TEV[NCH - 1], GR3=lambda: GRDVE[NCH - 1])
            if PREV_SPG is not None:
                pv["SPG"] = PREV_SPG
            NEXT_TH, FRONTS[fr + 1] = build_front(fr + 1, pv)
        else:
            NEXT_TH, pv = [], None
        beta_vf = beta_hb[:, par * F:(par + 1) * F]
        bem_vf = bem_h[:, par * F:(par + 1) * F]
        lam_allv = lam_all[:, par * 8 * NCH:(par + 1) * 8 * NCH]
        ela_dupv = ela_dup[:, par * 16 * NCH:(par + 1) * 16 * NCH]

        def q_amax(e):
            for g in range(NG):
                last = e.tensor_reduce(qs_s[g][:], xn_t[:, g * 1024:(g + 1) * 1024],
                                       axis=AX.X, op=ALU.max, apply_absolute_value=True)
            return last
        VE.g(q_amax)

        def q_scale(e):
            for g in range(NG):
                last = e.tensor_scalar(qs_s[g][:], qs_s[g][:], 1e-6, 1.0 / 127.0,
                                       op0=ALU.max, op1=ALU.mult)
            return last
        VE.g(q_scale)

        def q_recip(e):
            for g in range(NG):
                last = e.reciprocal(qsr_s[g][:], qs_s[g][:])
            return last
        QS_DVE = VE.g(q_recip)

        XQ_DVE = QS_DVE
        M32 = 12582912.0
        M52 = 6755399441055744.0
        for g in range(NG):
            sl = slice(g * 1024, (g + 1) * 1024)
            qw = [(s_act, SIG_ACT), (s_te, DUP_TE)]
            if fr > 0:
                qw.append((s_act, PREV_GEV))
            else:
                qw.append((s_dve, GP_LAST_CAST))
            VE.g(lambda e, sl=sl, g=g: e.tensor_scalar(
                sq_scr[:], xn_t[:, sl], qsr_s[g][:], None,
                op0=ALU.mult), *qw)
            VE.g(lambda e: e.tensor_scalar(s_scr[:], sq_scr[:], M32, None, op0=ALU.add))
            VE.g(lambda e: e.tensor_scalar(s_scr[:], s_scr[:], M32, None,
                                           op0=ALU.subtract))
            VE.g(lambda e: e.tensor_scalar(r2_scr[:], sq_scr[:], M52, None, op0=ALU.add))
            VE.g(lambda e: e.tensor_scalar(r2_scr[:], r2_scr[:], M52, None,
                                           op0=ALU.subtract))
            VE.g(lambda e: e.tensor_tensor(r2_scr[:], r2_scr[:], s_scr[:],
                                           op=ALU.subtract))
            XQ_DVE = VE.g(lambda e, sl=sl: e.scalar_tensor_tensor(
                xq_t[:, sl], r2_scr[:], flag1[:], s_scr[:],
                op0=ALU.mult, op1=ALU.add))
        if fr + 1 < NFR:
            pv["XQ"] = XQ_DVE


        QEV = {}
        for p in range(8):
            def qmm(e, p=p):
                for kc in range(8):
                    last = e.matmul(pb[p % 2][:, 0:FT],
                                    wq[:, kc * 1024 + p * 128:kc * 1024 + (p + 1) * 128],
                                    xn_f[:, kc * FT:(kc + 1) * FT],
                                    start=(kc == 0), stop=(kc == 7))
                return last
            waits = [(s_act, XF_ACT)]
            if p >= 2:
                waits.append((s_act, QEV[p - 2]))
            elif fr > 0:
                waits.append((s_dve, PREV_FEV1))
            else:
                waits.append((s_dve, SY_LAST_CAST))
            qte = TE.g(qmm, *waits)

            def qev(e, p=p):
                return e.activation(q_f[:, p * FT:(p + 1) * FT], pb[p % 2][:, 0:FT], AF.Silu)
            QEV[p] = SE.g(qev, (s_te, qte))

        KEV = {}
        for p in range(8):
            def kmm(e, p=p):
                for kc in range(8):
                    last = e.matmul(pb[p % 2][:, 0:FT],
                                    wk[:, kc * 1024 + p * 128:kc * 1024 + (p + 1) * 128],
                                    xn_f[:, kc * FT:(kc + 1) * FT],
                                    start=(kc == 0), stop=(kc == 7))
                return last
            waits = [(s_act, QEV[7] if p < 2 else KEV[p - 2])]
            kte = TE.g(kmm, *waits)

            def kev(e, p=p):
                return e.activation(k_f[:, p * FT:(p + 1) * FT], pb[p % 2][:, 0:FT], AF.Silu)
            KEV[p] = SE.g(kev, (s_te, kte))

        XQTR = []
        for g in range(NG):
            def trq(e, g=g):
                for dc in range(8):
                    col = dc * 128
                    last = e.transpose(pt2[:, col:col + 128],
                                       xq_t[:, g * 1024 + dc * 128:g * 1024 + (dc + 1) * 128],
                                       iden[:])
                return last
            waits = [(s_dve, XQ_DVE)]
            waits.append((s_act, XQTR[-1]) if XQTR else (s_act, s_act.v))
            trq_te = TE.g(trq, *waits)

            def trqev(e, g=g):
                for dc in range(8):
                    col = dc * 128
                    last = e.copy(xq_f[:, dc * FT + g * 128: dc * FT + (g + 1) * 128],
                                  pt2[:, col:col + 128])
                return last
            XQTR.append(SE.g(trqev, (s_te, trq_te)))
        XQF_ACT = XQTR[-1]

        RSQ = {}
        RTE = {}
        for p in range(8):
            def rsq(e, p=p):
                return e.activation(ksq[:, 0:FT], k_f[:, p * FT:(p + 1) * FT], AF.Square)
            waits = [(s_act, KEV[7])] if p == 0 else [(s_te, RTE[p - 1])]
            RSQ[p] = SE.g(rsq, *waits)

            def rmm(e, p=p):
                return e.matmul(pb[5][0:16, 0:FT], csel[:, p * 16:(p + 1) * 16],
                                ksq[:, 0:FT], start=(p == 0), stop=(p == 7))
            rw = [(s_act, RSQ[p])]
            if p == 0 and fr > 0:
                rw.append((s_dve, PREV_QHEV3))
            RTE[p] = TE.g(rmm, *rw)

        VE.g(lambda e: e.tensor_scalar(rn_f32[:], pb[5][0:16, 0:FT], 1e-12, None,
                                       op0=ALU.add), (s_te, RTE[7]))
        RN_DVE = VE.g(lambda e: e.reciprocal(rn_f32[:], rn_f32[:]))

        def rn_sqrt(e):
            return e.activation(rn_h[:], rn_f32[:], AF.Sqrt)
        RN_ACT = SE.g(rn_sqrt, (s_dve, RN_DVE))

        VEV = {}
        for p in range(8):
            def vmm(e, p=p, bem_vf=bem_vf):
                for kc in range(8):
                    e.matmul(pb[2 + p % 2][:, 0:FT],
                             wv[:, kc * 1024 + p * 128:kc * 1024 + (p + 1) * 128],
                             xn_f[:, kc * FT:(kc + 1) * FT],
                             start=(kc == 0), stop=(kc == 7))
                return e.matmul(pb[4][:, 0:FT], ep[:, p * 128:(p + 1) * 128],
                                bem_vf, start=True, stop=True)
            waits = [(s_dve, BEM_DVE), (s_act, XF_ACT), (s_dve, DUP_DVE)]
            if p >= 1:
                waits.append((s_dve, VEV[p - 1]))
                waits.append((s_act, FCP[p - 1]))
            elif fr == 0:
                waits.append((s_dve, GP_LAST_CAST))
            vte = TE.g(vmm, *waits)

            if p == 0:
                FCP = {}
            FCP[p] = fcp = SE.g(lambda e: e.copy(field_s[:, 0:FT], pb[4][:, 0:FT]),
                                (s_te, vte))

            def vev(e, p=p):
                return e.tensor_tensor(v_f[:, p * FT:(p + 1) * FT], pb[2 + p % 2][:, 0:FT],
                                       field_s[:, 0:FT], op=ALU.mult)
            VEV[p] = VE.g(vev, (s_te, vte), (s_act, fcp))
        if fr + 1 < NFR:
            pv["VTE7"] = vte

        KNB = {}
        for p in range(8):
            def fmm(e, p=p, beta_vf=beta_vf):
                e.matmul(pb[4][:, 0:FT], ep[:, p * 128:(p + 1) * 128],
                         rn_h[:], start=True, stop=True)
                return e.matmul(pb[5][:, 0:FT], ep[:, p * 128:(p + 1) * 128],
                                beta_vf, start=True, stop=True)
            waits = [(s_act, RN_ACT), (s_dve, VEV[7]), (s_dve, FR_["BETAD"])]
            if p > 0:
                waits.append((s_dve, KNB[p - 1]))
            fte = TE.g(fmm, *waits)

            VE.g(lambda e, p=p: e.tensor_tensor(
                kn_f[:, p * FT:(p + 1) * FT], k_f[:, p * FT:(p + 1) * FT],
                pb[4][:, 0:FT], op=ALU.mult), (s_te, fte))
            KNB[p] = VE.g(lambda e, p=p: e.tensor_tensor(
                knb_f[:, p * FT:(p + 1) * FT], kn_f[:, p * FT:(p + 1) * FT],
                pb[5][:, 0:FT], op=ALU.mult))
        FIELDS = KNB[7]

        GRDVE, SQEV, SQ_FIN, SQ_TE_LAST = {}, {}, {}, {}
        CH_TE, CHEV, CH_FIN = {}, {}, {}
        SEQ1, SEQ2, DS_TE, SUPD, OEVD, WT_ACT, TEV = {}, {}, {}, {}, {}, {}, {}
        QH_TE, QHEV, SPILL_AT, OSPILL_AT = {}, {}, {}, {}

        def ksl_f(t_, h, ci):
            p, r0 = h // 2, 64 * (h % 2)
            return t_[r0:r0 + 64, p * FT + ci * CK: p * FT + ci * CK + CK]

        def build_T(ci):
            rungs = []

            def t_kn(e, ci=ci):
                for h in range(H):
                    r0 = 64 * (h % 2)
                    last = trt(e, pt2[r0:r0 + 64, h * 64:(h + 1) * 64],
                               ksl_f(kn_f, h, ci), iden[r0:r0 + 64, r0:r0 + 64], (r0, r0))
                return last
            w = [(s_dve, FIELDS), (s_te, DUP_TE), (s_act, ELAT_ACT)]
            w.append((s_act, TEV[ci - 1]) if ci > 0 else (s_act, XQF_ACT))
            if fr == 0:
                w.append((s_dve, CAST_FFN))
            rungs.append(("tensor", t_kn, w, None))

            def t_knev(e, ci=ci):
                return e.copy(knt_all[:, ci * 1024:(ci + 1) * 1024], pt2[:, 0:1024])
            rungs.append(("scalar", t_knev, "AFTER_PREV_TE", None))

            def t_knb(e, ci=ci):
                for h in range(H):
                    r0 = 64 * (h % 2)
                    last = trt(e, pt2[r0:r0 + 64, h * 64:(h + 1) * 64],
                               ksl_f(knb_f, h, ci), iden[r0:r0 + 64, r0:r0 + 64], (r0, r0))
                return last
            rungs.append(("tensor", t_knb, "AFTER_PREV_ACT", None))

            def t_knbev(e, ci=ci):
                ra = r_all[:, ci * 1024:(ci + 1) * 1024]
                e.copy(ra[0:64, :].rearrange("p (u w) -> p u w", w=128)[:, :, 64:128],
                       pt2[0:64, :].rearrange("p (u w) -> p u w", w=128)[:, :, 0:64])
                return e.copy(ra[64:128, :].rearrange("p (u w) -> p u w", w=128)[:, :, 64:128],
                              pt2[64:128, :].rearrange("p (u w) -> p u w", w=128)[:, :, 64:128])
            rungs.append(("scalar", t_knbev, "AFTER_PREV_TE", None))

            def t_v(e, ci=ci):
                for h in range(H):
                    r0 = 64 * (h % 2)
                    last = trt(e, pt2[r0:r0 + 64, h * 64:(h + 1) * 64],
                               ksl_f(v_f, h, ci), iden[r0:r0 + 64, r0:r0 + 64], (r0, r0))
                return last
            rungs.append(("tensor", t_v, "AFTER_PREV_ACT", None))

            def t_vev(e, ci=ci):
                ra = r_all[:, ci * 1024:(ci + 1) * 1024]
                e.copy(ra[0:64, :].rearrange("p (u w) -> p u w", w=128)[:, :, 0:64],
                       pt2[0:64, :].rearrange("p (u w) -> p u w", w=128)[:, :, 0:64])
                return e.copy(ra[64:128, :].rearrange("p (u w) -> p u w", w=128)[:, :, 0:64],
                              pt2[64:128, :].rearrange("p (u w) -> p u w", w=128)[:, :, 64:128])
            rungs.append(("scalar", t_vev, "AFTER_PREV_TE", ("TEV", ci)))
            return rungs

        def build_M(ci):
            rungs = []
            sl2, sl3 = ci % 2, ci % 3

            def m_gram(e, ci=ci):
                last = None
                for h in range(H):
                    r0 = 64 * (h % 2)
                    tp = (r0, r0)
                    cc = (h // 2) * 64
                    mmt(e, pb[2][r0:r0 + 64, cc:cc + 64], ksl_f(kn_f, h, ci),
                        ksl_f(knb_f, h, ci), (h < 2), (h >= H - 2), tp)
                    mmt(e, pb[3][r0:r0 + 64, cc:cc + 64], ksl_f(kn_f, h, ci),
                        ksl_f(q_f, h, ci), (h < 2), (h >= H - 2), tp)
                    last = mmt(e, pb[4][r0:r0 + 64, cc:cc + 64], ksl_f(knb_f, h, ci),
                               ksl_f(kn_f, h, ci), (h < 2), (h >= H - 2), tp)
                return last
            w = [(s_dve, FIELDS), (s_dve, DUP_DVE)]
            if ci > 0:
                w += [(s_act, SQ_FIN[ci - 1]), (s_dve, GRDVE[ci - 1])]
            rungs.append(("tensor", m_gram, w, ("GR_TE", ci)))

            def m_gramev(e, ci=ci, sl2=sl2, sl3=sl3):
                v3 = lambda t: t.rearrange("p (a b) -> p a b", b=64)
                e.scalar_tensor_tensor(v3(t0_d[sl2][:]), v3(pb[2][:]), -1.0, mup,
                                       op0=ALU.mult, op1=ALU.mult)
                e.tensor_tensor(v3(dt_d[sl3][:]), v3(pb[3][:]), mui, op=ALU.mult)
                return e.scalar_tensor_tensor(v3(p0_s[:]), v3(pb[4][:]), -1.0, mlo,
                                              op0=ALU.mult, op1=ALU.mult)
            w = ["AFTER_PREV_TE"]
            if ci >= 2:
                w.append((s_te, CH_TE[(ci - 2, 5)]))
            if ci >= 1:
                w.append((s_te, SQ_TE_LAST[ci - 1]))
            if ci >= 3:
                w.append((s_te, QH_TE[ci - 3]))
            rungs.append(("vector", m_gramev, w, ("GRDVE", ci)))

            for lev in range(5):
                def m_sq(e, ci=ci, lev=lev, sl2=sl2):
                    tprev = t0_d[sl2] if lev == 0 else tl_d[sl2][lev - 1]
                    pprev = p0_s if lev == 0 else pl_s[lev - 1]
                    last = None
                    for h in range(H):
                        r0 = 64 * (h % 2)
                        tp = (r0, r0)
                        cc = (h // 2) * 64
                        rsl, csl = slice(r0, r0 + 64), slice(cc, cc + 64)
                        last = mmt(e, pb[2][rsl, csl], pprev[rsl, csl],
                                   tprev[rsl, csl], (h < 2), (h >= H - 2), tp)
                        if lev < 4:
                            last = mmt(e, pb[3][rsl, csl], tprev[rsl, csl],
                                       pprev[rsl, csl], (h < 2), (h >= H - 2), tp)
                    return last
                tag = ("SQ_TE_LAST", ci) if lev == 4 else None
                rungs.append(("tensor", m_sq, ["AFTER_PREV_DVE", "AFTER_PREV_ACT"], tag))

                def m_sqev(e, ci=ci, lev=lev, sl2=sl2):
                    last = e.copy(tl_d[sl2][lev][:], pb[2][:])
                    if lev < 4:
                        last = e.copy(pl_s[lev][:], pb[3][:])
                    return last
                w = ["AFTER_PREV_TE"]
                if ci >= 2:
                    w.append((s_te, CH_TE[(ci - 2, 5)]))
                tag = ("SQEV", (ci, lev))
                if lev == 4:
                    rungs.append(("scalar", m_sqev, w, [tag, ("SQ_FIN", ci)]))
                else:
                    rungs.append(("scalar", m_sqev, w, tag))
            return rungs

        def build_C(ci):
            rungs = []
            sl2 = ci % 2
            for lev in range(6):
                def c_mm(e, ci=ci, lev=lev, sl2=sl2):
                    tlev = t0_d[sl2] if lev == 0 else tl_d[sl2][lev - 1]
                    rsrc = r_all[:, ci * 1024:(ci + 1) * 1024] if lev == 0 else xch_s
                    last = None
                    for h in range(H):
                        r0 = 64 * (h % 2)
                        tp = (r0, r0)
                        cc = (h // 2) * 128
                        bank = pb[0] if cc < 512 else pb[1]
                        ccc = cc % 512
                        last = mmt(e, bank[r0:r0 + 64, ccc:ccc + 128],
                                   tlev[r0:r0 + 64, (h // 2) * 64:(h // 2) * 64 + 64],
                                   rsrc[r0:r0 + 64, cc:cc + 128],
                                   (h % 8 < 2), (h % 8 >= 6), tp)
                    return last
                if lev == 0:
                    w = [(s_act, TEV[ci]), (s_dve, GRDVE[ci])]
                    if ci > 0:
                        w.append((s_dve, CH_FIN[ci - 1]))
                        w.append((s_te, OT_TE[ci - 1]) if False else (s_dve, CH_FIN[ci - 1]))
                else:
                    w = ["AFTER_PREV_DVE", (s_act, SQEV[(ci, lev - 1)])]
                rungs.append(("tensor", c_mm, w, ("CH_TE", (ci, lev))))

                if lev == 0:
                    def c_ev0(e, ci=ci):
                        ra = r_all[:, ci * 1024:(ci + 1) * 1024]
                        e.tensor_tensor(xch_s[:, 0:512], ra[:, 0:512], pb[0][:],
                                        op=ALU.add)
                        return e.tensor_tensor(xch_s[:, 512:1024], ra[:, 512:1024],
                                               pb[1][:], op=ALU.add)
                    w = ["AFTER_PREV_TE"]
                    if ci >= 1:
                        w.append((s_te, CH_TE[(ci - 1, 5)]))
                    rungs.append(("vector", c_ev0, w, ("CHEV", (ci, 0))))
                elif lev < 5:
                    def c_ev(e, ci=ci, lev=lev):
                        e.tensor_tensor(xch_s[:, 0:512], xch_s[:, 0:512], pb[0][:],
                                        op=ALU.add)
                        return e.tensor_tensor(xch_s[:, 512:1024], xch_s[:, 512:1024],
                                               pb[1][:], op=ALU.add)
                    rungs.append(("vector", c_ev, ["AFTER_PREV_TE"], ("CHEV", (ci, lev))))
                else:
                    def c_split(e, ci=ci, sl2=sl2):
                        xv = xch_s.rearrange("p (u w) -> p u w", w=128)
                        e.tensor_tensor(vt_d[sl2][:, 0:256].rearrange("p (u w) -> p u w", w=64),
                                        xv[:, 0:4, 0:64],
                                        pb[0][:].rearrange("p (u w) -> p u w", w=128)[:, :, 0:64],
                                        op=ALU.add)
                        e.tensor_tensor(wt_d[sl2][:, 0:256].rearrange("p (u w) -> p u w", w=64),
                                        xv[:, 0:4, 64:128],
                                        pb[0][:].rearrange("p (u w) -> p u w", w=128)[:, :, 64:128],
                                        op=ALU.add)
                        e.tensor_tensor(vt_d[sl2][:, 256:512].rearrange("p (u w) -> p u w", w=64),
                                        xv[:, 4:8, 0:64],
                                        pb[1][:].rearrange("p (u w) -> p u w", w=128)[:, :, 0:64],
                                        op=ALU.add)
                        return e.tensor_tensor(wt_d[sl2][:, 256:512].rearrange("p (u w) -> p u w", w=64),
                                               xv[:, 4:8, 64:128],
                                               pb[1][:].rearrange("p (u w) -> p u w", w=128)[:, :, 64:128],
                                               op=ALU.add)
                    w = ["AFTER_PREV_TE"]
                    if ci >= 2:
                        w.append((s_te, QH_TE[ci - 2]))
                    rungs.append(("vector", c_split, w, ("CH_FIN", ci)))
            return rungs

        def build_S(ci):
            rungs = []
            sl2, sl3 = ci % 2, ci % 3

            def s_wt(e, ci=ci, sl2=sl2):
                for h in range(H):
                    r0 = 64 * (h % 2)
                    last = trt(e, pt[r0:r0 + 64, h * 64:(h + 1) * 64],
                               wt_d[sl2][r0:r0 + 64, (h // 2) * 64:(h // 2) * 64 + 64],
                               iden[r0:r0 + 64, r0:r0 + 64], (r0, r0))
                return last
            w = [(s_dve, CH_FIN[ci])]
            if ci > 0:
                w.append((s_act, WT_ACT[ci - 1]))
            rungs.append(("tensor", s_wt, w, None))

            def s_wtev(e, ci=ci, sl2=sl2):
                e.copy(wfm_d[sl2][0:64, :].rearrange("p (u w) -> p u w", w=64),
                       pt[0:64, 0:1024].rearrange("p (u w) -> p u w", w=128)[:, :, 0:64])
                return e.copy(wfm_d[sl2][64:128, :].rearrange("p (u w) -> p u w", w=64),
                              pt[64:128, 0:1024].rearrange("p (u w) -> p u w", w=128)[:, :, 64:128])
            rungs.append(("scalar", s_wtev, "AFTER_PREV_TE", ("WT_ACT", ci)))

            def s_vcorr(e, ci=ci, sl2=sl2):
                last = None
                for h in range(H):
                    r0 = 64 * (h % 2)
                    tp = (r0, r0)
                    cc = (h // 2) * 64
                    rsl, csl = slice(r0, r0 + 64), slice(cc, cc + 64)
                    last = mmt(e, pb[5][rsl, csl], wfm_d[sl2][rsl, csl],
                               state_b[rsl, csl], (h < 2), (h >= H - 2), tp)
                return last
            w = ["AFTER_PREV_ACT"]
            if ci > 0:
                w.append((s_dve, QHEV[ci - 1]))
            else:
                w.append((s_dve, W_DVE))
            rungs.append(("tensor", s_vcorr, w, ("SEQ1", ci)))

            def s_vnew(e, ci=ci, sl2=sl2):
                return e.tensor_tensor(vt_d[sl2][:], vt_d[sl2][:], pb[5][:],
                                       op=ALU.subtract)
            rungs.append(("vector", s_vnew, "AFTER_PREV_TE", None))

            def s_omm(e, ci=ci, sl2=sl2, sl3=sl3):
                last = None
                for h in range(H):
                    r0 = 64 * (h % 2)
                    tp = (r0, r0)
                    cc = (h // 2) * 64
                    rsl, csl = slice(r0, r0 + 64), slice(cc, cc + 64)
                    mmt(e, pb[5][rsl, csl], ksl_f(q_f, h, ci),
                        state_b[rsl, csl], (h < 2), False, tp)
                    last = mmt(e, pb[5][rsl, csl], dt_d[sl3][rsl, csl],
                               vt_d[sl2][rsl, csl], False, (h >= H - 2), tp)
                return last
            rungs.append(("tensor", s_omm, "AFTER_PREV_DVE", ("SEQ2", ci)))

            def s_oev(e, ci=ci, ela_dupv=ela_dupv):
                ed = ela_dupv[:, ci * 16:(ci + 1) * 16].rearrange("p (u t) -> p u t", t=2)
                e.tensor_tensor(
                    osc_s[0:64, :].rearrange("p (u w) -> p u w", w=64),
                    pb[5][0:64, :].rearrange("p (u w) -> p u w", w=64),
                    ed[0:64, :, 0:1].broadcast_to((64, 8, 64)), op=ALU.mult)
                return e.tensor_tensor(
                    osc_s[64:128, :].rearrange("p (u w) -> p u w", w=64),
                    pb[5][64:128, :].rearrange("p (u w) -> p u w", w=64),
                    ed[64:128, :, 1:2].broadcast_to((64, 8, 64)), op=ALU.mult)
            w = ["AFTER_PREV_TE", (s_dve, DUP_DVE)]
            if ci > 0:
                w.append((s_spo, OSPILL_AT[ci - 1]))
            elif s_spo.v:
                w.append((s_spo, s_spo.v))
            rungs.append(("vector", s_oev, w, ("OEVD", ci)))

            def s_ds(e, ci=ci, sl2=sl2):
                last = None
                for h in range(H):
                    r0 = 64 * (h % 2)
                    tp = (r0, r0)
                    cc = (h // 2) * 64
                    rsl, csl = slice(r0, r0 + 64), slice(cc, cc + 64)
                    last = mmt(e, pb[5][rsl, csl],
                               knt_all[rsl, ci * 1024 + h * 64: ci * 1024 + (h + 1) * 64],
                               vt_d[sl2][rsl, csl], (h < 2), (h >= H - 2), tp)
                return last
            rungs.append(("tensor", s_ds, "AFTER_PREV_DVE", ("DS_TE", ci)))

            def s_sup1(e):
                return e.tensor_tensor(state[:], state[:], pb[5][:], op=ALU.add)
            rungs.append(("vector", s_sup1, ["AFTER_PREV_TE", (s_dve, LAM_DVE)], None))

            def s_sup2(e, ci=ci, lam_allv=lam_allv):
                lam_c = lam_allv.rearrange("p (u c) -> p u c", c=NCH)[:, :, ci:ci + 1]
                return e.tensor_tensor(
                    state[:].rearrange("p (u w) -> p u w", w=64),
                    state[:].rearrange("p (u w) -> p u w", w=64),
                    lam_c.broadcast_to((128, 8, 64)), op=ALU.mult)
            rungs.append(("vector", s_sup2, "AFTER_PREV_DVE", None))

            def s_sup3(e):
                return e.tensor_copy(state_b[:], state[:])
            rungs.append(("vector", s_sup3, "AFTER_PREV_DVE", ("SUPD", ci)))

            gci = fr * NCH + ci

            def s_ospill(e, gci=gci):
                return [e.dma_start(out=oz_d[gci], in_=osc_s[:])]
            s_ospill.n_dma = 1
            w = ["AFTER_PREV_DVE"] + ([(s_spo, s_spo.v)] if s_spo.v else [])
            rungs.append(("spill_o", s_ospill, w, ("OSPILL_AT", ci)))

            def s_mh(e, ci=ci, sl2=sl2):
                last = None
                for h in range(H):
                    r0 = 64 * (h % 2)
                    tp = (r0, r0)
                    cc = (h // 2) * 64
                    rsl, csl = slice(r0, r0 + 64), slice(cc, cc + 64)
                    last = mmt(e, pb[5][rsl, csl], wt_d[sl2][rsl, csl],
                               knt_all[rsl, ci * 1024 + h * 64: ci * 1024 + (h + 1) * 64],
                               (h < 2), (h >= H - 2), tp)
                return last
            rungs.append(("tensor", s_mh, "AFTER_PREV_DVE", None))

            def s_mhev(e, ci=ci):
                v3 = lambda t: t.rearrange("p (a b) -> p a b", b=64)
                e.tensor_tensor(v3(o_f[0:64, 0:512]),
                                iden[0:64, 0:64].unsqueeze(1).broadcast_to((64, 8, 64)),
                                v3(pb[5][0:64, :]), op=ALU.subtract)
                return e.tensor_tensor(v3(o_f[64:128, 0:512]),
                                       iden[64:128, 64:128].unsqueeze(1).broadcast_to((64, 8, 64)),
                                       v3(pb[5][64:128, :]), op=ALU.subtract)
            w = ["AFTER_PREV_TE", (s_dve, LAM_DVE)]
            if ci > 0:
                w.append((s_spq, SPILL_AT[ci - 1]))
            elif s_spq.v:
                w.append((s_spq, s_spq.v))
            rungs.append(("vector", s_mhev, w, None))

            def s_mhev2(e, ci=ci, lam_allv=lam_allv):
                v3 = lambda t: t.rearrange("p (a b) -> p a b", b=64)
                lam_c = lam_allv.rearrange("p (u c) -> p u c", c=NCH)[:, :, ci:ci + 1]
                return e.tensor_tensor(v3(o_f[:, 0:512]), v3(o_f[:, 0:512]),
                                       lam_c.broadcast_to((128, 8, 64)), op=ALU.mult)
            rungs.append(("vector", s_mhev2, "AFTER_PREV_DVE", None))

            def s_qh(e, ci=ci, sl2=sl2, sl3=sl3):
                last = None
                for h in range(H):
                    r0 = 64 * (h % 2)
                    tp = (r0, r0)
                    cc = (h // 2) * 64
                    rsl, csl = slice(r0, r0 + 64), slice(cc, cc + 64)
                    last = mmt(e, pb[5][rsl, csl], wt_d[sl2][rsl, csl],
                               dt_d[sl3][rsl, csl], (h < 2), (h >= H - 2), tp)
                return last
            rungs.append(("tensor", s_qh, "AFTER_PREV_DVE", ("QH_TE", ci)))

            def s_qhev(e, ci=ci):
                qv = q_f.rearrange("p (u t) -> p u t", t=FT)[:, :, ci * CK:(ci + 1) * CK]
                return e.tensor_tensor(o_f[:, 512:1024].rearrange("p (a b) -> p a b", b=64),
                                       qv, pb[5][:].rearrange("p (a b) -> p a b", b=64),
                                       op=ALU.subtract)
            rungs.append(("vector", s_qhev, "AFTER_PREV_TE", ("QHEV", ci)))

            def s_mqspill(e, gci=gci):
                return [e.dma_start(out=mh_d[gci], in_=o_f[:, 0:512]),
                        e.dma_start(out=qh_d[gci], in_=o_f[:, 512:1024])]
            s_mqspill.n_dma = 2
            w = ["AFTER_PREV_DVE"] + ([(s_spq, s_spq.v)] if s_spq.v else [])
            rungs.append(("spill_q", s_mqspill, w, ("SPILL_AT", ci)))
            return rungs

        def emit_rungs(rungs, own):
            kind, fn, waits, tag = rungs
            if isinstance(waits, str):
                waits = [waits]
            rw = []
            for x in waits:
                if x == "AFTER_PREV_TE":
                    if own.get("tensor"):
                        rw.append((s_te, own["tensor"]))
                elif x == "AFTER_PREV_DVE":
                    if own.get("vector"):
                        rw.append((s_dve, own["vector"]))
                elif x == "AFTER_PREV_ACT":
                    if own.get("scalar"):
                        rw.append((s_act, own["scalar"]))
                else:
                    rw.append(x)
            if kind in ("spill", "spill_o", "spill_q"):
                sem = {"spill": s_sp, "spill_o": s_spo, "spill_q": s_spq}[kind]
                val = SY.g(fn, *rw, sem=sem)
                own[kind] = val
            else:
                eng = {"tensor": TE, "vector": VE, "scalar": SE}[kind]
                val = eng.g(fn, *rw)
                own[kind] = val
            if tag is not None:
                tags = tag if isinstance(tag, list) else [tag]
                for tname, tkey in tags:
                    globals_map[tname][tkey] = val
            return val

        globals_map = dict(GR_TE={}, GRDVE=GRDVE, SQEV=SQEV, SQ_FIN=SQ_FIN,
                           SQ_TE_LAST=SQ_TE_LAST, CH_TE=CH_TE, CHEV=CHEV,
                           CH_FIN=CH_FIN, SEQ1=SEQ1, SEQ2=SEQ2, DS_TE=DS_TE,
                           SUPD=SUPD, OEVD=OEVD, WT_ACT=WT_ACT, TEV=TEV,
                           QH_TE=QH_TE, QHEV=QHEV, SPILL_AT=SPILL_AT, OSPILL_AT=OSPILL_AT)

        th_i = 0
        for it in range(NCH + 3):
            stages = []
            if it < NCH:
                stages.append(("T", it, build_T(it), {}))
            if 0 <= it - 1 < NCH:
                stages.append(("M", it - 1, build_M(it - 1), {}))
            if 0 <= it - 2 < NCH:
                stages.append(("C", it - 2, build_C(it - 2), {}))
            if 0 <= it - 3 < NCH:
                stages.append(("S", it - 3, build_S(it - 3), {}))
            mx = max(len(s[2]) for s in stages) if stages else 0
            for r in range(mx):
                for (_, _, l, own) in stages:
                    if r < len(l):
                        emit_rungs(l[r], own)
                if it >= NCH + 1 and th_i < len(NEXT_TH):
                    NEXT_TH[th_i]()
                    th_i += 1
        while th_i < len(NEXT_TH):
            NEXT_TH[th_i]()
            th_i += 1

        GEV = {}
        for g in range(NG):
            def ffn_mm(e, g=g):
                for hh in range(2):
                    for kc in range(8):
                        last = e.matmul(pb[hh][:, 0:512],
                                        xq_f[:, kc * FT + g * 128: kc * FT + (g + 1) * 128],
                                        wfn[:, kc * 1024 + hh * 512: kc * 1024 + (hh + 1) * 512],
                                        start=(kc == 0), stop=(kc == 7))
                return last
            waits = [(s_act, XQF_ACT), (s_dve, CAST_FFN)]
            if g > 0:
                waits.append((s_dve, FEV1[g - 1]))
            else:
                waits.append((s_dve, CH_FIN[NCH - 1]))
            FTE = TE.g(ffn_mm, *waits)

            def ffn_ev1(e, g=g):
                for hh in range(2):
                    sl = slice(hh * 512, (hh + 1) * 512)
                    last = e.scalar_tensor_tensor(s_scr[:, sl], pb[hh][:, 0:512],
                                                  qs_s[g][:], fws_b[:, sl],
                                                  op0=ALU.mult, op1=ALU.mult)
                return last
            fdv_waits = [(s_te, FTE)]
            if g > 0:
                fdv_waits.append((s_act, GEV[g - 1]))
            if g == 0:
                FEV1 = {}
            FEV1[g] = VE.g(ffn_ev1, *fdv_waits)

            def ffn_ev2(e, g=g):
                for hh in range(2):
                    sl = slice(hh * 512, (hh + 1) * 512)
                    last = e.tensor_tensor(s_scr[:, sl], s_scr[:, sl], fbias_b[:, sl],
                                           op=ALU.add)
                return last
            FDV = VE.g(ffn_ev2)

            def ffn_silu(e, g=g):
                return e.activation(g_t[:, g * 1024:(g + 1) * 1024], s_scr[:], AF.Silu)
            sw = [(s_dve, FDV)]
            if g == 0 and fr > 0:
                sw.append((s_sp, PREV_SPG))
            GEV[g] = SE.g(ffn_silu, *sw)


        @dma_group(NG + 1)
        def sp_g(e, fr=fr, frow=frow, ela_dupv=ela_dupv):
            a = [e.dma_start(out=gz_d[frow + g * 128: frow + (g + 1) * 128, :],
                             in_=g_t[:, g * 1024:(g + 1) * 1024]) for g in range(NG)]
            a.append(e.dma_start(out=eld_d[fr], in_=ela_dupv))
            return a
        w = [(s_act, GEV[NG - 1]), (s_dve, DUP_DVE)] + ([(s_sp, s_sp.v)] if s_sp.v else [])
        SPG = SY.g(sp_g, *w, sem=s_sp)
        PREV_SPG = SPG
        PREV_FEV1 = FEV1[NG - 1]
        PREV_GEV = GEV[NG - 1]
        PREV_QHEV3 = QHEV[NCH - 1]
        LAST_SUPD = SUPD[NCH - 1]
        LAST_SPILL = (s_sp.v, s_spo.v, s_spq.v)

    sc_t = k_f[:, 0:512]
    P1_TE_END = s_te.v
    P1_DVE_END = s_dve.v
    P1_ACT_END = s_act.v

    @dma_group(1)
    def st_dma1(e):
        return [e.dma_start(out=scin_d, in_=state_b[:])]
    STD1 = SY.g(st_dma1, (s_dve, LAST_SUPD), sem=s_st)

    def cc_seq(e, STD1=STD1):
        e.wait_ge(s_st.h, STD1)
        e.collective_compute(
            "AllGather", ALU.bypass, replica_groups=groups,
            ins=[scin_d.opt()], outs=[scout_d.opt()]).then_inc(s_cc.h, 1)
        return e.nop()
    GP.g(cc_seq)
    s_cc.v += 1
    CC_GP = s_cc.v

    @dma_group(1)
    def st_dma2(e):
        return [e.dma_start(out=sc_t, in_=scout_d[0:128, :])]
    STD2 = SY.g(st_dma2, (s_cc, CC_GP), sem=s_st)

    def steer1(e):
        return e.tensor_scalar(state[:], sc_t, shalf[:], None, op0=ALU.mult)
    ST1 = VE.g(steer1, (s_st, STD2))

    def steer2(e):
        return e.tensor_copy(state_b[:], state[:])
    ST_DVE = VE.g(steer2, (s_dve, ST1))

    x2v = wq.bitcast(F32)
    g2v = wk
    x_p = lambda sl: x2v[:, sl * 2048:(sl + 1) * 2048]
    g_p = lambda sl: g2v[:, sl * 2048:(sl + 1) * 2048]
    ela_p = lambda sl: g2v[:, 4096 + sl * 64: 4096 + (sl + 1) * 64]
    for fr in range(NFR):
        frow = fr * F
        sl = fr % 2
        mh_t = lambda s: kn_f[:, s * 1024: s * 1024 + 1024]
        qh_t = lambda s: knb_f[:, s * 1024: s * 1024 + 1024]
        oz_t = lambda s: v_f[:, s * 1024: s * 1024 + 1024]
        mh_t = lambda s: kn_f[:, s * 512: (s + 1) * 512]
        qh_t = lambda s: knb_f[:, s * 512: (s + 1) * 512]
        oz_t = lambda s: v_f[:, s * 512: (s + 1) * 512]

        @dma_group(NG + NG + 1)
        def ld2(e, fr=fr, frow=frow, sl=sl):
            a = [e.dma_start(out=g_p(sl)[:, g * 1024:(g + 1) * 1024],
                             in_=gz_d[frow + g * 128: frow + (g + 1) * 128, :])
                 for g in range(NG)]
            a += [e.dma_start(out=x_p(sl)[:, g * 1024:(g + 1) * 1024],
                              in_=x_d[frow + g * 128: frow + (g + 1) * 128, :])
                  for g in range(NG)]
            a.append(e.dma_start(out=ela_p(sl), in_=eld_d[fr]))
            return a
        w2 = [(s_sp, s_sp.v), (s_spo, LAST_SPILL[1]), (s_spq, LAST_SPILL[2])]
        if fr == 0:
            w2.append((s_te, P1_TE_END))
            w2.append((s_dve, P1_DVE_END))
            w2.append((s_act, P1_ACT_END))
        elif fr >= 2:
            w2.append((s_dve, P2FE[fr - 2]))
        LD2 = SY.g(ld2, *w2, sem=s_sp)

        CHLD = {}
        HOMS = {}
        P2SUP = {}
        P2OEV = {}
        P2OT = {}
        P2OA = {}
        for ci in range(NCH):
            gci = fr * NCH + ci
            s2 = ci % 2

            @dma_group(3)
            def ldc(e, gci=gci, s2=s2):
                return [e.dma_start(out=mh_t(s2), in_=mh_d[gci]),
                        e.dma_start(out=qh_t(s2), in_=qh_d[gci]),
                        e.dma_start(out=oz_t(s2), in_=oz_d[gci])]
            w = [(s_sp, s_sp.v), (s_spo, LAST_SPILL[1]), (s_spq, LAST_SPILL[2])]
            if ci >= 2:
                w.append((s_te, P2OT[ci - 2]))
            elif fr > 0:
                w.append((s_te, PREV_HOM[ci + 2]))
                w.append((s_dve, PREV_OEV[ci + 2]))
            CHLD[ci] = SY.g(ldc, *w, sem=s_sp)

            def hom_mm(e, ci=ci, s2=s2):
                last = None
                for h in range(H):
                    r0 = 64 * (h % 2)
                    tp = (r0, r0)
                    cc = (h // 2) * 64
                    rsl, csl = slice(r0, r0 + 64), slice(cc, cc + 64)
                    mmt(e, pb[4][rsl, csl], mh_t(s2)[rsl, csl], state_b[rsl, csl],
                        (h < 2), (h >= H - 2), tp)
                    last = mmt(e, pb[5][rsl, csl], qh_t(s2)[rsl, csl],
                               state_b[rsl, csl], (h < 2), (h >= H - 2), tp)
                return last
            w = [(s_sp, CHLD[ci])]
            if ci > 0:
                w.append((s_act, P2SUP[ci - 1]))
                w.append((s_dve, P2OEV[ci - 1]))
            elif fr == 0:
                w.append((s_dve, ST_DVE))
            else:
                w.append((s_act, PREV_SUP3))
                w.append((s_dve, PREV_OEV[3]))
            HOM = TE.g(hom_mm, *w)
            HOMS[ci] = HOM

            def p2_oc(e, ci=ci, s2=s2, sl=sl):
                ed = ela_p(sl)[:, ci * 16:(ci + 1) * 16].rearrange("p (u t) -> p u t", t=2)
                e.scalar_tensor_tensor(
                    osc_s[0:64, :].rearrange("p (u w) -> p u w", w=64),
                    pb[5][0:64, :].rearrange("p (u w) -> p u w", w=64),
                    1.0, ed[0:64, :, 0:1].broadcast_to((64, 8, 64)),
                    op0=ALU.mult, op1=ALU.mult)
                return e.scalar_tensor_tensor(
                    osc_s[64:128, :].rearrange("p (u w) -> p u w", w=64),
                    pb[5][64:128, :].rearrange("p (u w) -> p u w", w=64),
                    1.0, ed[64:128, :, 1:2].broadcast_to((64, 8, 64)),
                    op0=ALU.mult, op1=ALU.mult)
            w = [(s_te, HOM)]
            if ci >= 1:
                w.append((s_te, P2OT[ci - 1]))
            P2OC1 = VE.g(p2_oc, *w)

            def p2_oc2(e, s2=s2):
                return e.tensor_tensor(osc_s[:], osc_s[:], oz_t(s2), op=ALU.add)
            P2OEV[ci] = VE.g(p2_oc2, (s_dve, P2OC1))

            def p2_sup(e):
                return e.copy(state_b[:], pb[4][:])
            P2SUP[ci] = SE.g(p2_sup, (s_te, HOM))

            def p2_ot(e, ci=ci):
                for h in range(H):
                    r0 = 64 * (h % 2)
                    last = trt(e, pt[r0:r0 + 64, h * 64:(h + 1) * 64],
                               osc_s[r0:r0 + 64, (h // 2) * 64:(h // 2) * 64 + 64],
                               iden[r0:r0 + 64, r0:r0 + 64], (r0, r0))
                return last
            w = [(s_dve, P2OEV[ci])]
            if ci >= 1:
                w.append((s_act, P2OA[ci - 1]))
            elif fr > 0:
                w.append((s_act, PREV_P2OA3))
            P2OT[ci] = TE.g(p2_ot, *w)

            def p2_oa(e, ci=ci):
                tk0 = ci * CK
                e.copy(o_f[0:64, :].rearrange("p (u w) -> p u w", w=FT)[:, :, tk0:tk0 + 64],
                       pt[0:64, 0:1024].rearrange("p (u w) -> p u w", w=128)[:, :, 0:64])
                return e.copy(o_f[64:128, :].rearrange("p (u w) -> p u w", w=FT)[:, :, tk0:tk0 + 64],
                              pt[64:128, 0:1024].rearrange("p (u w) -> p u w", w=128)[:, :, 64:128])
            aw = [(s_te, P2OT[ci])]
            if ci == 0 and fr > 0:
                aw.append((s_dve, PREV_ONF7))
                aw.append((s_te, PREV_WTE))
            P2OA[ci] = SE.g(p2_oa, *aw)

        ksqb = [ksq, field_s]
        OSQ = {}
        OTE = {}
        for p in range(8):
            def osq(e, p=p):
                return e.activation(ksqb[p % 2][:, 0:FT], o_f[:, p * FT:(p + 1) * FT],
                                    AF.Square)
            waits = [(s_act, P2OA[NCH - 1])] if p == 0 else                     ([(s_te, OTE[p - 2])] if p >= 2 else [])
            OSQ[p] = SE.g(osq, *waits)

            def omm(e, p=p):
                return e.matmul(pb[0][0:16, 0:FT], csel[:, p * 16:(p + 1) * 16],
                                ksqb[p % 2][:, 0:FT], start=(p == 0), stop=(p == 7))
            ow = [(s_act, OSQ[p])]
            if p == 0:
                ow.append((s_dve, ST_DVE if fr == 0 else PREV_ONF7))
            OTE[p] = TE.g(omm, *ow)

        VE.g(lambda e: e.tensor_scalar(rms_h[:], pb[0][0:16, 0:FT], 1.0 / HD, 1e-6,
                                       op0=ALU.mult, op1=ALU.add), (s_te, OTE[7]))
        RMS_DVE = VE.g(lambda e: e.reciprocal(rms_h[:], rms_h[:]))

        def rms_sqrt(e):
            return e.activation(rms_hb[:], rms_h[:], AF.Sqrt)
        RMS_ACT = SE.g(rms_sqrt, (s_dve, RMS_DVE))

        ONF = {}
        for p in range(8):
            def rfmm(e, p=p):
                bank = pb[0] if p % 2 == 0 else pb[1]
                return e.matmul(bank[:, 0:FT], ep[:, p * 128:(p + 1) * 128],
                                rms_hb[:], start=True, stop=True)
            waits = [(s_act, RMS_ACT)]
            if p >= 2:
                waits.append((s_dve, ONF[p - 2]))
            elif p == 1:
                waits.append((s_dve, ST_DVE if fr == 0 else P2FE[fr - 1]))
            rte = TE.g(rfmm, *waits)

            def onev(e, p=p):
                bank = pb[0] if p % 2 == 0 else pb[1]
                return e.scalar_tensor_tensor(o_f[:, p * FT:(p + 1) * FT],
                                              o_f[:, p * FT:(p + 1) * FT],
                                              normw_f[p][:], bank[:, 0:FT],
                                              op0=ALU.mult, op1=ALU.mult)
            ONF[p] = VE.g(onev, (s_te, rte))

        pt2f = pt2.bitcast(F32)
        WOBK = [(pb[2], pb[3]), (pb[1], pt2f)]
        g32 = g_t.bitcast(F32)
        ODST = [out_t, g32]
        for g in range(NG):
            def wo_mm(e, g=g):
                for hh in range(2):
                    for kc in range(8):
                        last = e.matmul(WOBK[g][hh][:, 0:512],
                                        o_f[:, kc * FT + g * 128: kc * FT + (g + 1) * 128],
                                        wo[:, kc * 1024 + hh * 512: kc * 1024 + (hh + 1) * 512],
                                        start=(kc == 0), stop=(kc == 7))
                return last
            waits = [(s_dve, ONF[7])]
            if fr > 0:
                waits.append((s_out, PREV_OUT[g]))
            WTE = TE.g(wo_mm, *waits)

            def final_ev1(e, g=g, slf=sl):
                for hh in range(2):
                    last = e.tensor_tensor(
                        ODST[g][:, hh * 512:(hh + 1) * 512], WOBK[g][hh][:, 0:512],
                        g_p(slf)[:, g * 1024 + hh * 512:g * 1024 + (hh + 1) * 512],
                        op=ALU.mult)
                return last
            ew1 = [(s_te, WTE), (s_sp, CHLD[NCH - 1])]
            if fr > 0:
                ew1.append((s_out, PREV_OUT[g]))
            VE.g(final_ev1, *ew1)

            def final_ev2(e, g=g, slf=sl):
                for hh in range(2):
                    last = e.tensor_tensor(
                        ODST[g][:, hh * 512:(hh + 1) * 512], ODST[g][:, hh * 512:(hh + 1) * 512],
                        x_p(slf)[:, g * 1024 + hh * 512:g * 1024 + (hh + 1) * 512],
                        op=ALU.add)
                return last
            FEV = VE.g(final_ev2)

            @dma_group(1)
            def st_out(e, g=g, frow=frow):
                return [e.dma_start(out=out_d[frow + g * 128: frow + (g + 1) * 128, :],
                                    in_=ODST[g][:])]
            if fr == 0 and g == 0:
                PREV_OUT = {}
            ow2 = [(s_dve, FEV)] + ([(s_out, s_out.v)] if s_out.v else [])
            PREV_OUT[g] = GPD.g(st_out, *ow2)
        if fr == 0:
            P2FE = {}
        P2FE[fr] = FEV
        PREV_HOM = dict(HOMS)
        PREV_OEV = dict(P2OEV)
        PREV_SUP3 = P2SUP[NCH - 1]
        PREV_RMS = RMS_DVE
        PREV_ONF7 = ONF[7]
        PREV_WTE = WTE
        PREV_P2OA3 = P2OA[NCH - 1]

    FINAL_OUT = s_out.v

    with nc.Block() as block:
        @block.sync
        def _(e):
            _emit(e, SY)
            e.wait_ge(s_out.h, FINAL_OUT)

        @block.tensor
        def _(e):
            _emit(e, TE)

        @block.vector
        def _(e):
            _emit(e, VE)

        @block.scalar
        def _(e):
            _emit(e, SE)

        @block.gpsimd
        def _(e):
            _emit(e, GPSTG)
            _emit(e, GP)
            _emit(e, GPD)

    return nc



def _consts(F):
    bf = ml_dtypes.bfloat16
    iden = np.eye(128, dtype=bf)
    r = np.arange(64)
    pat_up = (r[:, None] < r[None, :]).astype(np.float32)
    pat_lo = (r[:, None] > r[None, :]).astype(np.float32)
    pat_ui = (r[:, None] <= r[None, :]).astype(np.float32)
    mup = np.tile(pat_up, (2, 1)).astype(bf)
    mlo = np.tile(pat_lo, (2, 1)).astype(bf)
    mui = np.tile(pat_ui, (2, 1)).astype(bf)
    ep = np.zeros((16, 1024), np.float32)
    for h in range(16):
        p, hi = h // 2, h % 2
        ep[h, p * 128 + hi * 64: p * 128 + (hi + 1) * 64] = 1.0
    csel = np.zeros((128, 128), np.float32)
    for p in range(8):
        csel[0:64, p * 16 + 2 * p] = 1.0
        csel[64:128, p * 16 + 2 * p + 1] = 1.0
    dup = np.zeros((64, 128), np.float32)
    for t in range(64):
        dup[t, t] = 1.0
        dup[t, 64 + t] = 1.0
    reset = np.ones((16, F), np.float32)
    reset[:, 0::CK] = 0.0
    return dict(iden=iden, mup=mup, mlo=mlo, mui=mui, ep=ep.astype(bf),
                csel=csel.astype(bf), dup=dup.astype(bf), reset=reset)


def _in_map(xb, inputs, F):
    c = _consts(F)
    return dict(
        x=np.ascontiguousarray(xb),
        wq=np.asarray(inputs["L0_attn_Wq"], np.float32),
        wk=np.asarray(inputs["L0_attn_Wk"], np.float32),
        wv=np.asarray(inputs["L0_attn_Wv"], np.float32),
        wo=np.asarray(inputs["L0_attn_Wo"], np.float32),
        wbf=np.ascontiguousarray(np.concatenate(
            [np.asarray(inputs["L0_attn_Wbet"], np.float32),
             np.asarray(inputs["L0_attn_Wf"], np.float32)], axis=1)),
        normw=np.asarray(inputs["L0_attn_norm_w"], np.float32).reshape(1, D),
        fqt=np.ascontiguousarray(np.asarray(inputs["L0_ffn_weight_q"]).T),
        fscale=np.asarray(inputs["L0_ffn_weight_scale"], np.float32).reshape(1, D),
        fbias=np.asarray(inputs["L0_ffn_bias"], np.float32).reshape(1, D),
        **c,
    )


def kernel(L0_attn_x, L0_attn_Wq, L0_attn_Wk, L0_attn_Wv, L0_attn_Wo,
           L0_attn_Wbet, L0_attn_Wf, L0_attn_norm_w, L0_ffn_weight_q,
           L0_ffn_weight_scale, L0_ffn_bias):
    inputs = dict(L0_attn_Wq=L0_attn_Wq, L0_attn_Wk=L0_attn_Wk,
                  L0_attn_Wv=L0_attn_Wv, L0_attn_Wo=L0_attn_Wo,
                  L0_attn_Wbet=L0_attn_Wbet, L0_attn_Wf=L0_attn_Wf,
                  L0_attn_norm_w=L0_attn_norm_w, L0_ffn_weight_q=L0_ffn_weight_q,
                  L0_ffn_weight_scale=L0_ffn_weight_scale, L0_ffn_bias=L0_ffn_bias)
    x = np.asarray(L0_attn_x, np.float32)
    B, N, _ = x.shape
    T, F = N // 2, 256
    groups = [[b, b + B] for b in range(B)]
    nc = _build(T, F, groups)
    in_maps = []
    for c in range(2 * B):
        beta, j = c % B, c // B
        im = _in_map(x[beta, j * T:(j + 1) * T], inputs, F)
        im["shalf"] = np.full((128, 1), float(j), np.float32)
        in_maps.append(im)
    res = run_bass_kernel_spmd(nc, in_maps, core_ids=list(range(2 * B)))
    out = np.stack([np.concatenate([np.asarray(res.results[b]["out"]),
                                    np.asarray(res.results[b + B]["out"])], axis=0)
                    for b in range(B)], axis=0)
    return out.astype(np.float32)
